# version 25
# baseline (speedup 1.0000x reference)
import math
from contextlib import ExitStack

import numpy as np
import ml_dtypes
import sys

sys.path.insert(0, "/opt/trn_rl_repo")
import concourse.bass as bass
import concourse.mybir as mybir
from concourse.bass_utils import run_bass_kernel_spmd

F32 = mybir.dt.float32
BF16 = mybir.dt.bfloat16
AF = mybir.ActivationFunctionType
ALU = mybir.AluOpType
AX = mybir.AxisListType

D = 2048
T = 2048
TM = 1024
H = 16
HID = 5632
NHC = HID // 128
NDC = D // 128
EPS_LN = 1e-5
EPS_Q = 1e-8
SCL = 1.0 / math.sqrt(128.0)
QMAGIC = float(3 * 2 ** 22)


class Sem:
    def __init__(self, ctx, nc, name):
        self.h = ctx.enter_context(nc.semaphore(name))
        self.n = 0

    def mark(self, inst, k=1):
        inst.then_inc(self.h, k)
        self.n += k
        return (self, self.n)

    def mile(self):
        return (self, self.n)


def wait(eng, mile):
    if mile is not None:
        eng.wait_ge(mile[0].h, mile[1])


def fence(eng, sem, inst):
    mm = sem.mark(inst)
    wait(eng, mm)
    return mm


def build_nc(taps=()):
    taps = set(taps)
    nc = bass.Bass("TRN2", target_bir_lowering=False, debug=False)

    xp = nc.dram_tensor("xp", [T, D], F32, kind="ExternalInput").ap()
    wqk = nc.dram_tensor("wqk", [H, 128, NDC * 256], F32, kind="ExternalInput").ap()
    wv = nc.dram_tensor("wv", [4, 128, NDC * 512], F32, kind="ExternalInput").ap()
    wo = nc.dram_tensor("wo", [NDC, 128, D], F32, kind="ExternalInput").ap()
    w1t = nc.dram_tensor("w1t", [NHC, 128, D], F32, kind="ExternalInput").ap()
    w2t = nc.dram_tensor("w2t", [NHC, 128, D], F32, kind="ExternalInput").ap()
    w3r = nc.dram_tensor("w3r", [HID, D], F32, kind="ExternalInput").ap()
    mask_d = nc.dram_tensor("mask_d", [128, 128], BF16, kind="ExternalInput").ap()
    mask_a = nc.dram_tensor("mask_a", [128, 256], BF16, kind="ExternalInput").ap()
    mask_b = nc.dram_tensor("mask_b", [128, 256], BF16, kind="ExternalInput").ap()
    identd = nc.dram_tensor("identd", [128, 128], BF16, kind="ExternalInput").ap()
    bigmd = nc.dram_tensor("bigmd", [128, 896], BF16, kind="ExternalInput").ap()

    out = nc.dram_tensor("out", [TM, D], F32, kind="ExternalOutput").ap()

    xn_dram = nc.dram_tensor("xn_dram", [T, D], BF16).ap()
    s_dram = nc.dram_tensor("s_dram", [TM, D], F32).ap()
    w3q_h = nc.dram_tensor("w3q_dram", [HID, D], BF16)
    w3q_dram = w3q_h.ap()

    tapt = {}
    for name, shape, dt in [
        ("xn", [T, D], BF16), ("k0", [128, T], BF16), ("q0", [128, TM], BF16),
        ("v0", [128, 16 * 512], BF16), ("cat", [128, H * TM], BF16),
        ("sm", [TM, D], F32), ("gam", [128, 16], F32),
        ("hid", [128, NHC * TM], BF16), ("w3q", [HID, D], BF16),
        ("bias0", [128, 1], F32), ("pt0", [128, 2048], BF16),
    ]:
        if name in taps:
            tapt[name] = nc.dram_tensor("tap_" + name, shape, dt,
                                        kind="ExternalOutput").ap()

    M = {}

    with ExitStack() as ctx:
        def sb(name, shape, dt):
            return ctx.enter_context(nc.sbuf_tensor(name, shape, dt))

        def manual(name, shape, dt, side="right"):
            cm = nc.sbuf_tensor(name, shape, dt, side=side)
            return cm, cm.__enter__()

        ones_c = sb("ones_c", [128, 1], BF16)
        ones_r = sb("ones_r", [1, 128], BF16)
        ones_cf = sb("ones_cf", [128, 1], F32)
        ones_rf = sb("ones_rf", [1, 128], F32)
        identb = sb("identb", [128, 128], BF16)
        bigm = sb("bigm", [128, 896], BF16)
        gcols = sb("gcols", [128, 16], F32)
        scol = sb("scol", [128, 4], F32)
        bias4 = sb("bias4", [128, 4], F32)
        tiny = sb("tiny", [1, 16], F32)
        tinyb = sb("tinyb", [1, 4], BF16)
        epsc = sb("epsc", [128, 1], F32)

        pA = ctx.enter_context(nc.psum_tensor("pA", [128, 2048], F32))
        pB = ctx.enter_context(nc.psum_tensor("pB", [128, 2048], F32))

        s_ld = Sem(ctx, nc, "s_ld")
        s_gl = Sem(ctx, nc, "s_gl")
        s_st = Sem(ctx, nc, "s_st")
        s_gs = Sem(ctx, nc, "s_gs")
        s_pe = Sem(ctx, nc, "s_pe")
        s_dv = Sem(ctx, nc, "s_dv")
        s_ac = Sem(ctx, nc, "s_ac")
        s_pl = Sem(ctx, nc, "s_pl")
        s_gq = Sem(ctx, nc, "s_gq")
        s_cast = Sem(ctx, nc, "s_cast")

        xnT_m_cm, xnT_m = manual("xnT_m", [128, NDC * TM], BF16, side="left")
        grs_cm, grs = manual("grs", [128, 1024], F32, side="left")
        grs2_cm, grs2 = manual("grs2", [128, 1024], F32, side="left")
        catT_cm, catT = manual("catT", [128, H * TM], BF16, side="left")
        xnT_r_cm, xnT_r = manual("xnT_r", [128, NDC * TM], BF16, side="left")

        def xnT_chunk(c, tt):
            if tt < 8:
                return xnT_m[:, c * TM + tt * 128: c * TM + (tt + 1) * 128]
            return xnT_r[:, c * TM + (tt - 8) * 128: c * TM + (tt - 7) * 128]

        def xnT_half(c, tc):
            if tc < 2:
                return xnT_m[:, c * TM + tc * 512: c * TM + (tc + 1) * 512]
            return xnT_r[:, c * TM + (tc - 2) * 512: c * TM + (tc - 1) * 512]

        G_ORDER = []
        for j in range(NHC):
            for hh in range(2):
                G_ORDER.append(("w1", j, hh))
                G_ORDER.append(("w3", j, hh))
        for j in range(NHC):
            for hh in range(2):
                G_ORDER.append(("w2", j, hh))
        G_ACT = list(range(len(G_ORDER)))
        gst = {"dma": 0, "act": 0}

        def gamma_slab_in(i):
            m, j, hh = G_ORDER[i]
            if m == "w1":
                return w1t[j][:, hh * 1024:(hh + 1) * 1024]
            if m == "w2":
                return w2t[j][:, hh * 1024:(hh + 1) * 1024]
            return w3r[j * 128:(j + 1) * 128, hh * 1024:(hh + 1) * 1024]

        def gamma_dma(g, count):
            for _ in range(count):
                i = gst["dma"]
                if i >= len(G_ORDER):
                    return
                if i >= 2 and ("gfree", i - 2) not in M:
                    return
                buf = grs if i % 2 == 0 else grs2
                wait(g, M.get(("gfree", i - 2)))
                M[("gdma", i)] = s_gq.mark(
                    g.dma_start(out=buf[:, :], in_=gamma_slab_in(i)), 16)
                gst["dma"] += 1

        def gamma_reduce_dve(v, count):
            for _ in range(count):
                k = gst["act"]
                if k >= len(G_ACT):
                    return
                i = G_ACT[k]
                if ("gdma", i) not in M:
                    return
                buf = grs if i % 2 == 0 else grs2
                col = {"w1": 0, "w2": 1, "w3": 2}[G_ORDER[i][0]]
                wait(v, M[("gdma", i)])
                fence(v, s_dv, v.tensor_reduce(
                    gcols[:, 9:10], buf[:, :], AX.X, ALU.add,
                    apply_absolute_value=True))
                M[("gfree", i)] = fence(v, s_dv, v.tensor_tensor(
                    gcols[:, col:col + 1], gcols[:, col:col + 1],
                    gcols[:, 9:10], ALU.add))
                gst["act"] += 1

        with nc.Block() as blk:
            @blk.sync
            def _(sync):
                s_ld.mark(sync.dma_start(out=identb[:, :], in_=identd), 16)
                M["consts"] = s_ld.mark(sync.dma_start(out=bigm[:, :], in_=bigmd), 16)

            @blk.vector
            def _(v):
                v.memset(ones_c[:, :], 1.0)
                v.memset(ones_r[:, :], 1.0)
                v.memset(ones_cf[:, :], 1.0)
                v.memset(ones_rf[:, :], 1.0)
                v.memset(gcols[:, :], 0.0)
                v.memset(tiny[:, :], 0.0)
                v.memset(epsc[:, :], EPS_LN)

        ln_cms = []
        for nm, shp, dt in [("xbuf", [128, 2 * D], F32), ("xnb", [128, 2 * D], BF16),
                            ("lnsq", [128, D], F32), ("stat", [128, 8], F32)]:
            ln_cms.append(manual(nm, shp, dt))
        xbuf, xnb, lnsq, stat = [h for _, h in ln_cms]

        with nc.Block() as blk:
            @blk.sync
            def _(sync):
                for tt in range(2):
                    M[("lx", tt)] = s_ld.mark(sync.dma_start(
                        out=xbuf[:, tt * D:(tt + 1) * D],
                        in_=xp[tt * 128:(tt + 1) * 128, :]), 16)

            for tt in range(16):
                sl = (tt % 2) * D

                @blk.vector
                def _(v, tt=tt, sl=sl):
                    wait(v, M[("lx", tt)])
                    wait(v, M.get(("lnapp", tt - 1)))
                    fence(v, s_dv, v.tensor_reduce(
                        stat[:, 0:1], xbuf[:, sl:sl + D], AX.X, ALU.add))
                    fence(v, s_dv, v.tensor_scalar(
                        stat[:, 2:3], stat[:, 0:1], 1.0 / D, None, ALU.mult))
                    fence(v, s_dv, v.tensor_tensor(
                        lnsq[:, :], xbuf[:, sl:sl + D], xbuf[:, sl:sl + D],
                        ALU.mult))
                    fence(v, s_dv, v.tensor_reduce(
                        stat[:, 1:2], lnsq[:, :], AX.X, ALU.add))
                    fence(v, s_dv, v.tensor_tensor(
                        stat[:, 5:6], stat[:, 2:3], stat[:, 2:3], ALU.mult))
                    fence(v, s_dv, v.tensor_scalar(
                        stat[:, 3:4], stat[:, 1:2], 1.0 / D, None, ALU.mult))
                    M[("lnvar", tt)] = s_dv.mark(v.tensor_tensor(
                        stat[:, 3:4], stat[:, 3:4], stat[:, 5:6], ALU.subtract))

                @blk.scalar
                def _(a, tt=tt):
                    wait(a, M[("lnvar", tt)])
                    fence(a, s_ac, a.activation(stat[:, 6:7], stat[:, 3:4],
                                                AF.Ln, bias=epsc[:, 0:1]))
                    M[("lnrstd", tt)] = s_ac.mark(a.activation(
                        stat[:, 4:5], stat[:, 6:7], AF.Exp, scale=-0.5))

                @blk.vector
                def _(v, tt=tt, sl=sl):
                    wait(v, M[("lnrstd", tt)])
                    fence(v, s_dv, v.tensor_tensor(
                        stat[:, 5:6], stat[:, 2:3], stat[:, 4:5], ALU.mult))
                    fence(v, s_dv, v.tensor_scalar(
                        stat[:, 5:6], stat[:, 5:6], -1.0, None, ALU.mult))
                    wait(v, M.get(("xnst", tt - 2)))
                    M[("lnapp", tt)] = s_dv.mark(v.tensor_scalar(
                        xnb[:, sl:sl + D], xbuf[:, sl:sl + D],
                        stat[:, 4:5], stat[:, 5:6], ALU.mult, ALU.add))

                @blk.sync
                def _(sync, tt=tt, sl=sl):
                    wait(sync, M[("lnapp", tt)])
                    M[("xnst", tt)] = s_st.mark(sync.dma_start(
                        out=xn_dram[tt * 128:(tt + 1) * 128, :],
                        in_=xnb[:, sl:sl + D]), 16)
                    if tt + 2 < 16:
                        M[("lx", tt + 2)] = s_ld.mark(sync.dma_start(
                            out=xbuf[:, sl:sl + D],
                            in_=xp[(tt + 2) * 128:(tt + 3) * 128, :]), 16)

        with nc.Block() as blk:
            @blk.sync
            def _(sync):
                sync.wait_ge(s_st.h, s_st.n)
                for c in range(NDC):
                    s_ld.mark(sync.dma_start(
                        out=xnT_m[:, c * TM:(c + 1) * TM],
                        in_=xn_dram[0:TM, c * 128:(c + 1) * 128],
                        transpose=True), 16)
                    s_ld.mark(sync.dma_start(
                        out=xnT_r[:, c * TM:(c + 1) * TM],
                        in_=xn_dram[TM:T, c * 128:(c + 1) * 128],
                        transpose=True), 16)
                M["xt_all"] = s_ld.mile()

            for _ in range(16):
                @blk.sync
                def _(sync):
                    gamma_dma(sync, 2)

                @blk.vector
                def _(v):
                    gamma_reduce_dve(v, 2)

        if "xn" in tapt:
            with nc.Block() as blk:
                @blk.sync
                def _(sync):
                    mm = s_st.mark(sync.dma_start(out=tapt["xn"], in_=xn_dram), 16)
                    wait(sync, mm)

        for cm, _h in reversed(ln_cms):
            cm.__exit__(None, None, None)

        att_cms = []
        for nm, shp, dt in [("wqkb", [128, 3 * NDC * 256], BF16),
                            ("wvb", [128, NDC * 512], BF16),
                            ("Kh", [128, 4 * T], BF16),
                            ("Qh", [128, 4 * TM], BF16),
                            ("Vg", [128, 16 * 512], BF16),
                            ("PT", [128, 2 * 2048], BF16),
                            ("rrow", [1, 256], F32),
                            ("rrb", [1, 256], BF16),
                            ("rbc", [128, 256], BF16)]:
            att_cms.append(manual(nm, shp, dt))
        (wqkb, wvb, Kh, Qh, Vg, PT, rrow, rrb, rbc) = \
            [h for _, h in att_cms]
        sqb = PT[:, 0:2048]

        def pair_kbs(p):
            return list(range(0, 2 * p + 2)) + list(range(8, 8 + 2 * p + 2))

        def wqk_slot(h):
            return (h % 3) * 4096

        def emit_wqk_load(blk, h):
            @blk.gpsimd
            def _(gq, h=h):
                wait(gq, M.get(("qmm", h - 3)))
                M[("wqk", h)] = s_cast.mark(gq.dma_start(
                    out=wqkb[:, wqk_slot(h):wqk_slot(h) + 4096],
                    in_=wqk[h]), 16)

        def emit_wv_load(blk, g, guard):
            @blk.gpsimd
            def _(gq, g=g, guard=guard):
                wait(gq, guard)
                gq.dma_start(out=wvb[:, 0:4096], in_=wv[g][:, 0:4096])\
                    .then_inc(s_cast.h, 16)
                s_cast.n += 16
                M[("wvb", g)] = s_cast.mark(gq.dma_start(
                    out=wvb[:, 4096:8192], in_=wv[g][:, 4096:8192]), 16)

        for g in range(4):
            with nc.Block() as blk:
                if g == 0:
                    emit_wv_load(blk, 0, None)
                    emit_wqk_load(blk, 0)
                    emit_wqk_load(blk, 1)
                    emit_wqk_load(blk, 2)

                @blk.sync
                def _(sync, g=g):
                    gamma_dma(sync, 4)



                for tt in range(16):
                    bank = 1024 + (tt % 2) * 512

                    @blk.tensor
                    def _(t, g=g, tt=tt, bank=bank):
                        if tt == 0:
                            wait(t, M[("wvb", g)])
                            wait(t, M["xt_all"])
                        wait(t, M.get(("vev", g, tt - 2)))
                        last = None
                        for c in range(NDC):
                            last = t.matmul(pB[:, bank:bank + 512],
                                            xnT_chunk(c, tt),
                                            wvb[:, c * 512:(c + 1) * 512],
                                            start=(c == 0), stop=(c == NDC - 1))
                        M[("vmm", g, tt)] = s_pe.mark(last)

                    @blk.vector
                    def _(v, g=g, tt=tt, bank=bank):
                        wait(v, M[("vmm", g, tt)])
                        M[("vev", g, tt)] = s_dv.mark(v.tensor_copy(
                            Vg[:, tt * 512:(tt + 1) * 512], pB[:, bank:bank + 512]))

                for i in range(4):
                    h = g * 4 + i
                    wq = wqkb[:, wqk_slot(h):wqk_slot(h) + 4096]
                    khs = Kh[:, i * T:(i + 1) * T]
                    qhs = Qh[:, i * TM:(i + 1) * TM]

                    @blk.tensor
                    def _(t, h=h, i=i, g=g, wq=wq):
                        wait(t, M[("wqk", h)])
                        if i == 0:
                            wait(t, M.get(("exp_last", h - 1)))
                        else:
                            wait(t, M[("qnred", h - 1)])
                        last = None
                        for c in range(NDC):
                            lh = wq[:, c * 256 + 128: c * 256 + 256]
                            for tc in range(4):
                                last = t.matmul(pA[:, tc * 512:(tc + 1) * 512],
                                                lh, xnT_half(c, tc),
                                                start=(c == 0), stop=(c == NDC - 1))
                        M[("kmm", h)] = s_pe.mark(last)

                    @blk.vector
                    def _(v, h=h, khs=khs):
                        wait(v, M[("kmm", h)])
                        M[("kev", h)] = fence(v, s_dv, v.tensor_copy(khs, pA[:, :]))
                        M[("ksq", h)] = s_dv.mark(v.tensor_tensor(
                            sqb[:, :], khs, khs, ALU.mult))

                    @blk.tensor
                    def _(t, h=h, i=i, wq=wq):
                        if i == 0:
                            wait(t, M.get(("cat", h - 1, 3)))
                        else:
                            wait(t, M[("qev", h - 1)])
                        last = None
                        for c in range(NDC):
                            lh = wq[:, c * 256: c * 256 + 128]
                            for tc in range(2):
                                last = t.matmul(
                                    pB[:, tc * 512:(tc + 1) * 512], lh,
                                    xnT_m[:, c * TM + tc * 512: c * TM + (tc + 1) * 512],
                                    start=(c == 0), stop=(c == NDC - 1))
                        M[("qmm", h)] = s_pe.mark(last)

                    @blk.vector
                    def _(v, h=h, qhs=qhs):
                        wait(v, M[("qmm", h)])
                        M[("qev", h)] = fence(v, s_dv, v.tensor_copy(
                            qhs, pB[:, 0:1024]))
                        M[("qsq", h)] = s_dv.mark(v.tensor_tensor(
                            sqb[:, 0:1024], qhs, qhs, ALU.mult))

                    if g * 4 + i + 3 < H:
                        emit_wqk_load(blk, g * 4 + i + 3)

                    @blk.tensor
                    def _(t, h=h):
                        wait(t, M[("ksq", h)])
                        last = None
                        for j in range(4):
                            last = t.matmul(pA[0:1, j * 512:(j + 1) * 512],
                                            ones_c[:, :],
                                            sqb[:, j * 512:(j + 1) * 512],
                                            start=True, stop=True)
                        M[("nK", h)] = s_pe.mark(last)

                    @blk.vector
                    def _(v, h=h, i=i):
                        wait(v, M[("nK", h)])
                        M[("nKred", h)] = fence(v, s_dv, v.tensor_reduce(
                            tiny[0:1, 4 + i:5 + i], pA[0:1, 0:2048], AX.X,
                            ALU.max))

                    @blk.tensor
                    def _(t, h=h):
                        wait(t, M[("qsq", h)])
                        wait(t, M[("nKred", h)])
                        last = None
                        for j in range(2):
                            last = t.matmul(pA[0:1, j * 512:(j + 1) * 512],
                                            ones_c[:, :],
                                            sqb[:, j * 512:(j + 1) * 512],
                                            start=True, stop=True)
                        M[("nQ", h)] = s_pe.mark(last)

                    @blk.vector
                    def _(v, h=h, i=i):
                        wait(v, M[("nQ", h)])
                        M[("qnred", h)] = fence(v, s_dv, v.tensor_reduce(
                            tiny[0:1, 8 + i:9 + i], pA[0:1, 0:1024], AX.X,
                            ALU.max))

                @blk.vector
                def _(v, g=g):
                    wait(v, M[("qnred", g * 4 + 3)])
                    wait(v, M.get(("exp_last", g * 4 - 1)))
                    M[("m2g", g)] = fence(v, s_dv, v.tensor_tensor(
                        tiny[0:1, 4:8], tiny[0:1, 4:8], tiny[0:1, 8:12],
                        ALU.mult))

                @blk.scalar
                def _(a, g=g):
                    wait(a, M[("m2g", g)])
                    fence(a, s_ac, a.activation(tiny[0:1, 8:12], tiny[0:1, 4:8],
                                                AF.Ln))
                    M[("mhg", g)] = s_ac.mark(a.activation(
                        tiny[0:1, 4:8], tiny[0:1, 8:12], AF.Exp, scale=0.5))

                @blk.vector
                def _(v, g=g):
                    wait(v, M[("mhg", g)])
                    fence(v, s_dv, v.tensor_scalar(
                        tiny[0:1, 4:8], tiny[0:1, 4:8], -SCL, None, ALU.mult))
                    M[("negmg", g)] = s_dv.mark(v.tensor_copy(
                        tinyb[0:1, 0:4], tiny[0:1, 4:8]))

                @blk.tensor
                def _(t, g=g):
                    wait(t, M[("negmg", g)])
                    M[("biasmm", g)] = s_pe.mark(t.matmul(
                        pB[:, 1536:1540], ones_r[:, :], tinyb[0:1, 0:4],
                        start=True, stop=True))

                @blk.vector
                def _(v, g=g):
                    wait(v, M[("biasmm", g)])
                    M[("bias_g", g)] = s_dv.mark(v.tensor_copy(
                        bias4[:, 0:4], pB[:, 1536:1540]))

                pend = []

                def flush_pend():
                    while pend:
                        pend.pop(0)()

                bg = 0
                for i in range(4):
                    h = g * 4 + i
                    khs = Kh[:, i * T:(i + 1) * T]
                    qhs = Qh[:, i * TM:(i + 1) * TM]
                    for p in range(4):
                        pg = i * 4 + p
                        q = pg % 2
                        av0 = q * 512
                        bc0 = q * 512 + 256
                        rs0 = 1024 + q * 512
                        kbs = pair_kbs(p)
                        nkb = len(kbs)
                        batches = [kbs[j:j + 8] for j in range(0, nkb, 8)]
                        kidx = 0
                        for bi, batch in enumerate(batches):
                            nb = len(batch)
                            half = (bg % 2) * 2048
                            mybg = bg

                            @blk.tensor
                            def _(t, h=h, g=g, p=p, bi=bi, batch=batch,
                                  khs=khs, qhs=qhs, half=half, mybg=mybg):
                                if mybg < 1:
                                    wait(t, M[("qnred", g * 4 + 3)])
                                else:
                                    wait(t, M[("expg", g, mybg - 1)])
                                last = None
                                for idx, kb in enumerate(batch):
                                    dst = pA[:, idx * 256:(idx + 1) * 256]
                                    if kb == 2 * p:
                                        t.matmul(dst, khs[:, kb * 128:(kb + 1) * 128],
                                                 qhs[:, p * 256:(p + 1) * 256],
                                                 start=True, stop=False)
                                        last = t.matmul(
                                            pA[:, idx * 256: idx * 256 + 128],
                                            identb[:, :], bigm[:, 0:128],
                                            start=False, stop=True)
                                    elif kb == 2 * p + 1:
                                        t.matmul(dst, khs[:, kb * 128:(kb + 1) * 128],
                                                 qhs[:, p * 256:(p + 1) * 256],
                                                 start=True, stop=False)
                                        last = t.matmul(
                                            dst, identb[:, :], bigm[:, 128:384],
                                            start=False, stop=True)
                                    elif kb == 8 + 2 * p:
                                        t.matmul(dst, khs[:, kb * 128:(kb + 1) * 128],
                                                 qhs[:, p * 256:(p + 1) * 256],
                                                 start=True, stop=False)
                                        last = t.matmul(
                                            dst, identb[:, :], bigm[:, 384:640],
                                            start=False, stop=True)
                                    elif kb == 9 + 2 * p:
                                        t.matmul(dst, khs[:, kb * 128:(kb + 1) * 128],
                                                 qhs[:, p * 256:(p + 1) * 256],
                                                 start=True, stop=False)
                                        last = t.matmul(
                                            dst, identb[:, :], bigm[:, 640:896],
                                            start=False, stop=True)
                                    else:
                                        last = t.matmul(
                                            dst, khs[:, kb * 128:(kb + 1) * 128],
                                            qhs[:, p * 256:(p + 1) * 256],
                                            start=True, stop=True)
                                M[("scg", g, mybg)] = s_pe.mark(last)

                            @blk.scalar
                            def _(a, g=g, i=i, nb=nb, half=half, mybg=mybg):
                                wait(a, M[("scg", g, mybg)])
                                if mybg == 0:
                                    wait(a, M[("bias_g", g)])
                                wait(a, M.get(("rsavg", g, mybg - 2)))
                                M[("expg", g, mybg)] = s_ac.mark(a.activation(
                                    PT[:, half:half + nb * 256],
                                    pA[:, 0:nb * 256],
                                    AF.Exp, scale=SCL, bias=bias4[:, i:i + 1]))

                            if "pt0" in tapt and h == 0 and p == 1 and bi == 0:
                                @blk.sync
                                def _(sync):
                                    wait(sync, M[("expg", 0, 2)])
                                    mm = s_st.mark(sync.dma_start(
                                        out=tapt["pt0"], in_=PT[:, 0:2048]), 16)
                                    wait(sync, mm)

                            @blk.tensor
                            def _(t, h=h, g=g, p=p, bi=bi, batch=batch,
                                  kidx=kidx, nkb=nkb, half=half,
                                  mybg=mybg, pg=pg, av0=av0, rs0=rs0):
                                wait(t, M[("expg", g, mybg)])
                                if bi == 0:
                                    wait(t, M.get(("catg", g, pg - 2)))
                                last = None
                                for idx, kb in enumerate(batch):
                                    kg = kidx + idx
                                    t.matmul(pB[0:1, rs0:rs0 + 256], ones_c[:, :],
                                             PT[:, half + idx * 256:
                                                half + (idx + 1) * 256],
                                             start=(kg == 0), stop=(kg == nkb - 1))
                                    u0 = kb * 512 + (h - 4 * g) * 128
                                    last = t.matmul(
                                        pB[:, av0:av0 + 256], Vg[:, u0:u0 + 128],
                                        PT[:, half + idx * 256:
                                           half + (idx + 1) * 256],
                                        start=(kg == 0), stop=(kg == nkb - 1))
                                M[("rsavg", g, mybg)] = s_pe.mark(last)

                            kidx += len(batch)
                            bg += 1
                            if bi == 0:
                                flush_pend()

                        last_bg = bg - 1

                        def tail(h=h, g=g, p=p, pg=pg, av0=av0, bc0=bc0,
                                 rs0=rs0, last_bg=last_bg):
                            @blk.vector
                            def _(v):
                                wait(v, M[("rsavg", g, last_bg)])
                                fence(v, s_dv, v.tensor_copy(rrow[0:1, :],
                                                             pB[0:1, rs0:rs0 + 256]))
                                fence(v, s_dv, v.reciprocal(rrow[0:1, :],
                                                            rrow[0:1, :]))
                                M[("rrbg", g, pg)] = s_dv.mark(v.tensor_copy(
                                    rrb[0:1, :], rrow[0:1, :]))

                            @blk.tensor
                            def _(t):
                                wait(t, M[("rrbg", g, pg)])
                                M[("bcastg", g, pg)] = s_pe.mark(t.matmul(
                                    pB[:, bc0:bc0 + 256], ones_r[:, :],
                                    rrb[0:1, :], start=True, stop=True))

                            @blk.vector
                            def _(v):
                                wait(v, M[("bcastg", g, pg)])
                                fence(v, s_dv, v.tensor_copy(rbc[:, :],
                                                             pB[:, bc0:bc0 + 256]))
                                M[("catg", g, pg)] = s_dv.mark(v.tensor_tensor(
                                    catT[:, h * TM + p * 256:
                                         h * TM + (p + 1) * 256],
                                    pB[:, av0:av0 + 256], rbc[:, :], ALU.mult))
                                if p == 3:
                                    M[("cat", h, 3)] = M[("catg", g, pg)]

                        pend.append(tail)

                        for _ in range(2):
                            @blk.sync
                            def _(sync):
                                gamma_dma(sync, 2)

                            @blk.vector
                            def _(v):
                                gamma_reduce_dve(v, 2)

                flush_pend()

                if g < 3:
                    emit_wv_load(blk, g + 1, M[("vmm", g, 15)])

        for name, key in [("k0", ("kev", 0)), ("q0", ("qev", 0))]:
            if name in tapt:
                src = Kh[:, 0:T] if name == "k0" else Qh[:, 0:TM]
                with nc.Block() as blk:
                    @blk.sync
                    def _(sync, name=name, key=key, src=src):
                        wait(sync, M[key])
                        mm = s_st.mark(sync.dma_start(out=tapt[name], in_=src), 16)
                        wait(sync, mm)
        if "v0" in tapt:
            with nc.Block() as blk:
                @blk.sync
                def _(sync):
                    mm = s_st.mark(sync.dma_start(out=tapt["v0"], in_=Vg[:, :]), 16)
                    wait(sync, mm)
        if "cat" in tapt:
            with nc.Block() as blk:
                @blk.sync
                def _(sync):
                    wait(sync, M[("cat", 15, 3)])
                    mm = s_st.mark(sync.dma_start(out=tapt["cat"], in_=catT[:, :]),
                                   16)
                    wait(sync, mm)

        for cm, _h in reversed(att_cms):
            cm.__exit__(None, None, None)
        xnT_r_cm.__exit__(None, None, None)

        op_cms = [manual("woTb", [128, NDC * D], BF16),
                  manual("sstage", [128, 2 * D], F32)]
        woTb, sstage = [h for _, h in op_cms]

        with nc.Block() as blk:
            @blk.gpsimd
            def _(gq):
                for c in range(NDC):
                    M[("wo", c)] = s_cast.mark(gq.dma_start(
                        out=woTb[:, c * D:(c + 1) * D], in_=wo[c]), 16)
            for _ in range(12):
                @blk.sync
                def _(sync):
                    gamma_dma(sync, 2)

                @blk.vector
                def _(v):
                    gamma_reduce_dve(v, 2)

            for tt in range(8):
                pg = pA if tt % 2 == 0 else pB

                @blk.tensor
                def _(t, tt=tt, pg=pg):
                    if tt == 0:
                        wait(t, M[("wo", NDC - 1)])
                    wait(t, M.get(("sev", tt - 2)))
                    last = None
                    for c in range(NDC):
                        lh = catT[:, c * TM + tt * 128: c * TM + (tt + 1) * 128]
                        for oc in range(4):
                            last = t.matmul(
                                pg[:, oc * 512:(oc + 1) * 512], lh,
                                woTb[:, c * D + oc * 512: c * D + (oc + 1) * 512],
                                start=(c == 0), stop=(c == NDC - 1))
                    M[("smm", tt)] = s_pe.mark(last)

                @blk.vector
                def _(v, tt=tt, pg=pg):
                    wait(v, M[("smm", tt)])
                    wait(v, M.get(("sst", tt - 2)))
                    sl = (tt % 2) * D
                    M[("sev", tt)] = s_dv.mark(v.tensor_copy(
                        sstage[:, sl:sl + D], pg[:, :]))

                @blk.sync
                def _(sync, tt=tt):
                    wait(sync, M[("sev", tt)])
                    sl = (tt % 2) * D
                    M[("sst", tt)] = s_st.mark(sync.dma_start(
                        out=s_dram[tt * 128:(tt + 1) * 128, :],
                        in_=sstage[:, sl:sl + D]), 16)

            M["s_all"] = s_st.mile()

        if "sm" in tapt:
            with nc.Block() as blk:
                @blk.sync
                def _(sync):
                    sync.wait_ge(s_st.h, s_st.n)
                    mm = s_st.mark(sync.dma_start(out=tapt["sm"], in_=s_dram), 16)
                    wait(sync, mm)

        for cm, _h in reversed(op_cms):
            cm.__exit__(None, None, None)
        catT_cm.__exit__(None, None, None)

        with nc.Block() as blk:
            for _ in range(140):
                if gst["dma"] >= len(G_ORDER) and gst["act"] >= len(G_ACT):
                    break

                @blk.sync
                def _(sync):
                    gamma_dma(sync, 2)

                @blk.vector
                def _(v):
                    gamma_reduce_dve(v, 2)

            @blk.tensor
            def _(t):
                t.wait_ge(s_dv.h, s_dv.n)
                last = None
                for j in range(3):
                    last = t.matmul(pA[0:1, j * 512:j * 512 + 1], ones_cf[:, :],
                                    gcols[:, j:j + 1], start=True, stop=True)
                M[("gsum")] = s_pe.mark(last)

            @blk.vector
            def _(v):
                wait(v, M["gsum"])
                mm = None
                for j in range(3):
                    fence(v, s_dv, v.tensor_copy(tiny[0:1, j:j + 1],
                                                 pA[0:1, j * 512:j * 512 + 1]))
                    fence(v, s_dv, v.tensor_scalar(
                        tiny[0:1, j:j + 1], tiny[0:1, j:j + 1],
                        1.0 / (HID * D), EPS_Q, ALU.mult, ALU.add))
                    mm = fence(v, s_dv, v.reciprocal(tiny[0:1, j:j + 1],
                                                     tiny[0:1, j:j + 1]))
                M["ginv"] = mm

            @blk.tensor
            def _(t):
                wait(t, M["ginv"])
                last = None
                for j in range(3):
                    last = t.matmul(pB[:, j * 512:j * 512 + 1], ones_rf[:, :],
                                    tiny[0:1, j:j + 1], start=True, stop=True)
                M["gbc"] = s_pe.mark(last)

            @blk.vector
            def _(v):
                wait(v, M["gbc"])
                last = None
                for j in range(3):
                    last = v.tensor_copy(scol[:, j:j + 1], pB[:, j * 512:j * 512 + 1])
                M["scol"] = s_dv.mark(last)

        if "gam" in tapt:
            with nc.Block() as blk:
                @blk.sync
                def _(sync):
                    wait(sync, M["scol"])
                    mm = s_st.mark(sync.dma_start(out=tapt["gam"], in_=gcols[:, :]),
                                   16)
                    wait(sync, mm)

        grs2_cm.__exit__(None, None, None)
        grs_cm.__exit__(None, None, None)

        hid_cm, hidT = manual("hidT", [128, NHC * TM], BF16)
        ffn_cms = []
        for nm, shp, dt in [("w1f", [128, 2 * D], F32), ("w2f", [128, 2 * D], F32),
                            ("w1qb", [128, 2 * D], BF16), ("w2qb", [128, 2 * D], BF16),
                            ("w3f", [128, D], F32), ("w3qb", [128, D], BF16),
                            ("qi", [128, 2 * D], F32),
                            ("sigb", [128, TM], F32)]:
            ffn_cms.append(manual(nm, shp, dt))
        (w1f, w2f, w1qb, w2qb, w3f, w3qb, qi, sigb) = \
            [h for _, h in ffn_cms]

        def qi_slot(k):
            return qi[:, (k % 2) * D:(k % 2 + 1) * D]

        with nc.Block() as blk:
            for hc in range(NHC):
                sl = (hc % 2)

                @blk.sync
                def _(sync, hc=hc, sl=sl):
                    wait(sync, M.get(("q1a", hc - 1)))
                    M[("w1l", hc)] = s_ld.mark(sync.dma_start(
                        out=w1f[:, sl * D:(sl + 1) * D], in_=w1t[hc]), 16)
                    wait(sync, M.get(("q2a", hc - 1)))
                    M[("w2l", hc)] = s_ld.mark(sync.dma_start(
                        out=w2f[:, sl * D:(sl + 1) * D], in_=w2t[hc]), 16)

                @blk.gpsimd
                def _(gq, hc=hc):
                    wait(gq, M.get(("q3a", hc - 1)))
                    M[("w3l", hc)] = s_gl.mark(gq.dma_start(
                        out=w3f[:, :], in_=w3r[hc * 128:(hc + 1) * 128, :]), 16)

                @blk.gpsimd
                def _(a, hc=hc):
                    wait(a, M[("w3l", hc)])
                    if hc == 0:
                        wait(a, M["scol"])
                    wait(a, M.get(("dvq", 3 * hc + 1 - 2)))
                    M[("q3a", hc)] = s_pl.mark(a.tensor_scalar(
                        qi_slot(3 * hc + 1), w3f[:, :], scol[:, 2:3], None,
                        ALU.mult))

                @blk.vector
                def _(v, hc=hc):
                    wait(v, M[("q3a", hc)])
                    wait(v, M.get(("w3st", hc - 1)))
                    fence(v, s_dv, v.tensor_scalar(
                        qi_slot(3 * hc + 1), qi_slot(3 * hc + 1),
                        QMAGIC, -QMAGIC, ALU.add, ALU.add))
                    M[("w3q", hc)] = M[("dvq", 3 * hc + 1)] = fence(
                        v, s_dv, v.tensor_scalar(
                            w3qb[:, :], qi_slot(3 * hc + 1),
                            1.0, -1.0, ALU.min, ALU.max))

                @blk.gpsimd
                def _(gq, hc=hc):
                    wait(gq, M[("w3q", hc)])
                    M[("w3st", hc)] = s_gs.mark(gq.dma_start(
                        out=w3q_dram[hc * 128:(hc + 1) * 128, :],
                        in_=w3qb[:, :]), 16)

                @blk.gpsimd
                def _(a, hc=hc, sl=sl):
                    wait(a, M[("w1l", hc)])
                    if hc == 0:
                        wait(a, M["scol"])
                    wait(a, M.get(("dvq", 3 * hc + 0 - 2)))
                    M[("q1a", hc)] = s_pl.mark(a.tensor_scalar(
                        qi_slot(3 * hc + 0),
                        w1f[:, sl * D:(sl + 1) * D], scol[:, 0:1], None,
                        ALU.mult))

                @blk.vector
                def _(v, hc=hc, sl=sl):
                    wait(v, M[("q1a", hc)])
                    wait(v, M.get(("x12", hc - 2)))
                    fence(v, s_dv, v.tensor_scalar(
                        qi_slot(3 * hc + 0), qi_slot(3 * hc + 0),
                        QMAGIC, -QMAGIC, ALU.add, ALU.add))
                    M[("w1q", hc)] = M[("dvq", 3 * hc + 0)] = fence(
                        v, s_dv, v.tensor_scalar(
                            w1qb[:, sl * D:(sl + 1) * D], qi_slot(3 * hc + 0),
                            1.0, -1.0, ALU.min, ALU.max))

                @blk.scalar
                def _(a, hc=hc, sl=sl):
                    wait(a, M[("w2l", hc)])
                    M[("q2a", hc)] = s_ac.mark(a.activation(
                        w2f[:, (1 - sl) * D:(2 - sl) * D].bitcast(mybir.dt.int32),
                        w2f[:, sl * D:(sl + 1) * D], AF.Identity,
                        scale=scol[:, 1:2]))

                @blk.vector
                def _(v, hc=hc, sl=sl):
                    wait(v, M[("q2a", hc)])
                    wait(v, M.get(("x12", hc - 2)))
                    M[("w12q", hc)] = fence(v, s_dv, v.tensor_scalar(
                        w2qb[:, sl * D:(sl + 1) * D],
                        w2f[:, (1 - sl) * D:(2 - sl) * D].bitcast(mybir.dt.int32),
                        1.0, -1.0, ALU.min, ALU.max))

                pg = pA if hc % 2 == 0 else pB

                @blk.tensor
                def _(t, hc=hc, sl=sl, pg=pg):
                    wait(t, M[("w1q", hc)])
                    wait(t, M[("w12q", hc)])
                    wait(t, M.get(("hid", hc - 2)))
                    last = None
                    for th in range(2):
                        for c in range(NDC):
                            t.matmul(pg[:, th * 512:(th + 1) * 512],
                                     w1qb[:, sl * D + c * 128: sl * D + (c + 1) * 128],
                                     xnT_m[:, c * TM + th * 512: c * TM + (th + 1) * 512],
                                     start=(c == 0), stop=(c == NDC - 1))
                    for th in range(2):
                        for c in range(NDC):
                            last = t.matmul(
                                pg[:, 1024 + th * 512: 1024 + (th + 1) * 512],
                                w2qb[:, sl * D + c * 128: sl * D + (c + 1) * 128],
                                xnT_m[:, c * TM + th * 512: c * TM + (th + 1) * 512],
                                start=(c == 0), stop=(c == NDC - 1))
                    M[("x12", hc)] = s_pe.mark(last)

                @blk.scalar
                def _(a, hc=hc, pg=pg):
                    wait(a, M[("x12", hc)])
                    wait(a, M.get(("hid", hc - 1)))
                    M[("sig", hc)] = s_ac.mark(a.activation(
                        sigb[:, :], pg[:, 0:1024], AF.Sigmoid))

                @blk.vector
                def _(v, hc=hc, pg=pg):
                    wait(v, M[("sig", hc)])
                    M[("hid", hc)] = s_dv.mark(v.tensor_tensor(
                        hidT[:, hc * TM:(hc + 1) * TM], pg[:, 1024:2048],
                        sigb[:, :], ALU.mult))

        if "hid" in tapt:
            with nc.Block() as blk:
                @blk.sync
                def _(sync):
                    wait(sync, M[("hid", NHC - 1)])
                    mm = s_st.mark(sync.dma_start(out=tapt["hid"], in_=hidT[:, :]),
                                   16)
                    wait(sync, mm)
        if "w3q" in tapt:
            with nc.Block() as blk:
                @blk.sync
                def _(sync):
                    sync.wait_ge(s_gs.h, s_gs.n)
                    mm = s_st.mark(sync.dma_start(out=tapt["w3q"], in_=w3q_dram),
                                   16)
                    wait(sync, mm)

        for cm, _h in reversed(ffn_cms):
            cm.__exit__(None, None, None)

        f2_cms = [manual("w3qd", [128, NHC * 512], BF16),
                  manual("gt", [128, 2 * 512], F32),
                  manual("stile", [128, 8 * 512], F32),
                  manual("xtile", [128, 8 * 512], F32),
                  manual("otile", [128, 2 * 512], F32)]
        w3qd, gt, stile, xtile, otile = [h for _, h in f2_cms]

        with nc.Block() as blk:
            for dg in range(4):
                @blk.sync
                def _(sync, dg=dg):
                    if dg == 0:
                        sync.wait_ge(s_gs.h, s_gs.n)
                        sync.wait_ge(s_st.h, M["s_all"][1])
                    wait(sync, M.get(("ffdone", dg - 1)))
                    M[("w3qd", dg)] = s_ld.mark(sync.dma_start(
                        out=w3qd[:, :],
                        in_=bass.AP(w3q_h, dg * 512,
                                    [[D, 128], [128 * D, NHC], [1, 512]])), 16)
                    for tt in range(8):
                        wait(sync, M.get(("comb", dg - 1, tt)))
                        M[("sxl", dg, tt)] = s_ld.mark(sync.dma_start(
                            out=stile[:, tt * 512:(tt + 1) * 512],
                            in_=s_dram[tt * 128:(tt + 1) * 128,
                                       dg * 512:(dg + 1) * 512]), 16)
                        M[("sxl2", dg, tt)] = s_ld.mark(sync.dma_start(
                            out=xtile[:, tt * 512:(tt + 1) * 512],
                            in_=xp[tt * 128:(tt + 1) * 128,
                                   dg * 512:(dg + 1) * 512]), 16)

                for tt in range(8):
                    @blk.tensor
                    def _(t, dg=dg, tt=tt):
                        if tt == 0:
                            wait(t, M[("w3qd", dg)])
                        wait(t, M.get(("silu", dg - 1, tt)))
                        pg = pA if tt < 4 else pB
                        col = (tt % 4) * 512
                        last = None
                        for hc in range(NHC):
                            last = t.matmul(
                                pg[:, col:col + 512],
                                hidT[:, hc * TM + tt * 128: hc * TM + (tt + 1) * 128],
                                w3qd[:, hc * 512:(hc + 1) * 512],
                                start=(hc == 0), stop=(hc == NHC - 1))
                        M[("fft", dg, tt)] = s_pe.mark(last)
                        if tt == 7:
                            M[("ffdone", dg)] = M[("fft", dg, tt)]

                for tt in range(8):
                    pg = pA if tt < 4 else pB
                    col = (tt % 4) * 512

                    @blk.scalar
                    def _(a, dg=dg, tt=tt, pg=pg, col=col):
                        wait(a, M[("fft", dg, tt)])
                        wait(a, M.get(("comb", dg, tt - 2)))
                        M[("silu", dg, tt)] = s_ac.mark(a.activation(
                            gt[:, (tt % 2) * 512:(tt % 2 + 1) * 512],
                            pg[:, col:col + 512], AF.Silu))

                    @blk.vector
                    def _(v, dg=dg, tt=tt):
                        wait(v, M[("silu", dg, tt)])
                        wait(v, M[("sxl", dg, tt)])
                        wait(v, M[("sxl2", dg, tt)])
                        wait(v, M.get(("ost", dg, tt - 2)))
                        o = otile[:, (tt % 2) * 512:(tt % 2 + 1) * 512]
                        fence(v, s_dv, v.tensor_tensor(
                            o, gt[:, (tt % 2) * 512:(tt % 2 + 1) * 512],
                            stile[:, tt * 512:(tt + 1) * 512], ALU.mult))
                        M[("comb", dg, tt)] = s_dv.mark(v.tensor_tensor(
                            o, o, xtile[:, tt * 512:(tt + 1) * 512], ALU.add))

                    @blk.sync
                    def _(sync, dg=dg, tt=tt):
                        wait(sync, M[("comb", dg, tt)])
                        M[("ost", dg, tt)] = s_st.mark(sync.dma_start(
                            out=out[tt * 128:(tt + 1) * 128,
                                    dg * 512:(dg + 1) * 512],
                            in_=otile[:, (tt % 2) * 512:(tt % 2 + 1) * 512]), 16)

        with nc.Block() as blk:
            @blk.sync
            def _(sync):
                sync.wait_ge(s_st.h, s_st.n)

            @blk.gpsimd
            def _(gq):
                gq.wait_ge(s_gs.h, s_gs.n)

        for cm, _h in reversed(f2_cms):
            cm.__exit__(None, None, None)
        hid_cm.__exit__(None, None, None)
        xnT_m_cm.__exit__(None, None, None)

    return nc



def _prep_weights(wp, wout, w1, w2, w3):
    wp = np.asarray(wp, np.float32)
    wq = wp[0:D].reshape(H, 128, NDC, 128)
    wk = wp[D:2 * D].reshape(H, 128, NDC, 128)
    wqk = np.stack([wq.transpose(0, 3, 2, 1), wk.transpose(0, 3, 2, 1)],
                   axis=3)
    wqk = np.ascontiguousarray(wqk.reshape(H, 128, NDC * 256))
    wvv = wp[2 * D:3 * D].reshape(4, 512, NDC, 128)
    wvv = np.ascontiguousarray(wvv.transpose(0, 3, 2, 1).reshape(4, 128, NDC * 512))
    wo_t = np.ascontiguousarray(
        np.asarray(wout, np.float32).T.reshape(NDC, 128, D))
    w1t = np.ascontiguousarray(
        np.asarray(w1, np.float32).reshape(NHC, 128, NDC, 128)
        .transpose(0, 3, 2, 1).reshape(NHC, 128, D))
    w2t = np.ascontiguousarray(
        np.asarray(w2, np.float32).reshape(NHC, 128, NDC, 128)
        .transpose(0, 3, 2, 1).reshape(NHC, 128, D))
    w3r = np.ascontiguousarray(np.asarray(w3, np.float32).T)
    return wqk, wvv, wo_t, w1t, w2t, w3r


def _masks(par):
    ii = np.arange(128)[:, None]
    jj = np.arange(128)[None, :]
    md = (ii <= jj).astype(ml_dtypes.bfloat16)
    ma = np.ones((128, 256), ml_dtypes.bfloat16)
    mb = np.zeros((128, 256), ml_dtypes.bfloat16)
    if par == 0:
        ma[:, 0:128] = 0
    else:
        mb[:, 128:256] = 1
    return md, ma, mb


NEG = np.float32(-1e9)


def _bigmask(par):
    dg = np.where(np.arange(128)[:, None] <= np.arange(128)[None, :],
                  np.float32(0), NEG)
    full = np.full((128, 128), NEG, np.float32)
    zero = np.zeros((128, 128), np.float32)
    eb = np.concatenate([full, dg], 1)
    if par == 0:
        ab = np.concatenate([full, zero], 1)
        bb = np.concatenate([full, full], 1)
    else:
        ab = np.concatenate([zero, zero], 1)
        bb = np.concatenate([full, zero], 1)
    bigm = np.concatenate([dg, eb, ab, bb], 1)
    return np.ascontiguousarray(bigm.astype(ml_dtypes.bfloat16))


def _perm_rows(par):
    mine = [par + 2 * j for j in range(8)]
    rest = [(1 - par) + 2 * j for j in range(8)]
    idx = []
    for gg in mine + rest:
        idx.extend(range(gg * 128, (gg + 1) * 128))
    return np.array(idx), mine


def kernel(L0_attn_x, L0_attn_w_project, L0_attn_w_out,
           L0_ffn_w1, L0_ffn_w2, L0_ffn_w3, _taps=(), _nc=None):
    x = np.asarray(L0_attn_x, np.float32)
    wqk, wvv, wo_t, w1t, w2t, w3r = _prep_weights(
        L0_attn_w_project, L0_attn_w_out, L0_ffn_w1, L0_ffn_w2, L0_ffn_w3)

    nc = _nc if _nc is not None else build_nc(_taps)
    in_maps = []
    metas = []
    for c in range(8):
        b, par = c // 2, c % 2
        idx, mine = _perm_rows(par)
        md, ma, mb = _masks(par)
        in_maps.append({
            "xp": np.ascontiguousarray(x[b][idx]),
            "wqk": wqk, "wv": wvv, "wo": wo_t,
            "w1t": w1t, "w2t": w2t, "w3r": w3r,
            "mask_d": md, "mask_a": ma, "mask_b": mb,
            "identd": np.ascontiguousarray(np.eye(128, dtype=ml_dtypes.bfloat16)),
            "bigmd": _bigmask(par),
        })
        metas.append((b, mine))

    res = run_bass_kernel_spmd(nc, in_maps, core_ids=list(range(8)))

    final = np.empty((4, T, D), np.float32)
    for c in range(8):
        b, mine = metas[c]
        o = res.results[c]["out"]
        for j, gg in enumerate(mine):
            final[b, gg * 128:(gg + 1) * 128, :] = o[j * 128:(j + 1) * 128, :]
    if _taps:
        return final, list(res.results)
    return final


# version 27
# speedup vs baseline: 1.6253x; 1.6253x over previous
import math
from contextlib import ExitStack

import numpy as np
import ml_dtypes
import sys

sys.path.insert(0, "/opt/trn_rl_repo")
import concourse.bass as bass
import concourse.mybir as mybir
from concourse.bass_utils import run_bass_kernel_spmd

F32 = mybir.dt.float32
BF16 = mybir.dt.bfloat16
AF = mybir.ActivationFunctionType
ALU = mybir.AluOpType
AX = mybir.AxisListType

D = 2048
T = 2048
TM = 1024
H = 16
HID = 5632
NHC = HID // 128
NDC = D // 128
EPS_LN = 1e-5
EPS_Q = 1e-8
SCL = 1.0 / math.sqrt(128.0)
QMAGIC = float(3 * 2 ** 22)


class Sem:
    def __init__(self, ctx, nc, name):
        self.h = ctx.enter_context(nc.semaphore(name))
        self.n = 0

    def mark(self, inst, k=1):
        inst.then_inc(self.h, k)
        self.n += k
        return (self, self.n)

    def mile(self):
        return (self, self.n)


def wait(eng, mile):
    if mile is not None:
        eng.wait_ge(mile[0].h, mile[1])


def fence(eng, sem, inst):
    mm = sem.mark(inst)
    wait(eng, mm)
    return mm


def build_nc(taps=()):
    taps = set(taps)
    nc = bass.Bass("TRN2", target_bir_lowering=False, debug=False)

    xp = nc.dram_tensor("xp", [T, D], F32, kind="ExternalInput").ap()
    wqk = nc.dram_tensor("wqk", [H, 128, NDC * 256], F32, kind="ExternalInput").ap()
    wv = nc.dram_tensor("wv", [4, 128, NDC * 512], F32, kind="ExternalInput").ap()
    wo = nc.dram_tensor("wo", [NDC, 128, D], F32, kind="ExternalInput").ap()
    w1t = nc.dram_tensor("w1t", [NHC, 128, D], F32, kind="ExternalInput").ap()
    w2t = nc.dram_tensor("w2t", [NHC, 128, D], F32, kind="ExternalInput").ap()
    w3r = nc.dram_tensor("w3r", [HID, D], F32, kind="ExternalInput").ap()
    mask_d = nc.dram_tensor("mask_d", [128, 128], BF16, kind="ExternalInput").ap()
    mask_a = nc.dram_tensor("mask_a", [128, 256], BF16, kind="ExternalInput").ap()
    mask_b = nc.dram_tensor("mask_b", [128, 256], BF16, kind="ExternalInput").ap()
    identd = nc.dram_tensor("identd", [128, 128], BF16, kind="ExternalInput").ap()
    bigmd = nc.dram_tensor("bigmd", [128, 896], BF16, kind="ExternalInput").ap()

    out = nc.dram_tensor("out", [TM, D], F32, kind="ExternalOutput").ap()

    xn_dram = nc.dram_tensor("xn_dram", [T, D], BF16).ap()
    s_dram = nc.dram_tensor("s_dram", [TM, D], F32).ap()
    w3q_h = nc.dram_tensor("w3q_dram", [HID, D], BF16)
    w3q_dram = w3q_h.ap()

    tapt = {}
    for name, shape, dt in [
        ("xn", [T, D], BF16), ("k0", [128, T], BF16), ("q0", [128, TM], BF16),
        ("v0", [128, 16 * 512], BF16), ("cat", [128, H * TM], BF16),
        ("sm", [TM, D], F32), ("gam", [128, 16], F32),
        ("hid", [128, NHC * TM], BF16), ("w3q", [HID, D], BF16),
        ("bias0", [128, 1], F32), ("pt0", [128, 2048], BF16),
    ]:
        if name in taps:
            tapt[name] = nc.dram_tensor("tap_" + name, shape, dt,
                                        kind="ExternalOutput").ap()

    M = {}

    with ExitStack() as ctx:
        def sb(name, shape, dt):
            return ctx.enter_context(nc.sbuf_tensor(name, shape, dt))

        def manual(name, shape, dt, side="right"):
            cm = nc.sbuf_tensor(name, shape, dt, side=side)
            return cm, cm.__enter__()

        ones_c = sb("ones_c", [128, 1], BF16)
        ones_r = sb("ones_r", [1, 128], BF16)
        ones_cf = sb("ones_cf", [128, 1], F32)
        ones_rf = sb("ones_rf", [1, 128], F32)
        identb = sb("identb", [128, 128], BF16)
        bigm = sb("bigm", [128, 896], BF16)
        gcols = sb("gcols", [128, 16], F32)
        scol = sb("scol", [128, 4], F32)
        bias4 = sb("bias4", [128, 4], F32)
        tiny = sb("tiny", [1, 16], F32)
        tinyb = sb("tinyb", [1, 4], BF16)
        epsc = sb("epsc", [128, 1], F32)

        pA = ctx.enter_context(nc.psum_tensor("pA", [128, 2048], F32))
        pB = ctx.enter_context(nc.psum_tensor("pB", [128, 2048], F32))

        s_ld = Sem(ctx, nc, "s_ld")
        s_gl = Sem(ctx, nc, "s_gl")
        s_st = Sem(ctx, nc, "s_st")
        s_gs = Sem(ctx, nc, "s_gs")
        s_pe = Sem(ctx, nc, "s_pe")
        s_dv = Sem(ctx, nc, "s_dv")
        s_ac = Sem(ctx, nc, "s_ac")
        s_pl = Sem(ctx, nc, "s_pl")
        s_gq = Sem(ctx, nc, "s_gq")
        s_cast = Sem(ctx, nc, "s_cast")

        xnT_m_cm, xnT_m = manual("xnT_m", [128, NDC * TM], BF16, side="left")
        grs_cm, grs = manual("grs", [128, 1024], F32, side="left")
        grs2_cm, grs2 = manual("grs2", [128, 1024], F32, side="left")
        catT_cm, catT = manual("catT", [128, H * TM], BF16, side="left")
        xnT_r_cm, xnT_r = manual("xnT_r", [128, NDC * TM], BF16, side="left")

        def xnT_chunk(c, tt):
            if tt < 8:
                return xnT_m[:, c * TM + tt * 128: c * TM + (tt + 1) * 128]
            return xnT_r[:, c * TM + (tt - 8) * 128: c * TM + (tt - 7) * 128]

        def xnT_half(c, tc):
            if tc < 2:
                return xnT_m[:, c * TM + tc * 512: c * TM + (tc + 1) * 512]
            return xnT_r[:, c * TM + (tc - 2) * 512: c * TM + (tc - 1) * 512]

        G_ORDER = []
        for j in range(NHC):
            for hh in range(2):
                G_ORDER.append(("w1", j, hh))
                G_ORDER.append(("w3", j, hh))
        for j in range(NHC):
            for hh in range(2):
                G_ORDER.append(("w2", j, hh))
        G_ACT = list(range(len(G_ORDER)))
        gst = {"dma": 0, "act": 0}

        def gamma_slab_in(i):
            m, j, hh = G_ORDER[i]
            if m == "w1":
                return w1t[j][:, hh * 1024:(hh + 1) * 1024]
            if m == "w2":
                return w2t[j][:, hh * 1024:(hh + 1) * 1024]
            return w3r[j * 128:(j + 1) * 128, hh * 1024:(hh + 1) * 1024]

        def gamma_dma(g, count):
            for _ in range(count):
                i = gst["dma"]
                if i >= len(G_ORDER):
                    return
                if i >= 2 and ("gfree", i - 2) not in M:
                    return
                buf = grs if i % 2 == 0 else grs2
                wait(g, M.get(("gfree", i - 2)))
                M[("gdma", i)] = s_gq.mark(
                    g.dma_start(out=buf[:, :], in_=gamma_slab_in(i)), 16)
                gst["dma"] += 1

        def gamma_reduce_dve(v, count):
            for _ in range(count):
                k = gst["act"]
                if k >= len(G_ACT):
                    return
                i = G_ACT[k]
                if ("gdma", i) not in M:
                    return
                buf = grs if i % 2 == 0 else grs2
                col = {"w1": 0, "w2": 1, "w3": 2}[G_ORDER[i][0]]
                wait(v, M[("gdma", i)])
                fence(v, s_dv, v.tensor_reduce(
                    gcols[:, 9:10], buf[:, :], AX.X, ALU.add,
                    apply_absolute_value=True))
                M[("gfree", i)] = fence(v, s_dv, v.tensor_tensor(
                    gcols[:, col:col + 1], gcols[:, col:col + 1],
                    gcols[:, 9:10], ALU.add))
                gst["act"] += 1

        with nc.Block() as blk:
            @blk.sync
            def _(sync):
                s_ld.mark(sync.dma_start(out=identb[:, :], in_=identd), 16)
                M["consts"] = s_ld.mark(sync.dma_start(out=bigm[:, :], in_=bigmd), 16)

            @blk.vector
            def _(v):
                v.memset(ones_c[:, :], 1.0)
                v.memset(ones_r[:, :], 1.0)
                v.memset(ones_cf[:, :], 1.0)
                v.memset(ones_rf[:, :], 1.0)
                v.memset(gcols[:, :], 0.0)
                v.memset(tiny[:, :], 0.0)
                v.memset(epsc[:, :], EPS_LN)

        ln_cms = []
        for nm, shp, dt in [("xbuf", [128, 2 * D], F32), ("xnb", [128, 2 * D], BF16),
                            ("lnsq", [128, D], F32), ("stat", [128, 8], F32)]:
            ln_cms.append(manual(nm, shp, dt))
        xbuf, xnb, lnsq, stat = [h for _, h in ln_cms]

        with nc.Block() as blk:
            @blk.sync
            def _(sync):
                for tt in range(2):
                    M[("lx", tt)] = s_ld.mark(sync.dma_start(
                        out=xbuf[:, tt * D:(tt + 1) * D],
                        in_=xp[tt * 128:(tt + 1) * 128, :]), 16)

            for tt in range(16):
                sl = (tt % 2) * D

                @blk.vector
                def _(v, tt=tt, sl=sl):
                    wait(v, M[("lx", tt)])
                    wait(v, M.get(("lnapp", tt - 1)))
                    fence(v, s_dv, v.tensor_reduce(
                        stat[:, 0:1], xbuf[:, sl:sl + D], AX.X, ALU.add))
                    fence(v, s_dv, v.tensor_scalar(
                        stat[:, 2:3], stat[:, 0:1], 1.0 / D, None, ALU.mult))
                    fence(v, s_dv, v.tensor_tensor(
                        lnsq[:, :], xbuf[:, sl:sl + D], xbuf[:, sl:sl + D],
                        ALU.mult))
                    fence(v, s_dv, v.tensor_reduce(
                        stat[:, 1:2], lnsq[:, :], AX.X, ALU.add))
                    fence(v, s_dv, v.tensor_tensor(
                        stat[:, 5:6], stat[:, 2:3], stat[:, 2:3], ALU.mult))
                    fence(v, s_dv, v.tensor_scalar(
                        stat[:, 3:4], stat[:, 1:2], 1.0 / D, None, ALU.mult))
                    M[("lnvar", tt)] = s_dv.mark(v.tensor_tensor(
                        stat[:, 3:4], stat[:, 3:4], stat[:, 5:6], ALU.subtract))

                @blk.scalar
                def _(a, tt=tt):
                    wait(a, M[("lnvar", tt)])
                    fence(a, s_ac, a.activation(stat[:, 6:7], stat[:, 3:4],
                                                AF.Ln, bias=epsc[:, 0:1]))
                    M[("lnrstd", tt)] = s_ac.mark(a.activation(
                        stat[:, 4:5], stat[:, 6:7], AF.Exp, scale=-0.5))

                @blk.vector
                def _(v, tt=tt, sl=sl):
                    wait(v, M[("lnrstd", tt)])
                    fence(v, s_dv, v.tensor_tensor(
                        stat[:, 5:6], stat[:, 2:3], stat[:, 4:5], ALU.mult))
                    fence(v, s_dv, v.tensor_scalar(
                        stat[:, 5:6], stat[:, 5:6], -1.0, None, ALU.mult))
                    wait(v, M.get(("xnst", tt - 2)))
                    M[("lnapp", tt)] = s_dv.mark(v.tensor_scalar(
                        xnb[:, sl:sl + D], xbuf[:, sl:sl + D],
                        stat[:, 4:5], stat[:, 5:6], ALU.mult, ALU.add))

                @blk.sync
                def _(sync, tt=tt, sl=sl):
                    wait(sync, M[("lnapp", tt)])
                    M[("xnst", tt)] = s_st.mark(sync.dma_start(
                        out=xn_dram[tt * 128:(tt + 1) * 128, :],
                        in_=xnb[:, sl:sl + D]), 16)
                    if tt + 2 < 16:
                        M[("lx", tt + 2)] = s_ld.mark(sync.dma_start(
                            out=xbuf[:, sl:sl + D],
                            in_=xp[(tt + 2) * 128:(tt + 3) * 128, :]), 16)

        with nc.Block() as blk:
            @blk.sync
            def _(sync):
                sync.wait_ge(s_st.h, s_st.n)
                for c in range(NDC):
                    s_ld.mark(sync.dma_start(
                        out=xnT_m[:, c * TM:(c + 1) * TM],
                        in_=xn_dram[0:TM, c * 128:(c + 1) * 128],
                        transpose=True), 16)
                    s_ld.mark(sync.dma_start(
                        out=xnT_r[:, c * TM:(c + 1) * TM],
                        in_=xn_dram[TM:T, c * 128:(c + 1) * 128],
                        transpose=True), 16)
                M["xt_all"] = s_ld.mile()

            for _ in range(16):
                @blk.sync
                def _(sync):
                    gamma_dma(sync, 2)

                @blk.vector
                def _(v):
                    gamma_reduce_dve(v, 2)

        if "xn" in tapt:
            with nc.Block() as blk:
                @blk.sync
                def _(sync):
                    mm = s_st.mark(sync.dma_start(out=tapt["xn"], in_=xn_dram), 16)
                    wait(sync, mm)

        for cm, _h in reversed(ln_cms):
            cm.__exit__(None, None, None)

        att_cms = []
        for nm, shp, dt in [("wqkb", [128, 3 * NDC * 256], BF16),
                            ("wvb", [128, NDC * 512], BF16),
                            ("Kh", [128, 4 * T], BF16),
                            ("Qh", [128, 4 * TM], BF16),
                            ("Vg", [128, 16 * 512], BF16),
                            ("PT", [128, 2 * 2048], BF16),
                            ("rrow", [1, 256], F32),
                            ("rrb", [1, 256], BF16),
                            ("rbc", [128, 256], BF16)]:
            att_cms.append(manual(nm, shp, dt))
        (wqkb, wvb, Kh, Qh, Vg, PT, rrow, rrb, rbc) = \
            [h for _, h in att_cms]
        sqb = PT[:, 0:2048]

        def pair_kbs(p):
            return list(range(0, 2 * p + 2)) + list(range(8, 8 + 2 * p + 2))

        def wqk_slot(h):
            return (h % 3) * 4096

        def emit_wqk_load(blk, h):
            @blk.gpsimd
            def _(gq, h=h):
                wait(gq, M.get(("qmm", h - 3)))
                M[("wqk", h)] = s_cast.mark(gq.dma_start(
                    out=wqkb[:, wqk_slot(h):wqk_slot(h) + 4096],
                    in_=wqk[h]), 16)

        def emit_wv_load(blk, g, guard):
            @blk.gpsimd
            def _(gq, g=g, guard=guard):
                wait(gq, guard)
                gq.dma_start(out=wvb[:, 0:4096], in_=wv[g][:, 0:4096])\
                    .then_inc(s_cast.h, 16)
                s_cast.n += 16
                M[("wvb", g)] = s_cast.mark(gq.dma_start(
                    out=wvb[:, 4096:8192], in_=wv[g][:, 4096:8192]), 16)

        for g in range(4):
            with nc.Block() as blk:
                if g == 0:
                    emit_wv_load(blk, 0, None)
                    emit_wqk_load(blk, 0)
                    emit_wqk_load(blk, 1)
                    emit_wqk_load(blk, 2)

                @blk.sync
                def _(sync, g=g):
                    gamma_dma(sync, 4)



                for tt in range(16):
                    bank = 1024 + (tt % 2) * 512

                    @blk.tensor
                    def _(t, g=g, tt=tt, bank=bank):
                        if tt == 0:
                            wait(t, M[("wvb", g)])
                            wait(t, M["xt_all"])
                        wait(t, M.get(("vev", g, tt - 2)))
                        last = None
                        for c in range(NDC):
                            last = t.matmul(pB[:, bank:bank + 512],
                                            xnT_chunk(c, tt),
                                            wvb[:, c * 512:(c + 1) * 512],
                                            start=(c == 0), stop=(c == NDC - 1))
                        M[("vmm", g, tt)] = s_pe.mark(last)

                    @blk.vector
                    def _(v, g=g, tt=tt, bank=bank):
                        wait(v, M[("vmm", g, tt)])
                        M[("vev", g, tt)] = s_dv.mark(v.tensor_copy(
                            Vg[:, tt * 512:(tt + 1) * 512], pB[:, bank:bank + 512]))

                for i in range(4):
                    h = g * 4 + i
                    wq = wqkb[:, wqk_slot(h):wqk_slot(h) + 4096]
                    khs = Kh[:, i * T:(i + 1) * T]
                    qhs = Qh[:, i * TM:(i + 1) * TM]

                    @blk.tensor
                    def _(t, h=h, i=i, g=g, wq=wq):
                        wait(t, M[("wqk", h)])
                        if i == 0:
                            wait(t, M.get(("exp_last", h - 1)))
                        else:
                            wait(t, M[("qnred", h - 1)])
                        last = None
                        for c in range(NDC):
                            lh = wq[:, c * 256 + 128: c * 256 + 256]
                            for tc in range(4):
                                last = t.matmul(pA[:, tc * 512:(tc + 1) * 512],
                                                lh, xnT_half(c, tc),
                                                start=(c == 0), stop=(c == NDC - 1))
                        M[("kmm", h)] = s_pe.mark(last)

                    @blk.vector
                    def _(v, h=h, khs=khs):
                        wait(v, M[("kmm", h)])
                        M[("kev", h)] = fence(v, s_dv, v.tensor_copy(khs, pA[:, :]))
                        M[("ksq", h)] = s_dv.mark(v.tensor_tensor(
                            sqb[:, :], khs, khs, ALU.mult))

                    @blk.tensor
                    def _(t, h=h, i=i, wq=wq):
                        if i == 0:
                            wait(t, M.get(("cat", h - 1, 3)))
                        else:
                            wait(t, M[("qev", h - 1)])
                        last = None
                        for c in range(NDC):
                            lh = wq[:, c * 256: c * 256 + 128]
                            for tc in range(2):
                                last = t.matmul(
                                    pB[:, tc * 512:(tc + 1) * 512], lh,
                                    xnT_m[:, c * TM + tc * 512: c * TM + (tc + 1) * 512],
                                    start=(c == 0), stop=(c == NDC - 1))
                        M[("qmm", h)] = s_pe.mark(last)

                    @blk.vector
                    def _(v, h=h, qhs=qhs):
                        wait(v, M[("qmm", h)])
                        M[("qev", h)] = fence(v, s_dv, v.tensor_copy(
                            qhs, pB[:, 0:1024]))
                        M[("qsq", h)] = s_dv.mark(v.tensor_tensor(
                            sqb[:, 0:1024], qhs, qhs, ALU.mult))

                    if g * 4 + i + 3 < H:
                        emit_wqk_load(blk, g * 4 + i + 3)

                    @blk.tensor
                    def _(t, h=h):
                        wait(t, M[("ksq", h)])
                        last = None
                        for j in range(4):
                            last = t.matmul(pA[0:1, j * 512:(j + 1) * 512],
                                            ones_c[:, :],
                                            sqb[:, j * 512:(j + 1) * 512],
                                            start=True, stop=True)
                        M[("nK", h)] = s_pe.mark(last)

                    @blk.vector
                    def _(v, h=h, i=i):
                        wait(v, M[("nK", h)])
                        M[("nKred", h)] = fence(v, s_dv, v.tensor_reduce(
                            tiny[0:1, 4 + i:5 + i], pA[0:1, 0:2048], AX.X,
                            ALU.max))

                    @blk.tensor
                    def _(t, h=h):
                        wait(t, M[("qsq", h)])
                        wait(t, M[("nKred", h)])
                        last = None
                        for j in range(2):
                            last = t.matmul(pA[0:1, j * 512:(j + 1) * 512],
                                            ones_c[:, :],
                                            sqb[:, j * 512:(j + 1) * 512],
                                            start=True, stop=True)
                        M[("nQ", h)] = s_pe.mark(last)

                    @blk.vector
                    def _(v, h=h, i=i):
                        wait(v, M[("nQ", h)])
                        M[("qnred", h)] = fence(v, s_dv, v.tensor_reduce(
                            tiny[0:1, 8 + i:9 + i], pA[0:1, 0:1024], AX.X,
                            ALU.max))

                @blk.vector
                def _(v, g=g):
                    wait(v, M[("qnred", g * 4 + 3)])
                    wait(v, M.get(("exp_last", g * 4 - 1)))
                    M[("m2g", g)] = fence(v, s_dv, v.tensor_tensor(
                        tiny[0:1, 4:8], tiny[0:1, 4:8], tiny[0:1, 8:12],
                        ALU.mult))

                @blk.scalar
                def _(a, g=g):
                    wait(a, M[("m2g", g)])
                    fence(a, s_ac, a.activation(tiny[0:1, 8:12], tiny[0:1, 4:8],
                                                AF.Ln))
                    M[("mhg", g)] = s_ac.mark(a.activation(
                        tiny[0:1, 4:8], tiny[0:1, 8:12], AF.Exp, scale=0.5))

                @blk.vector
                def _(v, g=g):
                    wait(v, M[("mhg", g)])
                    fence(v, s_dv, v.tensor_scalar(
                        tiny[0:1, 4:8], tiny[0:1, 4:8], -SCL, None, ALU.mult))
                    M[("negmg", g)] = s_dv.mark(v.tensor_copy(
                        tinyb[0:1, 0:4], tiny[0:1, 4:8]))

                @blk.tensor
                def _(t, g=g):
                    wait(t, M[("negmg", g)])
                    M[("biasmm", g)] = s_pe.mark(t.matmul(
                        pB[:, 1536:1540], ones_r[:, :], tinyb[0:1, 0:4],
                        start=True, stop=True))

                @blk.vector
                def _(v, g=g):
                    wait(v, M[("biasmm", g)])
                    M[("bias_g", g)] = s_dv.mark(v.tensor_copy(
                        bias4[:, 0:4], pB[:, 1536:1540]))

                pend = []
                rsav_pend = []

                def flush_pend():
                    while pend:
                        pend.pop(0)()

                def flush_rsav():
                    while rsav_pend:
                        rsav_pend.pop(0)()

                bg = 0
                for i in range(4):
                    h = g * 4 + i
                    khs = Kh[:, i * T:(i + 1) * T]
                    qhs = Qh[:, i * TM:(i + 1) * TM]
                    for p in range(4):
                        pg = i * 4 + p
                        q = pg % 2
                        av0 = q * 512
                        bc0 = q * 512 + 256
                        rs0 = 1024 + q * 512
                        kbs = pair_kbs(p)
                        nkb = len(kbs)
                        batches = [kbs[j:j + 8] for j in range(0, nkb, 8)]
                        kidx = 0
                        for bi, batch in enumerate(batches):
                            nb = len(batch)
                            half = (bg % 2) * 2048
                            mybg = bg

                            @blk.tensor
                            def _(t, h=h, g=g, p=p, bi=bi, batch=batch,
                                  khs=khs, qhs=qhs, half=half, mybg=mybg):
                                if mybg < 1:
                                    wait(t, M[("qnred", g * 4 + 3)])
                                else:
                                    wait(t, M[("expg", g, mybg - 1)])
                                last = None
                                for idx, kb in enumerate(batch):
                                    dst = pA[:, idx * 256:(idx + 1) * 256]
                                    if kb == 2 * p:
                                        t.matmul(dst, khs[:, kb * 128:(kb + 1) * 128],
                                                 qhs[:, p * 256:(p + 1) * 256],
                                                 start=True, stop=False)
                                        last = t.matmul(
                                            pA[:, idx * 256: idx * 256 + 128],
                                            identb[:, :], bigm[:, 0:128],
                                            start=False, stop=True)
                                    elif kb == 2 * p + 1:
                                        t.matmul(dst, khs[:, kb * 128:(kb + 1) * 128],
                                                 qhs[:, p * 256:(p + 1) * 256],
                                                 start=True, stop=False)
                                        last = t.matmul(
                                            dst, identb[:, :], bigm[:, 128:384],
                                            start=False, stop=True)
                                    elif kb == 8 + 2 * p:
                                        t.matmul(dst, khs[:, kb * 128:(kb + 1) * 128],
                                                 qhs[:, p * 256:(p + 1) * 256],
                                                 start=True, stop=False)
                                        last = t.matmul(
                                            dst, identb[:, :], bigm[:, 384:640],
                                            start=False, stop=True)
                                    elif kb == 9 + 2 * p:
                                        t.matmul(dst, khs[:, kb * 128:(kb + 1) * 128],
                                                 qhs[:, p * 256:(p + 1) * 256],
                                                 start=True, stop=False)
                                        last = t.matmul(
                                            dst, identb[:, :], bigm[:, 640:896],
                                            start=False, stop=True)
                                    else:
                                        last = t.matmul(
                                            dst, khs[:, kb * 128:(kb + 1) * 128],
                                            qhs[:, p * 256:(p + 1) * 256],
                                            start=True, stop=True)
                                M[("scg", g, mybg)] = s_pe.mark(last)

                            @blk.scalar
                            def _(a, g=g, i=i, nb=nb, half=half, mybg=mybg):
                                wait(a, M[("scg", g, mybg)])
                                if mybg == 0:
                                    wait(a, M[("bias_g", g)])
                                wait(a, M.get(("rsavg", g, mybg - 2)))
                                M[("expg", g, mybg)] = s_ac.mark(a.activation(
                                    PT[:, half:half + nb * 256],
                                    pA[:, 0:nb * 256],
                                    AF.Exp, scale=SCL, bias=bias4[:, i:i + 1]))

                            if "pt0" in tapt and h == 0 and p == 1 and bi == 0:
                                @blk.sync
                                def _(sync):
                                    wait(sync, M[("expg", 0, 2)])
                                    mm = s_st.mark(sync.dma_start(
                                        out=tapt["pt0"], in_=PT[:, 0:2048]), 16)
                                    wait(sync, mm)

                            def rsav_cl(h=h, g=g, p=p, bi=bi, batch=batch,
                                        kidx=kidx, nkb=nkb, half=half,
                                        mybg=mybg, pg=pg, av0=av0, rs0=rs0):
                                @blk.tensor
                                def _(t):
                                    wait(t, M[("expg", g, mybg)])
                                    if bi == 0:
                                        wait(t, M.get(("catg", g, pg - 2)))
                                    last = None
                                    for idx, kb in enumerate(batch):
                                        kg = kidx + idx
                                        t.matmul(pB[0:1, rs0:rs0 + 256],
                                                 ones_c[:, :],
                                                 PT[:, half + idx * 256:
                                                    half + (idx + 1) * 256],
                                                 start=(kg == 0),
                                                 stop=(kg == nkb - 1))
                                        u0 = kb * 512 + (h - 4 * g) * 128
                                        last = t.matmul(
                                            pB[:, av0:av0 + 256],
                                            Vg[:, u0:u0 + 128],
                                            PT[:, half + idx * 256:
                                               half + (idx + 1) * 256],
                                            start=(kg == 0), stop=(kg == nkb - 1))
                                    M[("rsavg", g, mybg)] = s_pe.mark(last)

                            rsav_pend.append(rsav_cl)
                            if len(rsav_pend) > 1:
                                rsav_pend.pop(0)()

                            kidx += len(batch)
                            bg += 1
                            if bi == 0:
                                flush_pend()

                        last_bg = bg - 1

                        def tail(h=h, g=g, p=p, pg=pg, av0=av0, bc0=bc0,
                                 rs0=rs0, last_bg=last_bg):
                            @blk.vector
                            def _(v):
                                wait(v, M[("rsavg", g, last_bg)])
                                fence(v, s_dv, v.tensor_copy(rrow[0:1, :],
                                                             pB[0:1, rs0:rs0 + 256]))
                                fence(v, s_dv, v.reciprocal(rrow[0:1, :],
                                                            rrow[0:1, :]))
                                M[("rrbg", g, pg)] = s_dv.mark(v.tensor_copy(
                                    rrb[0:1, :], rrow[0:1, :]))

                            @blk.tensor
                            def _(t):
                                wait(t, M[("rrbg", g, pg)])
                                M[("bcastg", g, pg)] = s_pe.mark(t.matmul(
                                    pB[:, bc0:bc0 + 256], ones_r[:, :],
                                    rrb[0:1, :], start=True, stop=True))

                            @blk.vector
                            def _(v):
                                wait(v, M[("bcastg", g, pg)])
                                fence(v, s_dv, v.tensor_copy(rbc[:, :],
                                                             pB[:, bc0:bc0 + 256]))
                                M[("catg", g, pg)] = s_dv.mark(v.tensor_tensor(
                                    catT[:, h * TM + p * 256:
                                         h * TM + (p + 1) * 256],
                                    pB[:, av0:av0 + 256], rbc[:, :], ALU.mult))
                                if p == 3:
                                    M[("cat", h, 3)] = M[("catg", g, pg)]

                        flush_rsav()
                        pend.append(tail)

                        for _ in range(2):
                            @blk.sync
                            def _(sync):
                                gamma_dma(sync, 2)

                            @blk.vector
                            def _(v):
                                gamma_reduce_dve(v, 2)

                flush_pend()

                if g < 3:
                    emit_wv_load(blk, g + 1, M[("vmm", g, 15)])

        for name, key in [("k0", ("kev", 0)), ("q0", ("qev", 0))]:
            if name in tapt:
                src = Kh[:, 0:T] if name == "k0" else Qh[:, 0:TM]
                with nc.Block() as blk:
                    @blk.sync
                    def _(sync, name=name, key=key, src=src):
                        wait(sync, M[key])
                        mm = s_st.mark(sync.dma_start(out=tapt[name], in_=src), 16)
                        wait(sync, mm)
        if "v0" in tapt:
            with nc.Block() as blk:
                @blk.sync
                def _(sync):
                    mm = s_st.mark(sync.dma_start(out=tapt["v0"], in_=Vg[:, :]), 16)
                    wait(sync, mm)
        if "cat" in tapt:
            with nc.Block() as blk:
                @blk.sync
                def _(sync):
                    wait(sync, M[("cat", 15, 3)])
                    mm = s_st.mark(sync.dma_start(out=tapt["cat"], in_=catT[:, :]),
                                   16)
                    wait(sync, mm)

        for cm, _h in reversed(att_cms):
            cm.__exit__(None, None, None)
        xnT_r_cm.__exit__(None, None, None)

        op_cms = [manual("woTb", [128, NDC * D], BF16),
                  manual("sstage", [128, 2 * D], F32)]
        woTb, sstage = [h for _, h in op_cms]

        with nc.Block() as blk:
            @blk.gpsimd
            def _(gq):
                for c in range(NDC):
                    M[("wo", c)] = s_cast.mark(gq.dma_start(
                        out=woTb[:, c * D:(c + 1) * D], in_=wo[c]), 16)
            for _ in range(12):
                @blk.sync
                def _(sync):
                    gamma_dma(sync, 2)

                @blk.vector
                def _(v):
                    gamma_reduce_dve(v, 2)

            for tt in range(8):
                pg = pA if tt % 2 == 0 else pB

                @blk.tensor
                def _(t, tt=tt, pg=pg):
                    if tt == 0:
                        wait(t, M[("wo", NDC - 1)])
                    wait(t, M.get(("sev", tt - 2)))
                    last = None
                    for c in range(NDC):
                        lh = catT[:, c * TM + tt * 128: c * TM + (tt + 1) * 128]
                        for oc in range(4):
                            last = t.matmul(
                                pg[:, oc * 512:(oc + 1) * 512], lh,
                                woTb[:, c * D + oc * 512: c * D + (oc + 1) * 512],
                                start=(c == 0), stop=(c == NDC - 1))
                    M[("smm", tt)] = s_pe.mark(last)

                @blk.vector
                def _(v, tt=tt, pg=pg):
                    wait(v, M[("smm", tt)])
                    wait(v, M.get(("sst", tt - 2)))
                    sl = (tt % 2) * D
                    M[("sev", tt)] = s_dv.mark(v.tensor_copy(
                        sstage[:, sl:sl + D], pg[:, :]))

                @blk.sync
                def _(sync, tt=tt):
                    wait(sync, M[("sev", tt)])
                    sl = (tt % 2) * D
                    M[("sst", tt)] = s_st.mark(sync.dma_start(
                        out=s_dram[tt * 128:(tt + 1) * 128, :],
                        in_=sstage[:, sl:sl + D]), 16)

            M["s_all"] = s_st.mile()

        if "sm" in tapt:
            with nc.Block() as blk:
                @blk.sync
                def _(sync):
                    sync.wait_ge(s_st.h, s_st.n)
                    mm = s_st.mark(sync.dma_start(out=tapt["sm"], in_=s_dram), 16)
                    wait(sync, mm)

        for cm, _h in reversed(op_cms):
            cm.__exit__(None, None, None)
        catT_cm.__exit__(None, None, None)

        with nc.Block() as blk:
            for _ in range(140):
                if gst["dma"] >= len(G_ORDER) and gst["act"] >= len(G_ACT):
                    break

                @blk.sync
                def _(sync):
                    gamma_dma(sync, 2)

                @blk.vector
                def _(v):
                    gamma_reduce_dve(v, 2)

            @blk.tensor
            def _(t):
                t.wait_ge(s_dv.h, s_dv.n)
                last = None
                for j in range(3):
                    last = t.matmul(pA[0:1, j * 512:j * 512 + 1], ones_cf[:, :],
                                    gcols[:, j:j + 1], start=True, stop=True)
                M[("gsum")] = s_pe.mark(last)

            @blk.vector
            def _(v):
                wait(v, M["gsum"])
                mm = None
                for j in range(3):
                    fence(v, s_dv, v.tensor_copy(tiny[0:1, j:j + 1],
                                                 pA[0:1, j * 512:j * 512 + 1]))
                    fence(v, s_dv, v.tensor_scalar(
                        tiny[0:1, j:j + 1], tiny[0:1, j:j + 1],
                        1.0 / (HID * D), EPS_Q, ALU.mult, ALU.add))
                    mm = fence(v, s_dv, v.reciprocal(tiny[0:1, j:j + 1],
                                                     tiny[0:1, j:j + 1]))
                M["ginv"] = mm

            @blk.tensor
            def _(t):
                wait(t, M["ginv"])
                last = None
                for j in range(3):
                    last = t.matmul(pB[:, j * 512:j * 512 + 1], ones_rf[:, :],
                                    tiny[0:1, j:j + 1], start=True, stop=True)
                M["gbc"] = s_pe.mark(last)

            @blk.vector
            def _(v):
                wait(v, M["gbc"])
                last = None
                for j in range(3):
                    last = v.tensor_copy(scol[:, j:j + 1], pB[:, j * 512:j * 512 + 1])
                M["scol"] = s_dv.mark(last)

        if "gam" in tapt:
            with nc.Block() as blk:
                @blk.sync
                def _(sync):
                    wait(sync, M["scol"])
                    mm = s_st.mark(sync.dma_start(out=tapt["gam"], in_=gcols[:, :]),
                                   16)
                    wait(sync, mm)

        grs2_cm.__exit__(None, None, None)
        grs_cm.__exit__(None, None, None)

        hid_cm, hidT = manual("hidT", [128, NHC * TM], BF16)
        ffn_cms = []
        for nm, shp, dt in [("w1f", [128, 2 * D], F32), ("w2f", [128, 2 * D], F32),
                            ("w1qb", [128, 2 * D], BF16), ("w2qb", [128, 2 * D], BF16),
                            ("w3f", [128, D], F32), ("w3qb", [128, D], BF16),
                            ("qi", [128, 2 * D], F32),
                            ("sigb", [128, TM], F32)]:
            ffn_cms.append(manual(nm, shp, dt))
        (w1f, w2f, w1qb, w2qb, w3f, w3qb, qi, sigb) = \
            [h for _, h in ffn_cms]

        def qi_slot(k):
            return qi[:, (k % 2) * D:(k % 2 + 1) * D]

        with nc.Block() as blk:
            ffn_pend = []

            def ffn_flush():
                while ffn_pend:
                    ffn_pend.pop(0)()

            for hc in range(NHC):
                sl = (hc % 2)

                @blk.sync
                def _(sync, hc=hc, sl=sl):
                    wait(sync, M.get(("q1a", hc - 1)))
                    M[("w1l", hc)] = s_ld.mark(sync.dma_start(
                        out=w1f[:, sl * D:(sl + 1) * D], in_=w1t[hc]), 16)
                    wait(sync, M.get(("q2a", hc - 1)))
                    M[("w2l", hc)] = s_ld.mark(sync.dma_start(
                        out=w2f[:, sl * D:(sl + 1) * D], in_=w2t[hc]), 16)

                @blk.gpsimd
                def _(gq, hc=hc):
                    wait(gq, M.get(("q3a", hc - 1)))
                    M[("w3l", hc)] = s_gl.mark(gq.dma_start(
                        out=w3f[:, :], in_=w3r[hc * 128:(hc + 1) * 128, :]), 16)

                @blk.scalar
                def _(a, hc=hc):
                    wait(a, M[("w3l", hc)])
                    if hc == 0:
                        wait(a, M["scol"])
                    wait(a, M.get(("dvq", 3 * hc + 1 - 2)))
                    M[("q3a", hc)] = s_ac.mark(a.mul(
                        qi_slot(3 * hc + 1), w3f[:, :], scol[:, 2:3]))

                @blk.vector
                def _(v, hc=hc):
                    wait(v, M[("q3a", hc)])
                    wait(v, M.get(("w3st", hc - 1)))
                    fence(v, s_dv, v.tensor_scalar(
                        qi_slot(3 * hc + 1), qi_slot(3 * hc + 1),
                        QMAGIC, -QMAGIC, ALU.add, ALU.add))
                    M[("w3q", hc)] = M[("dvq", 3 * hc + 1)] = fence(
                        v, s_dv, v.tensor_scalar(
                            w3qb[:, :], qi_slot(3 * hc + 1),
                            1.0, -1.0, ALU.min, ALU.max))

                @blk.gpsimd
                def _(gq, hc=hc):
                    wait(gq, M[("w3q", hc)])
                    M[("w3st", hc)] = s_gs.mark(gq.dma_start(
                        out=w3q_dram[hc * 128:(hc + 1) * 128, :],
                        in_=w3qb[:, :]), 16)

                @blk.scalar
                def _(a, hc=hc, sl=sl):
                    wait(a, M[("w1l", hc)])
                    if hc == 0:
                        wait(a, M["scol"])
                    wait(a, M.get(("dvq", 3 * hc + 0 - 2)))
                    M[("q1a", hc)] = s_ac.mark(a.mul(
                        qi_slot(3 * hc + 0),
                        w1f[:, sl * D:(sl + 1) * D], scol[:, 0:1]))

                @blk.vector
                def _(v, hc=hc, sl=sl):
                    wait(v, M[("q1a", hc)])
                    wait(v, M.get(("x12", hc - 2)))
                    fence(v, s_dv, v.tensor_scalar(
                        qi_slot(3 * hc + 0), qi_slot(3 * hc + 0),
                        QMAGIC, -QMAGIC, ALU.add, ALU.add))
                    M[("w1q", hc)] = M[("dvq", 3 * hc + 0)] = fence(
                        v, s_dv, v.tensor_scalar(
                            w1qb[:, sl * D:(sl + 1) * D], qi_slot(3 * hc + 0),
                            1.0, -1.0, ALU.min, ALU.max))

                @blk.scalar
                def _(a, hc=hc, sl=sl):
                    wait(a, M[("w2l", hc)])
                    M[("q2a", hc)] = s_ac.mark(a.activation(
                        w2f[:, (1 - sl) * D:(2 - sl) * D].bitcast(mybir.dt.int32),
                        w2f[:, sl * D:(sl + 1) * D], AF.Identity,
                        scale=scol[:, 1:2]))

                @blk.vector
                def _(v, hc=hc, sl=sl):
                    wait(v, M[("q2a", hc)])
                    wait(v, M.get(("x12", hc - 2)))
                    M[("w12q", hc)] = fence(v, s_dv, v.tensor_scalar(
                        w2qb[:, sl * D:(sl + 1) * D],
                        w2f[:, (1 - sl) * D:(2 - sl) * D].bitcast(mybir.dt.int32),
                        1.0, -1.0, ALU.min, ALU.max))

                pg = pA if hc % 2 == 0 else pB

                @blk.tensor
                def _(t, hc=hc, sl=sl, pg=pg):
                    wait(t, M[("w1q", hc)])
                    wait(t, M[("w12q", hc)])
                    wait(t, M.get(("hid", hc - 2)))
                    last = None
                    for th in range(2):
                        for c in range(NDC):
                            t.matmul(pg[:, th * 512:(th + 1) * 512],
                                     w1qb[:, sl * D + c * 128: sl * D + (c + 1) * 128],
                                     xnT_m[:, c * TM + th * 512: c * TM + (th + 1) * 512],
                                     start=(c == 0), stop=(c == NDC - 1))
                    for th in range(2):
                        for c in range(NDC):
                            last = t.matmul(
                                pg[:, 1024 + th * 512: 1024 + (th + 1) * 512],
                                w2qb[:, sl * D + c * 128: sl * D + (c + 1) * 128],
                                xnT_m[:, c * TM + th * 512: c * TM + (th + 1) * 512],
                                start=(c == 0), stop=(c == NDC - 1))
                    M[("x12", hc)] = s_pe.mark(last)

                def sighid(hc=hc, pg=pg):
                    @blk.scalar
                    def _(a, hc=hc, pg=pg):
                        wait(a, M[("x12", hc)])
                        wait(a, M.get(("hid", hc - 1)))
                        M[("sig", hc)] = s_ac.mark(a.activation(
                            sigb[:, :], pg[:, 0:1024], AF.Sigmoid))

                    @blk.vector
                    def _(v, hc=hc, pg=pg):
                        wait(v, M[("sig", hc)])
                        M[("hid", hc)] = s_dv.mark(v.tensor_tensor(
                            hidT[:, hc * TM:(hc + 1) * TM], pg[:, 1024:2048],
                            sigb[:, :], ALU.mult))

                ffn_pend.append(sighid)
                if hc >= 1:
                    ffn_flush() if False else None
                if len(ffn_pend) > 1:
                    ffn_pend.pop(0)()
            ffn_flush()

        if "hid" in tapt:
            with nc.Block() as blk:
                @blk.sync
                def _(sync):
                    wait(sync, M[("hid", NHC - 1)])
                    mm = s_st.mark(sync.dma_start(out=tapt["hid"], in_=hidT[:, :]),
                                   16)
                    wait(sync, mm)
        if "w3q" in tapt:
            with nc.Block() as blk:
                @blk.sync
                def _(sync):
                    sync.wait_ge(s_gs.h, s_gs.n)
                    mm = s_st.mark(sync.dma_start(out=tapt["w3q"], in_=w3q_dram),
                                   16)
                    wait(sync, mm)

        for cm, _h in reversed(ffn_cms):
            cm.__exit__(None, None, None)

        f2_cms = [manual("w3qd", [128, NHC * 512], BF16),
                  manual("gt", [128, 2 * 512], F32),
                  manual("stile", [128, 8 * 512], F32),
                  manual("xtile", [128, 8 * 512], F32),
                  manual("otile", [128, 2 * 512], F32)]
        w3qd, gt, stile, xtile, otile = [h for _, h in f2_cms]

        with nc.Block() as blk:
            for dg in range(4):
                @blk.sync
                def _(sync, dg=dg):
                    if dg == 0:
                        sync.wait_ge(s_gs.h, s_gs.n)
                        sync.wait_ge(s_st.h, M["s_all"][1])
                    wait(sync, M.get(("ffdone", dg - 1)))
                    M[("w3qd", dg)] = s_ld.mark(sync.dma_start(
                        out=w3qd[:, :],
                        in_=bass.AP(w3q_h, dg * 512,
                                    [[D, 128], [128 * D, NHC], [1, 512]])), 16)
                    for tt in range(8):
                        wait(sync, M.get(("comb", dg - 1, tt)))
                        M[("sxl", dg, tt)] = s_ld.mark(sync.dma_start(
                            out=stile[:, tt * 512:(tt + 1) * 512],
                            in_=s_dram[tt * 128:(tt + 1) * 128,
                                       dg * 512:(dg + 1) * 512]), 16)
                        M[("sxl2", dg, tt)] = s_ld.mark(sync.dma_start(
                            out=xtile[:, tt * 512:(tt + 1) * 512],
                            in_=xp[tt * 128:(tt + 1) * 128,
                                   dg * 512:(dg + 1) * 512]), 16)

                for tt in range(8):
                    @blk.tensor
                    def _(t, dg=dg, tt=tt):
                        if tt == 0:
                            wait(t, M[("w3qd", dg)])
                        wait(t, M.get(("silu", dg - 1, tt)))
                        pg = pA if tt < 4 else pB
                        col = (tt % 4) * 512
                        last = None
                        for hc in range(NHC):
                            last = t.matmul(
                                pg[:, col:col + 512],
                                hidT[:, hc * TM + tt * 128: hc * TM + (tt + 1) * 128],
                                w3qd[:, hc * 512:(hc + 1) * 512],
                                start=(hc == 0), stop=(hc == NHC - 1))
                        M[("fft", dg, tt)] = s_pe.mark(last)
                        if tt == 7:
                            M[("ffdone", dg)] = M[("fft", dg, tt)]

                for tt in range(8):
                    pg = pA if tt < 4 else pB
                    col = (tt % 4) * 512

                    @blk.scalar
                    def _(a, dg=dg, tt=tt, pg=pg, col=col):
                        wait(a, M[("fft", dg, tt)])
                        wait(a, M.get(("comb", dg, tt - 2)))
                        M[("silu", dg, tt)] = s_ac.mark(a.activation(
                            gt[:, (tt % 2) * 512:(tt % 2 + 1) * 512],
                            pg[:, col:col + 512], AF.Silu))

                    @blk.vector
                    def _(v, dg=dg, tt=tt):
                        wait(v, M[("silu", dg, tt)])
                        wait(v, M[("sxl", dg, tt)])
                        wait(v, M[("sxl2", dg, tt)])
                        wait(v, M.get(("ost", dg, tt - 2)))
                        o = otile[:, (tt % 2) * 512:(tt % 2 + 1) * 512]
                        fence(v, s_dv, v.tensor_tensor(
                            o, gt[:, (tt % 2) * 512:(tt % 2 + 1) * 512],
                            stile[:, tt * 512:(tt + 1) * 512], ALU.mult))
                        M[("comb", dg, tt)] = s_dv.mark(v.tensor_tensor(
                            o, o, xtile[:, tt * 512:(tt + 1) * 512], ALU.add))

                    @blk.sync
                    def _(sync, dg=dg, tt=tt):
                        wait(sync, M[("comb", dg, tt)])
                        M[("ost", dg, tt)] = s_st.mark(sync.dma_start(
                            out=out[tt * 128:(tt + 1) * 128,
                                    dg * 512:(dg + 1) * 512],
                            in_=otile[:, (tt % 2) * 512:(tt % 2 + 1) * 512]), 16)

        with nc.Block() as blk:
            @blk.sync
            def _(sync):
                sync.wait_ge(s_st.h, s_st.n)

            @blk.gpsimd
            def _(gq):
                gq.wait_ge(s_gs.h, s_gs.n)

        for cm, _h in reversed(f2_cms):
            cm.__exit__(None, None, None)
        hid_cm.__exit__(None, None, None)
        xnT_m_cm.__exit__(None, None, None)

    return nc



def _prep_weights(wp, wout, w1, w2, w3):
    wp = np.asarray(wp, np.float32)
    wq = wp[0:D].reshape(H, 128, NDC, 128)
    wk = wp[D:2 * D].reshape(H, 128, NDC, 128)
    wqk = np.stack([wq.transpose(0, 3, 2, 1), wk.transpose(0, 3, 2, 1)],
                   axis=3)
    wqk = np.ascontiguousarray(wqk.reshape(H, 128, NDC * 256))
    wvv = wp[2 * D:3 * D].reshape(4, 512, NDC, 128)
    wvv = np.ascontiguousarray(wvv.transpose(0, 3, 2, 1).reshape(4, 128, NDC * 512))
    wo_t = np.ascontiguousarray(
        np.asarray(wout, np.float32).T.reshape(NDC, 128, D))
    w1t = np.ascontiguousarray(
        np.asarray(w1, np.float32).reshape(NHC, 128, NDC, 128)
        .transpose(0, 3, 2, 1).reshape(NHC, 128, D))
    w2t = np.ascontiguousarray(
        np.asarray(w2, np.float32).reshape(NHC, 128, NDC, 128)
        .transpose(0, 3, 2, 1).reshape(NHC, 128, D))
    w3r = np.ascontiguousarray(np.asarray(w3, np.float32).T)
    return wqk, wvv, wo_t, w1t, w2t, w3r


def _masks(par):
    ii = np.arange(128)[:, None]
    jj = np.arange(128)[None, :]
    md = (ii <= jj).astype(ml_dtypes.bfloat16)
    ma = np.ones((128, 256), ml_dtypes.bfloat16)
    mb = np.zeros((128, 256), ml_dtypes.bfloat16)
    if par == 0:
        ma[:, 0:128] = 0
    else:
        mb[:, 128:256] = 1
    return md, ma, mb


NEG = np.float32(-1e9)


def _bigmask(par):
    dg = np.where(np.arange(128)[:, None] <= np.arange(128)[None, :],
                  np.float32(0), NEG)
    full = np.full((128, 128), NEG, np.float32)
    zero = np.zeros((128, 128), np.float32)
    eb = np.concatenate([full, dg], 1)
    if par == 0:
        ab = np.concatenate([full, zero], 1)
        bb = np.concatenate([full, full], 1)
    else:
        ab = np.concatenate([zero, zero], 1)
        bb = np.concatenate([full, zero], 1)
    bigm = np.concatenate([dg, eb, ab, bb], 1)
    return np.ascontiguousarray(bigm.astype(ml_dtypes.bfloat16))


def _perm_rows(par):
    mine = [par + 2 * j for j in range(8)]
    rest = [(1 - par) + 2 * j for j in range(8)]
    idx = []
    for gg in mine + rest:
        idx.extend(range(gg * 128, (gg + 1) * 128))
    return np.array(idx), mine


def kernel(L0_attn_x, L0_attn_w_project, L0_attn_w_out,
           L0_ffn_w1, L0_ffn_w2, L0_ffn_w3, _taps=(), _nc=None):
    x = np.asarray(L0_attn_x, np.float32)
    wqk, wvv, wo_t, w1t, w2t, w3r = _prep_weights(
        L0_attn_w_project, L0_attn_w_out, L0_ffn_w1, L0_ffn_w2, L0_ffn_w3)

    nc = _nc if _nc is not None else build_nc(_taps)
    in_maps = []
    metas = []
    for c in range(8):
        b, par = c // 2, c % 2
        idx, mine = _perm_rows(par)
        md, ma, mb = _masks(par)
        in_maps.append({
            "xp": np.ascontiguousarray(x[b][idx]),
            "wqk": wqk, "wv": wvv, "wo": wo_t,
            "w1t": w1t, "w2t": w2t, "w3r": w3r,
            "mask_d": md, "mask_a": ma, "mask_b": mb,
            "identd": np.ascontiguousarray(np.eye(128, dtype=ml_dtypes.bfloat16)),
            "bigmd": _bigmask(par),
        })
        metas.append((b, mine))

    res = run_bass_kernel_spmd(nc, in_maps, core_ids=list(range(8)))

    final = np.empty((4, T, D), np.float32)
    for c in range(8):
        b, mine = metas[c]
        o = res.results[c]["out"]
        for j, gg in enumerate(mine):
            final[b, gg * 128:(gg + 1) * 128, :] = o[j * 128:(j + 1) * 128, :]
    if _taps:
        return final, list(res.results)
    return final


# version 30
# speedup vs baseline: 1.6914x; 1.0407x over previous
import math
from contextlib import ExitStack

import numpy as np
import ml_dtypes
import sys

sys.path.insert(0, "/opt/trn_rl_repo")
import concourse.bass as bass
import concourse.mybir as mybir
from concourse.bass_utils import run_bass_kernel_spmd

F32 = mybir.dt.float32
BF16 = mybir.dt.bfloat16
AF = mybir.ActivationFunctionType
ALU = mybir.AluOpType
AX = mybir.AxisListType

D = 2048
T = 2048
TM = 1024
H = 16
HID = 5632
NHC = HID // 128
NDC = D // 128
EPS_LN = 1e-5
EPS_Q = 1e-8
SCL = 1.0 / math.sqrt(128.0)
QMAGIC = float(3 * 2 ** 22)


class Sem:
    def __init__(self, ctx, nc, name):
        self.h = ctx.enter_context(nc.semaphore(name))
        self.n = 0

    def mark(self, inst, k=1):
        inst.then_inc(self.h, k)
        self.n += k
        return (self, self.n)

    def mile(self):
        return (self, self.n)


def wait(eng, mile):
    if mile is not None:
        eng.wait_ge(mile[0].h, mile[1])


def fence(eng, sem, inst):
    mm = sem.mark(inst)
    wait(eng, mm)
    return mm


def build_nc(taps=()):
    taps = set(taps)
    nc = bass.Bass("TRN2", target_bir_lowering=False, debug=False)

    xp = nc.dram_tensor("xp", [T, D], F32, kind="ExternalInput").ap()
    wqk = nc.dram_tensor("wqk", [H, 128, NDC * 256], F32, kind="ExternalInput").ap()
    wv = nc.dram_tensor("wv", [4, 128, NDC * 512], F32, kind="ExternalInput").ap()
    wo = nc.dram_tensor("wo", [NDC, 128, D], F32, kind="ExternalInput").ap()
    w1t = nc.dram_tensor("w1t", [NHC, 128, D], F32, kind="ExternalInput").ap()
    w2t = nc.dram_tensor("w2t", [NHC, 128, D], F32, kind="ExternalInput").ap()
    w3r = nc.dram_tensor("w3r", [HID, D], F32, kind="ExternalInput").ap()
    mask_d = nc.dram_tensor("mask_d", [128, 128], BF16, kind="ExternalInput").ap()
    mask_a = nc.dram_tensor("mask_a", [128, 256], BF16, kind="ExternalInput").ap()
    mask_b = nc.dram_tensor("mask_b", [128, 256], BF16, kind="ExternalInput").ap()
    identd = nc.dram_tensor("identd", [128, 128], BF16, kind="ExternalInput").ap()
    bigmd = nc.dram_tensor("bigmd", [128, 896], BF16, kind="ExternalInput").ap()

    out = nc.dram_tensor("out", [TM, D], F32, kind="ExternalOutput").ap()

    xn_dram = nc.dram_tensor("xn_dram", [T, D], BF16).ap()
    s_dram = nc.dram_tensor("s_dram", [TM, D], F32).ap()
    w3q_h = nc.dram_tensor("w3q_dram", [HID, D], BF16)
    w3q_dram = w3q_h.ap()

    tapt = {}
    for name, shape, dt in [
        ("xn", [T, D], BF16), ("k0", [128, T], BF16), ("q0", [128, TM], BF16),
        ("v0", [128, 16 * 512], BF16), ("cat", [128, H * TM], BF16),
        ("sm", [TM, D], F32), ("gam", [128, 16], F32),
        ("hid", [128, NHC * TM], BF16), ("w3q", [HID, D], BF16),
        ("bias0", [128, 1], F32), ("pt0", [128, 2048], BF16),
    ]:
        if name in taps:
            tapt[name] = nc.dram_tensor("tap_" + name, shape, dt,
                                        kind="ExternalOutput").ap()

    M = {}

    with ExitStack() as ctx:
        def sb(name, shape, dt):
            return ctx.enter_context(nc.sbuf_tensor(name, shape, dt))

        def manual(name, shape, dt, side="right"):
            cm = nc.sbuf_tensor(name, shape, dt, side=side)
            return cm, cm.__enter__()

        ones_c = sb("ones_c", [128, 1], BF16)
        ones_r = sb("ones_r", [1, 128], BF16)
        ones_cf = sb("ones_cf", [128, 1], F32)
        ones_rf = sb("ones_rf", [1, 128], F32)
        identb = sb("identb", [128, 128], BF16)
        bigm = sb("bigm", [128, 896], BF16)
        gcols = sb("gcols", [128, 16], F32)
        scol = sb("scol", [128, 4], F32)
        bias4 = sb("bias4", [128, 4], F32)
        tiny = sb("tiny", [1, 16], F32)
        tinyb = sb("tinyb", [1, 4], BF16)
        epsc = sb("epsc", [128, 1], F32)

        pA = ctx.enter_context(nc.psum_tensor("pA", [128, 2048], F32))
        pB = ctx.enter_context(nc.psum_tensor("pB", [128, 2048], F32))

        s_ld = Sem(ctx, nc, "s_ld")
        s_gl = Sem(ctx, nc, "s_gl")
        s_st = Sem(ctx, nc, "s_st")
        s_gs = Sem(ctx, nc, "s_gs")
        s_pe = Sem(ctx, nc, "s_pe")
        s_dv = Sem(ctx, nc, "s_dv")
        s_ac = Sem(ctx, nc, "s_ac")
        s_pl = Sem(ctx, nc, "s_pl")
        s_gq = Sem(ctx, nc, "s_gq")
        s_cast = Sem(ctx, nc, "s_cast")

        xnT_m_cm, xnT_m = manual("xnT_m", [128, NDC * TM], BF16, side="left")
        grs_cm, grs = manual("grs", [128, 1024], F32, side="left")
        grs2_cm, grs2 = manual("grs2", [128, 1024], F32, side="left")
        catT_cm, catT = manual("catT", [128, H * TM], BF16, side="left")
        xnT_r_cm, xnT_r = manual("xnT_r", [128, NDC * TM], BF16, side="left")

        def xnT_chunk(c, tt):
            if tt < 8:
                return xnT_m[:, c * TM + tt * 128: c * TM + (tt + 1) * 128]
            return xnT_r[:, c * TM + (tt - 8) * 128: c * TM + (tt - 7) * 128]

        def xnT_half(c, tc):
            if tc < 2:
                return xnT_m[:, c * TM + tc * 512: c * TM + (tc + 1) * 512]
            return xnT_r[:, c * TM + (tc - 2) * 512: c * TM + (tc - 1) * 512]

        G_ORDER = []
        for j in range(NHC):
            for hh in range(2):
                G_ORDER.append(("w1", j, hh))
                G_ORDER.append(("w3", j, hh))
        for j in range(NHC):
            for hh in range(2):
                G_ORDER.append(("w2", j, hh))
        G_ACT = list(range(len(G_ORDER)))
        gst = {"dma": 0, "act": 0}

        def gamma_slab_in(i):
            m, j, hh = G_ORDER[i]
            if m == "w1":
                return w1t[j][:, hh * 1024:(hh + 1) * 1024]
            if m == "w2":
                return w2t[j][:, hh * 1024:(hh + 1) * 1024]
            return w3r[j * 128:(j + 1) * 128, hh * 1024:(hh + 1) * 1024]

        def gamma_dma(g, count):
            for _ in range(count):
                i = gst["dma"]
                if i >= len(G_ORDER):
                    return
                if i >= 2 and ("gfree", i - 2) not in M:
                    return
                buf = grs if i % 2 == 0 else grs2
                wait(g, M.get(("gfree", i - 2)))
                M[("gdma", i)] = s_gq.mark(
                    g.dma_start(out=buf[:, :], in_=gamma_slab_in(i)), 16)
                gst["dma"] += 1

        def gamma_reduce_dve(v, count):
            for _ in range(count):
                k = gst["act"]
                if k >= len(G_ACT):
                    return
                i = G_ACT[k]
                if ("gdma", i) not in M:
                    return
                buf = grs if i % 2 == 0 else grs2
                col = {"w1": 0, "w2": 1, "w3": 2}[G_ORDER[i][0]]
                wait(v, M[("gdma", i)])
                fence(v, s_dv, v.tensor_reduce(
                    gcols[:, 9:10], buf[:, :], AX.X, ALU.add,
                    apply_absolute_value=True))
                M[("gfree", i)] = fence(v, s_dv, v.tensor_tensor(
                    gcols[:, col:col + 1], gcols[:, col:col + 1],
                    gcols[:, 9:10], ALU.add))
                gst["act"] += 1

        with nc.Block() as blk:
            @blk.sync
            def _(sync):
                s_ld.mark(sync.dma_start(out=identb[:, :], in_=identd), 16)
                M["consts"] = s_ld.mark(sync.dma_start(out=bigm[:, :], in_=bigmd), 16)

            @blk.vector
            def _(v):
                v.memset(ones_c[:, :], 1.0)
                v.memset(ones_r[:, :], 1.0)
                v.memset(ones_cf[:, :], 1.0)
                v.memset(ones_rf[:, :], 1.0)
                v.memset(gcols[:, :], 0.0)
                v.memset(tiny[:, :], 0.0)
                v.memset(epsc[:, :], EPS_LN)

        ln_cms = []
        for nm, shp, dt in [("xbuf", [128, 2 * D], F32), ("xnb", [128, 2 * D], BF16),
                            ("lnsq", [128, D], F32), ("stat", [128, 8], F32)]:
            ln_cms.append(manual(nm, shp, dt))
        xbuf, xnb, lnsq, stat = [h for _, h in ln_cms]

        with nc.Block() as blk:
            @blk.sync
            def _(sync):
                for tt in range(2):
                    M[("lx", tt)] = s_ld.mark(sync.dma_start(
                        out=xbuf[:, tt * D:(tt + 1) * D],
                        in_=xp[tt * 128:(tt + 1) * 128, :]), 16)

            for tt in range(16):
                sl = (tt % 2) * D

                @blk.vector
                def _(v, tt=tt, sl=sl):
                    wait(v, M[("lx", tt)])
                    wait(v, M.get(("lnapp", tt - 1)))
                    fence(v, s_dv, v.tensor_reduce(
                        stat[:, 0:1], xbuf[:, sl:sl + D], AX.X, ALU.add))
                    fence(v, s_dv, v.tensor_scalar(
                        stat[:, 2:3], stat[:, 0:1], 1.0 / D, None, ALU.mult))
                    fence(v, s_dv, v.tensor_tensor(
                        lnsq[:, :], xbuf[:, sl:sl + D], xbuf[:, sl:sl + D],
                        ALU.mult))
                    fence(v, s_dv, v.tensor_reduce(
                        stat[:, 1:2], lnsq[:, :], AX.X, ALU.add))
                    fence(v, s_dv, v.tensor_tensor(
                        stat[:, 5:6], stat[:, 2:3], stat[:, 2:3], ALU.mult))
                    fence(v, s_dv, v.tensor_scalar(
                        stat[:, 3:4], stat[:, 1:2], 1.0 / D, None, ALU.mult))
                    M[("lnvar", tt)] = s_dv.mark(v.tensor_tensor(
                        stat[:, 3:4], stat[:, 3:4], stat[:, 5:6], ALU.subtract))

                @blk.scalar
                def _(a, tt=tt):
                    wait(a, M[("lnvar", tt)])
                    fence(a, s_ac, a.activation(stat[:, 6:7], stat[:, 3:4],
                                                AF.Ln, bias=epsc[:, 0:1]))
                    M[("lnrstd", tt)] = s_ac.mark(a.activation(
                        stat[:, 4:5], stat[:, 6:7], AF.Exp, scale=-0.5))

                @blk.vector
                def _(v, tt=tt, sl=sl):
                    wait(v, M[("lnrstd", tt)])
                    fence(v, s_dv, v.tensor_tensor(
                        stat[:, 5:6], stat[:, 2:3], stat[:, 4:5], ALU.mult))
                    fence(v, s_dv, v.tensor_scalar(
                        stat[:, 5:6], stat[:, 5:6], -1.0, None, ALU.mult))
                    wait(v, M.get(("xnst", tt - 2)))
                    M[("lnapp", tt)] = s_dv.mark(v.tensor_scalar(
                        xnb[:, sl:sl + D], xbuf[:, sl:sl + D],
                        stat[:, 4:5], stat[:, 5:6], ALU.mult, ALU.add))

                @blk.sync
                def _(sync, tt=tt, sl=sl):
                    wait(sync, M[("lnapp", tt)])
                    M[("xnst", tt)] = s_st.mark(sync.dma_start(
                        out=xn_dram[tt * 128:(tt + 1) * 128, :],
                        in_=xnb[:, sl:sl + D]), 16)
                    if tt + 2 < 16:
                        M[("lx", tt + 2)] = s_ld.mark(sync.dma_start(
                            out=xbuf[:, sl:sl + D],
                            in_=xp[(tt + 2) * 128:(tt + 3) * 128, :]), 16)

        with nc.Block() as blk:
            @blk.sync
            def _(sync):
                sync.wait_ge(s_st.h, s_st.n)
                for c in range(NDC):
                    s_ld.mark(sync.dma_start(
                        out=xnT_m[:, c * TM:(c + 1) * TM],
                        in_=xn_dram[0:TM, c * 128:(c + 1) * 128],
                        transpose=True), 16)
                    s_ld.mark(sync.dma_start(
                        out=xnT_r[:, c * TM:(c + 1) * TM],
                        in_=xn_dram[TM:T, c * 128:(c + 1) * 128],
                        transpose=True), 16)
                M["xt_all"] = s_ld.mile()

            for _ in range(16):
                @blk.sync
                def _(sync):
                    gamma_dma(sync, 2)

                @blk.vector
                def _(v):
                    gamma_reduce_dve(v, 2)

        if "xn" in tapt:
            with nc.Block() as blk:
                @blk.sync
                def _(sync):
                    mm = s_st.mark(sync.dma_start(out=tapt["xn"], in_=xn_dram), 16)
                    wait(sync, mm)

        for cm, _h in reversed(ln_cms):
            cm.__exit__(None, None, None)

        att_cms = []
        for nm, shp, dt in [("wqkb", [128, 3 * NDC * 256], BF16),
                            ("wvb", [128, NDC * 512], BF16),
                            ("Kh", [128, 4 * T], BF16),
                            ("Qh", [128, 4 * TM], BF16),
                            ("Vg", [128, 16 * 512], BF16),
                            ("PT", [128, 2 * 2048], BF16),
                            ("rrow", [1, 256], F32),
                            ("rrb", [1, 256], BF16),
                            ("rbc", [128, 256], BF16)]:
            att_cms.append(manual(nm, shp, dt))
        (wqkb, wvb, Kh, Qh, Vg, PT, rrow, rrb, rbc) = \
            [h for _, h in att_cms]
        sqb = PT[:, 0:2048]

        def pair_kbs(p):
            return list(range(0, 2 * p + 2)) + list(range(8, 8 + 2 * p + 2))

        def wqk_slot(h):
            return (h % 3) * 4096

        def emit_wqk_load(blk, h):
            @blk.gpsimd
            def _(gq, h=h):
                wait(gq, M.get(("qmm", h - 3)))
                M[("wqk", h)] = s_cast.mark(gq.dma_start(
                    out=wqkb[:, wqk_slot(h):wqk_slot(h) + 4096],
                    in_=wqk[h]), 16)

        def emit_wv_load(blk, g, guard):
            @blk.gpsimd
            def _(gq, g=g, guard=guard):
                wait(gq, guard)
                gq.dma_start(out=wvb[:, 0:4096], in_=wv[g][:, 0:4096])\
                    .then_inc(s_cast.h, 16)
                s_cast.n += 16
                M[("wvb", g)] = s_cast.mark(gq.dma_start(
                    out=wvb[:, 4096:8192], in_=wv[g][:, 4096:8192]), 16)

        for g in range(4):
            with nc.Block() as blk:
                if g == 0:
                    emit_wv_load(blk, 0, None)
                    emit_wqk_load(blk, 0)
                    emit_wqk_load(blk, 1)
                    emit_wqk_load(blk, 2)

                @blk.sync
                def _(sync, g=g):
                    gamma_dma(sync, 4)



                for tt in range(16):
                    bank = 1024 + (tt % 2) * 512

                    @blk.tensor
                    def _(t, g=g, tt=tt, bank=bank):
                        if tt == 0:
                            wait(t, M[("wvb", g)])
                            wait(t, M["xt_all"])
                        wait(t, M.get(("vev", g, tt - 2)))
                        last = None
                        for c in range(NDC):
                            last = t.matmul(pB[:, bank:bank + 512],
                                            xnT_chunk(c, tt),
                                            wvb[:, c * 512:(c + 1) * 512],
                                            start=(c == 0), stop=(c == NDC - 1))
                        M[("vmm", g, tt)] = s_pe.mark(last)

                    @blk.vector
                    def _(v, g=g, tt=tt, bank=bank):
                        wait(v, M[("vmm", g, tt)])
                        M[("vev", g, tt)] = s_dv.mark(v.tensor_copy(
                            Vg[:, tt * 512:(tt + 1) * 512], pB[:, bank:bank + 512]))

                for i in range(4):
                    h = g * 4 + i
                    wq = wqkb[:, wqk_slot(h):wqk_slot(h) + 4096]
                    khs = Kh[:, i * T:(i + 1) * T]
                    qhs = Qh[:, i * TM:(i + 1) * TM]

                    @blk.tensor
                    def _(t, h=h, i=i, g=g, wq=wq):
                        wait(t, M[("wqk", h)])
                        if i == 0:
                            wait(t, M.get(("exp_last", h - 1)))
                        else:
                            wait(t, M[("qnred", h - 1)])
                        last = None
                        for c in range(NDC):
                            lh = wq[:, c * 256 + 128: c * 256 + 256]
                            for tc in range(4):
                                last = t.matmul(pA[:, tc * 512:(tc + 1) * 512],
                                                lh, xnT_half(c, tc),
                                                start=(c == 0), stop=(c == NDC - 1))
                        M[("kmm", h)] = s_pe.mark(last)

                    @blk.vector
                    def _(v, h=h, khs=khs):
                        wait(v, M[("kmm", h)])
                        M[("kev", h)] = fence(v, s_dv, v.tensor_copy(khs, pA[:, :]))
                        M[("ksq", h)] = s_dv.mark(v.tensor_tensor(
                            sqb[:, :], khs, khs, ALU.mult))

                    @blk.tensor
                    def _(t, h=h, i=i, wq=wq):
                        if i == 0:
                            wait(t, M.get(("cat", h - 1, 3)))
                        else:
                            wait(t, M[("qev", h - 1)])
                        last = None
                        for c in range(NDC):
                            lh = wq[:, c * 256: c * 256 + 128]
                            for tc in range(2):
                                last = t.matmul(
                                    pB[:, tc * 512:(tc + 1) * 512], lh,
                                    xnT_m[:, c * TM + tc * 512: c * TM + (tc + 1) * 512],
                                    start=(c == 0), stop=(c == NDC - 1))
                        M[("qmm", h)] = s_pe.mark(last)

                    @blk.vector
                    def _(v, h=h, qhs=qhs):
                        wait(v, M[("qmm", h)])
                        M[("qev", h)] = fence(v, s_dv, v.tensor_copy(
                            qhs, pB[:, 0:1024]))
                        M[("qsq", h)] = s_dv.mark(v.tensor_tensor(
                            sqb[:, 0:1024], qhs, qhs, ALU.mult))

                    if g * 4 + i + 3 < H:
                        emit_wqk_load(blk, g * 4 + i + 3)

                    @blk.tensor
                    def _(t, h=h):
                        wait(t, M[("ksq", h)])
                        last = None
                        for j in range(4):
                            last = t.matmul(pA[0:1, j * 512:(j + 1) * 512],
                                            ones_c[:, :],
                                            sqb[:, j * 512:(j + 1) * 512],
                                            start=True, stop=True)
                        M[("nK", h)] = s_pe.mark(last)

                    @blk.vector
                    def _(v, h=h, i=i):
                        wait(v, M[("nK", h)])
                        M[("nKred", h)] = fence(v, s_dv, v.tensor_reduce(
                            tiny[0:1, 4 + i:5 + i], pA[0:1, 0:2048], AX.X,
                            ALU.max))

                    @blk.tensor
                    def _(t, h=h):
                        wait(t, M[("qsq", h)])
                        wait(t, M[("nKred", h)])
                        last = None
                        for j in range(2):
                            last = t.matmul(pA[0:1, j * 512:(j + 1) * 512],
                                            ones_c[:, :],
                                            sqb[:, j * 512:(j + 1) * 512],
                                            start=True, stop=True)
                        M[("nQ", h)] = s_pe.mark(last)

                    @blk.vector
                    def _(v, h=h, i=i):
                        wait(v, M[("nQ", h)])
                        M[("qnred", h)] = fence(v, s_dv, v.tensor_reduce(
                            tiny[0:1, 8 + i:9 + i], pA[0:1, 0:1024], AX.X,
                            ALU.max))

                @blk.vector
                def _(v, g=g):
                    wait(v, M[("qnred", g * 4 + 3)])
                    wait(v, M.get(("exp_last", g * 4 - 1)))
                    M[("m2g", g)] = fence(v, s_dv, v.tensor_tensor(
                        tiny[0:1, 4:8], tiny[0:1, 4:8], tiny[0:1, 8:12],
                        ALU.mult))

                @blk.scalar
                def _(a, g=g):
                    wait(a, M[("m2g", g)])
                    fence(a, s_ac, a.activation(tiny[0:1, 8:12], tiny[0:1, 4:8],
                                                AF.Ln))
                    M[("mhg", g)] = s_ac.mark(a.activation(
                        tiny[0:1, 4:8], tiny[0:1, 8:12], AF.Exp, scale=0.5))

                @blk.vector
                def _(v, g=g):
                    wait(v, M[("mhg", g)])
                    fence(v, s_dv, v.tensor_scalar(
                        tiny[0:1, 4:8], tiny[0:1, 4:8], -SCL, None, ALU.mult))
                    M[("negmg", g)] = s_dv.mark(v.tensor_copy(
                        tinyb[0:1, 0:4], tiny[0:1, 4:8]))

                @blk.tensor
                def _(t, g=g):
                    wait(t, M[("negmg", g)])
                    M[("biasmm", g)] = s_pe.mark(t.matmul(
                        pB[:, 1536:1540], ones_r[:, :], tinyb[0:1, 0:4],
                        start=True, stop=True))

                @blk.vector
                def _(v, g=g):
                    wait(v, M[("biasmm", g)])
                    M[("bias_g", g)] = s_dv.mark(v.tensor_copy(
                        bias4[:, 0:4], pB[:, 1536:1540]))

                pend = []
                rsav_pend = []

                def flush_pend():
                    while pend:
                        pend.pop(0)()

                def flush_rsav():
                    while rsav_pend:
                        rsav_pend.pop(0)()

                bg = 0
                for i in range(4):
                    h = g * 4 + i
                    khs = Kh[:, i * T:(i + 1) * T]
                    qhs = Qh[:, i * TM:(i + 1) * TM]
                    for p in range(4):
                        pg = i * 4 + p
                        q = pg % 2
                        av0 = q * 512
                        bc0 = q * 512 + 256
                        rs0 = 1024 + q * 512
                        kbs = pair_kbs(p)
                        nkb = len(kbs)
                        batches = [kbs[j:j + 8] for j in range(0, nkb, 8)]
                        kidx = 0
                        for bi, batch in enumerate(batches):
                            nb = len(batch)
                            half = (bg % 2) * 2048
                            mybg = bg

                            @blk.tensor
                            def _(t, h=h, g=g, p=p, bi=bi, batch=batch,
                                  khs=khs, qhs=qhs, half=half, mybg=mybg):
                                if mybg < 1:
                                    wait(t, M[("qnred", g * 4 + 3)])
                                else:
                                    wait(t, M[("expg", g, mybg - 1)])
                                last = None
                                for idx, kb in enumerate(batch):
                                    dst = pA[:, idx * 256:(idx + 1) * 256]
                                    if kb == 2 * p:
                                        t.matmul(dst, khs[:, kb * 128:(kb + 1) * 128],
                                                 qhs[:, p * 256:(p + 1) * 256],
                                                 start=True, stop=False)
                                        last = t.matmul(
                                            pA[:, idx * 256: idx * 256 + 128],
                                            identb[:, :], bigm[:, 0:128],
                                            start=False, stop=True)
                                    elif kb == 2 * p + 1:
                                        t.matmul(dst, khs[:, kb * 128:(kb + 1) * 128],
                                                 qhs[:, p * 256:(p + 1) * 256],
                                                 start=True, stop=False)
                                        last = t.matmul(
                                            dst, identb[:, :], bigm[:, 128:384],
                                            start=False, stop=True)
                                    elif kb == 8 + 2 * p:
                                        t.matmul(dst, khs[:, kb * 128:(kb + 1) * 128],
                                                 qhs[:, p * 256:(p + 1) * 256],
                                                 start=True, stop=False)
                                        last = t.matmul(
                                            dst, identb[:, :], bigm[:, 384:640],
                                            start=False, stop=True)
                                    elif kb == 9 + 2 * p:
                                        t.matmul(dst, khs[:, kb * 128:(kb + 1) * 128],
                                                 qhs[:, p * 256:(p + 1) * 256],
                                                 start=True, stop=False)
                                        last = t.matmul(
                                            dst, identb[:, :], bigm[:, 640:896],
                                            start=False, stop=True)
                                    else:
                                        last = t.matmul(
                                            dst, khs[:, kb * 128:(kb + 1) * 128],
                                            qhs[:, p * 256:(p + 1) * 256],
                                            start=True, stop=True)
                                M[("scg", g, mybg)] = s_pe.mark(last)

                            @blk.scalar
                            def _(a, g=g, i=i, nb=nb, half=half, mybg=mybg):
                                wait(a, M[("scg", g, mybg)])
                                if mybg == 0:
                                    wait(a, M[("bias_g", g)])
                                wait(a, M.get(("rsavg", g, mybg - 2)))
                                M[("expg", g, mybg)] = s_ac.mark(a.activation(
                                    PT[:, half:half + nb * 256],
                                    pA[:, 0:nb * 256],
                                    AF.Exp, scale=SCL, bias=bias4[:, i:i + 1]))

                            if "pt0" in tapt and h == 0 and p == 1 and bi == 0:
                                @blk.sync
                                def _(sync):
                                    wait(sync, M[("expg", 0, 2)])
                                    mm = s_st.mark(sync.dma_start(
                                        out=tapt["pt0"], in_=PT[:, 0:2048]), 16)
                                    wait(sync, mm)

                            def rsav_cl(h=h, g=g, p=p, bi=bi, batch=batch,
                                        kidx=kidx, nkb=nkb, half=half,
                                        mybg=mybg, pg=pg, av0=av0, rs0=rs0):
                                @blk.tensor
                                def _(t):
                                    wait(t, M[("expg", g, mybg)])
                                    if bi == 0:
                                        wait(t, M.get(("catg", g, pg - 2)))
                                    last = None
                                    for idx, kb in enumerate(batch):
                                        kg = kidx + idx
                                        t.matmul(pB[0:1, rs0:rs0 + 256],
                                                 ones_c[:, :],
                                                 PT[:, half + idx * 256:
                                                    half + (idx + 1) * 256],
                                                 start=(kg == 0),
                                                 stop=(kg == nkb - 1))
                                        u0 = kb * 512 + (h - 4 * g) * 128
                                        last = t.matmul(
                                            pB[:, av0:av0 + 256],
                                            Vg[:, u0:u0 + 128],
                                            PT[:, half + idx * 256:
                                               half + (idx + 1) * 256],
                                            start=(kg == 0), stop=(kg == nkb - 1))
                                    M[("rsavg", g, mybg)] = s_pe.mark(last)

                            rsav_pend.append(rsav_cl)
                            if len(rsav_pend) > 1:
                                rsav_pend.pop(0)()

                            kidx += len(batch)
                            bg += 1
                            if bi == 0:
                                flush_pend()

                        last_bg = bg - 1

                        def tail(h=h, g=g, p=p, pg=pg, av0=av0, bc0=bc0,
                                 rs0=rs0, last_bg=last_bg):
                            @blk.vector
                            def _(v):
                                wait(v, M[("rsavg", g, last_bg)])
                                fence(v, s_dv, v.tensor_copy(rrow[0:1, :],
                                                             pB[0:1, rs0:rs0 + 256]))
                                fence(v, s_dv, v.reciprocal(rrow[0:1, :],
                                                            rrow[0:1, :]))
                                M[("rrbg", g, pg)] = s_dv.mark(v.tensor_copy(
                                    rrb[0:1, :], rrow[0:1, :]))

                            @blk.tensor
                            def _(t):
                                wait(t, M[("rrbg", g, pg)])
                                M[("bcastg", g, pg)] = s_pe.mark(t.matmul(
                                    pB[:, bc0:bc0 + 256], ones_r[:, :],
                                    rrb[0:1, :], start=True, stop=True))

                            @blk.vector
                            def _(v):
                                wait(v, M[("bcastg", g, pg)])
                                fence(v, s_dv, v.tensor_copy(rbc[:, :],
                                                             pB[:, bc0:bc0 + 256]))
                                M[("catg", g, pg)] = s_dv.mark(v.tensor_tensor(
                                    catT[:, h * TM + p * 256:
                                         h * TM + (p + 1) * 256],
                                    pB[:, av0:av0 + 256], rbc[:, :], ALU.mult))
                                if p == 3:
                                    M[("cat", h, 3)] = M[("catg", g, pg)]

                        flush_rsav()
                        pend.append(tail)

                        for _ in range(2):
                            @blk.sync
                            def _(sync):
                                gamma_dma(sync, 2)

                            @blk.vector
                            def _(v):
                                gamma_reduce_dve(v, 2)

                flush_pend()

                if g < 3:
                    emit_wv_load(blk, g + 1, M[("vmm", g, 15)])

        for name, key in [("k0", ("kev", 0)), ("q0", ("qev", 0))]:
            if name in tapt:
                src = Kh[:, 0:T] if name == "k0" else Qh[:, 0:TM]
                with nc.Block() as blk:
                    @blk.sync
                    def _(sync, name=name, key=key, src=src):
                        wait(sync, M[key])
                        mm = s_st.mark(sync.dma_start(out=tapt[name], in_=src), 16)
                        wait(sync, mm)
        if "v0" in tapt:
            with nc.Block() as blk:
                @blk.sync
                def _(sync):
                    mm = s_st.mark(sync.dma_start(out=tapt["v0"], in_=Vg[:, :]), 16)
                    wait(sync, mm)
        if "cat" in tapt:
            with nc.Block() as blk:
                @blk.sync
                def _(sync):
                    wait(sync, M[("cat", 15, 3)])
                    mm = s_st.mark(sync.dma_start(out=tapt["cat"], in_=catT[:, :]),
                                   16)
                    wait(sync, mm)

        for cm, _h in reversed(att_cms):
            cm.__exit__(None, None, None)
        xnT_r_cm.__exit__(None, None, None)

        op_cms = [manual("woTb", [128, NDC * D], BF16),
                  manual("sstage", [128, 2 * D], F32)]
        woTb, sstage = [h for _, h in op_cms]

        with nc.Block() as blk:
            @blk.gpsimd
            def _(gq):
                for c in range(NDC):
                    M[("wo", c)] = s_cast.mark(gq.dma_start(
                        out=woTb[:, c * D:(c + 1) * D], in_=wo[c]), 16)
            for _ in range(12):
                @blk.sync
                def _(sync):
                    gamma_dma(sync, 2)

                @blk.vector
                def _(v):
                    gamma_reduce_dve(v, 2)

            for tt in range(8):
                pg = pA if tt % 2 == 0 else pB

                @blk.tensor
                def _(t, tt=tt, pg=pg):
                    if tt == 0:
                        wait(t, M[("wo", NDC - 1)])
                    wait(t, M.get(("sev", tt - 2)))
                    last = None
                    for c in range(NDC):
                        lh = catT[:, c * TM + tt * 128: c * TM + (tt + 1) * 128]
                        for oc in range(4):
                            last = t.matmul(
                                pg[:, oc * 512:(oc + 1) * 512], lh,
                                woTb[:, c * D + oc * 512: c * D + (oc + 1) * 512],
                                start=(c == 0), stop=(c == NDC - 1))
                    M[("smm", tt)] = s_pe.mark(last)

                @blk.vector
                def _(v, tt=tt, pg=pg):
                    wait(v, M[("smm", tt)])
                    wait(v, M.get(("sst", tt - 2)))
                    sl = (tt % 2) * D
                    M[("sev", tt)] = s_dv.mark(v.tensor_copy(
                        sstage[:, sl:sl + D], pg[:, :]))

                @blk.sync
                def _(sync, tt=tt):
                    wait(sync, M[("sev", tt)])
                    sl = (tt % 2) * D
                    M[("sst", tt)] = s_st.mark(sync.dma_start(
                        out=s_dram[tt * 128:(tt + 1) * 128, :],
                        in_=sstage[:, sl:sl + D]), 16)

            M["s_all"] = s_st.mile()

        if "sm" in tapt:
            with nc.Block() as blk:
                @blk.sync
                def _(sync):
                    sync.wait_ge(s_st.h, s_st.n)
                    mm = s_st.mark(sync.dma_start(out=tapt["sm"], in_=s_dram), 16)
                    wait(sync, mm)

        for cm, _h in reversed(op_cms):
            cm.__exit__(None, None, None)
        catT_cm.__exit__(None, None, None)

        with nc.Block() as blk:
            for _ in range(140):
                if gst["dma"] >= len(G_ORDER) and gst["act"] >= len(G_ACT):
                    break

                @blk.sync
                def _(sync):
                    gamma_dma(sync, 2)

                @blk.vector
                def _(v):
                    gamma_reduce_dve(v, 2)

            @blk.tensor
            def _(t):
                t.wait_ge(s_dv.h, s_dv.n)
                last = None
                for j in range(3):
                    last = t.matmul(pA[0:1, j * 512:j * 512 + 1], ones_cf[:, :],
                                    gcols[:, j:j + 1], start=True, stop=True)
                M[("gsum")] = s_pe.mark(last)

            @blk.vector
            def _(v):
                wait(v, M["gsum"])
                mm = None
                for j in range(3):
                    fence(v, s_dv, v.tensor_copy(tiny[0:1, j:j + 1],
                                                 pA[0:1, j * 512:j * 512 + 1]))
                    fence(v, s_dv, v.tensor_scalar(
                        tiny[0:1, j:j + 1], tiny[0:1, j:j + 1],
                        1.0 / (HID * D), EPS_Q, ALU.mult, ALU.add))
                    mm = fence(v, s_dv, v.reciprocal(tiny[0:1, j:j + 1],
                                                     tiny[0:1, j:j + 1]))
                M["ginv"] = mm

            @blk.tensor
            def _(t):
                wait(t, M["ginv"])
                last = None
                for j in range(3):
                    last = t.matmul(pB[:, j * 512:j * 512 + 1], ones_rf[:, :],
                                    tiny[0:1, j:j + 1], start=True, stop=True)
                M["gbc"] = s_pe.mark(last)

            @blk.vector
            def _(v):
                wait(v, M["gbc"])
                last = None
                for j in range(3):
                    last = v.tensor_copy(scol[:, j:j + 1], pB[:, j * 512:j * 512 + 1])
                M["scol"] = s_dv.mark(last)

        if "gam" in tapt:
            with nc.Block() as blk:
                @blk.sync
                def _(sync):
                    wait(sync, M["scol"])
                    mm = s_st.mark(sync.dma_start(out=tapt["gam"], in_=gcols[:, :]),
                                   16)
                    wait(sync, mm)

        grs2_cm.__exit__(None, None, None)
        grs_cm.__exit__(None, None, None)

        hid_cm, hidT = manual("hidT", [128, NHC * TM], BF16)
        ffn_cms = []
        for nm, shp, dt in [("w1f", [128, 2 * D], F32), ("w2f", [128, 2 * D], F32),
                            ("w1qb", [128, 2 * D], BF16), ("w2qb", [128, 2 * D], BF16),
                            ("w3f", [128, D], F32), ("w3qb", [128, D], BF16),
                            ("qi", [128, 2 * D], F32),
                            ("sigb", [128, TM], F32)]:
            ffn_cms.append(manual(nm, shp, dt))
        (w1f, w2f, w1qb, w2qb, w3f, w3qb, qi, sigb) = \
            [h for _, h in ffn_cms]

        def qi_slot(k):
            return qi[:, (k % 2) * D:(k % 2 + 1) * D]

        with nc.Block() as blk:
            ffn_pend = []

            def ffn_flush():
                while ffn_pend:
                    ffn_pend.pop(0)()

            for hc in range(NHC):
                sl = (hc % 2)

                @blk.sync
                def _(sync, hc=hc, sl=sl):
                    wait(sync, M.get(("q1a", hc - 1)))
                    M[("w1l", hc)] = s_ld.mark(sync.dma_start(
                        out=w1f[:, sl * D:(sl + 1) * D], in_=w1t[hc]), 16)
                    wait(sync, M.get(("q2a", hc - 1)))
                    M[("w2l", hc)] = s_ld.mark(sync.dma_start(
                        out=w2f[:, sl * D:(sl + 1) * D], in_=w2t[hc]), 16)

                @blk.gpsimd
                def _(gq, hc=hc):
                    wait(gq, M.get(("q3a", hc - 1)))
                    M[("w3l", hc)] = s_gl.mark(gq.dma_start(
                        out=w3f[:, :], in_=w3r[hc * 128:(hc + 1) * 128, :]), 16)

                @blk.scalar
                def _(a, hc=hc):
                    wait(a, M[("w3l", hc)])
                    if hc == 0:
                        wait(a, M["scol"])
                    wait(a, M.get(("dvq", 3 * hc + 1 - 2)))
                    M[("q3a", hc)] = s_ac.mark(a.mul(
                        qi_slot(3 * hc + 1), w3f[:, :], scol[:, 2:3]))

                @blk.vector
                def _(v, hc=hc):
                    wait(v, M[("q3a", hc)])
                    wait(v, M.get(("w3st", hc - 1)))
                    fence(v, s_dv, v.tensor_scalar(
                        qi_slot(3 * hc + 1), qi_slot(3 * hc + 1),
                        QMAGIC, -QMAGIC, ALU.add, ALU.add))
                    M[("w3q", hc)] = M[("dvq", 3 * hc + 1)] = fence(
                        v, s_dv, v.tensor_scalar(
                            w3qb[:, :], qi_slot(3 * hc + 1),
                            1.0, -1.0, ALU.min, ALU.max))

                @blk.gpsimd
                def _(gq, hc=hc):
                    wait(gq, M[("w3q", hc)])
                    M[("w3st", hc)] = s_gs.mark(gq.dma_start(
                        out=w3q_dram[hc * 128:(hc + 1) * 128, :],
                        in_=w3qb[:, :]), 16)

                @blk.scalar
                def _(a, hc=hc, sl=sl):
                    wait(a, M[("w1l", hc)])
                    if hc == 0:
                        wait(a, M["scol"])
                    wait(a, M.get(("dvq", 3 * hc + 0 - 2)))
                    M[("q1a", hc)] = s_ac.mark(a.mul(
                        qi_slot(3 * hc + 0),
                        w1f[:, sl * D:(sl + 1) * D], scol[:, 0:1]))

                @blk.vector
                def _(v, hc=hc, sl=sl):
                    wait(v, M[("q1a", hc)])
                    wait(v, M.get(("x12", hc - 2)))
                    fence(v, s_dv, v.tensor_scalar(
                        qi_slot(3 * hc + 0), qi_slot(3 * hc + 0),
                        QMAGIC, -QMAGIC, ALU.add, ALU.add))
                    M[("w1q", hc)] = M[("dvq", 3 * hc + 0)] = fence(
                        v, s_dv, v.tensor_scalar(
                            w1qb[:, sl * D:(sl + 1) * D], qi_slot(3 * hc + 0),
                            1.0, -1.0, ALU.min, ALU.max))

                @blk.scalar
                def _(a, hc=hc, sl=sl):
                    wait(a, M[("w2l", hc)])
                    if hc == 0:
                        wait(a, M["scol"])
                    wait(a, M.get(("dvq", 3 * hc + 2 - 2)))
                    M[("q2a", hc)] = s_ac.mark(a.mul(
                        qi_slot(3 * hc + 2),
                        w2f[:, sl * D:(sl + 1) * D], scol[:, 1:2]))

                @blk.vector
                def _(v, hc=hc, sl=sl):
                    wait(v, M[("q2a", hc)])
                    wait(v, M.get(("x12", hc - 2)))
                    fence(v, s_dv, v.tensor_scalar(
                        qi_slot(3 * hc + 2), qi_slot(3 * hc + 2),
                        QMAGIC, -QMAGIC, ALU.add, ALU.add))
                    M[("w12q", hc)] = M[("dvq", 3 * hc + 2)] = fence(
                        v, s_dv, v.tensor_scalar(
                            w2qb[:, sl * D:(sl + 1) * D], qi_slot(3 * hc + 2),
                            1.0, -1.0, ALU.min, ALU.max))

                pg = pA if hc % 2 == 0 else pB

                @blk.tensor
                def _(t, hc=hc, sl=sl, pg=pg):
                    wait(t, M[("w1q", hc)])
                    wait(t, M[("w12q", hc)])
                    wait(t, M.get(("hid", hc - 2)))
                    last = None
                    for th in range(2):
                        for c in range(NDC):
                            t.matmul(pg[:, th * 512:(th + 1) * 512],
                                     w1qb[:, sl * D + c * 128: sl * D + (c + 1) * 128],
                                     xnT_m[:, c * TM + th * 512: c * TM + (th + 1) * 512],
                                     start=(c == 0), stop=(c == NDC - 1))
                    for th in range(2):
                        for c in range(NDC):
                            last = t.matmul(
                                pg[:, 1024 + th * 512: 1024 + (th + 1) * 512],
                                w2qb[:, sl * D + c * 128: sl * D + (c + 1) * 128],
                                xnT_m[:, c * TM + th * 512: c * TM + (th + 1) * 512],
                                start=(c == 0), stop=(c == NDC - 1))
                    M[("x12", hc)] = s_pe.mark(last)

                def sighid(hc=hc, pg=pg):
                    @blk.scalar
                    def _(a, hc=hc, pg=pg):
                        wait(a, M[("x12", hc)])
                        wait(a, M.get(("hid", hc - 1)))
                        M[("sig", hc)] = s_ac.mark(a.activation(
                            sigb[:, :], pg[:, 0:1024], AF.Sigmoid))

                    @blk.vector
                    def _(v, hc=hc, pg=pg):
                        wait(v, M[("sig", hc)])
                        M[("hid", hc)] = s_dv.mark(v.tensor_tensor(
                            hidT[:, hc * TM:(hc + 1) * TM], pg[:, 1024:2048],
                            sigb[:, :], ALU.mult))

                ffn_pend.append(sighid)
                if hc >= 1:
                    ffn_flush() if False else None
                if len(ffn_pend) > 1:
                    ffn_pend.pop(0)()
            ffn_flush()

        if "hid" in tapt:
            with nc.Block() as blk:
                @blk.sync
                def _(sync):
                    wait(sync, M[("hid", NHC - 1)])
                    mm = s_st.mark(sync.dma_start(out=tapt["hid"], in_=hidT[:, :]),
                                   16)
                    wait(sync, mm)
        if "w3q" in tapt:
            with nc.Block() as blk:
                @blk.sync
                def _(sync):
                    sync.wait_ge(s_gs.h, s_gs.n)
                    mm = s_st.mark(sync.dma_start(out=tapt["w3q"], in_=w3q_dram),
                                   16)
                    wait(sync, mm)

        for cm, _h in reversed(ffn_cms):
            cm.__exit__(None, None, None)

        f2_cms = [manual("w3qd", [128, NHC * 512], BF16),
                  manual("gt", [128, 2 * 512], F32),
                  manual("stile", [128, 8 * 512], F32),
                  manual("xtile", [128, 8 * 512], F32),
                  manual("otile", [128, 2 * 512], F32)]
        w3qd, gt, stile, xtile, otile = [h for _, h in f2_cms]

        with nc.Block() as blk:
            for dg in range(4):
                @blk.sync
                def _(sync, dg=dg):
                    if dg == 0:
                        sync.wait_ge(s_gs.h, s_gs.n)
                        sync.wait_ge(s_st.h, M["s_all"][1])
                    wait(sync, M.get(("ffdone", dg - 1)))
                    M[("w3qd", dg)] = s_ld.mark(sync.dma_start(
                        out=w3qd[:, :],
                        in_=bass.AP(w3q_h, dg * 512,
                                    [[D, 128], [128 * D, NHC], [1, 512]])), 16)
                    for tt in range(8):
                        wait(sync, M.get(("comb", dg - 1, tt)))
                        M[("sxl", dg, tt)] = s_ld.mark(sync.dma_start(
                            out=stile[:, tt * 512:(tt + 1) * 512],
                            in_=s_dram[tt * 128:(tt + 1) * 128,
                                       dg * 512:(dg + 1) * 512]), 16)
                        M[("sxl2", dg, tt)] = s_ld.mark(sync.dma_start(
                            out=xtile[:, tt * 512:(tt + 1) * 512],
                            in_=xp[tt * 128:(tt + 1) * 128,
                                   dg * 512:(dg + 1) * 512]), 16)

                for tt in range(8):
                    @blk.tensor
                    def _(t, dg=dg, tt=tt):
                        if tt == 0:
                            wait(t, M[("w3qd", dg)])
                        wait(t, M.get(("silu", dg - 1, tt)))
                        pg = pA if tt < 4 else pB
                        col = (tt % 4) * 512
                        last = None
                        for hc in range(NHC):
                            last = t.matmul(
                                pg[:, col:col + 512],
                                hidT[:, hc * TM + tt * 128: hc * TM + (tt + 1) * 128],
                                w3qd[:, hc * 512:(hc + 1) * 512],
                                start=(hc == 0), stop=(hc == NHC - 1))
                        M[("fft", dg, tt)] = s_pe.mark(last)
                        if tt == 7:
                            M[("ffdone", dg)] = M[("fft", dg, tt)]

                for tt in range(8):
                    pg = pA if tt < 4 else pB
                    col = (tt % 4) * 512

                    @blk.scalar
                    def _(a, dg=dg, tt=tt, pg=pg, col=col):
                        wait(a, M[("fft", dg, tt)])
                        wait(a, M.get(("comb", dg, tt - 2)))
                        M[("silu", dg, tt)] = s_ac.mark(a.activation(
                            gt[:, (tt % 2) * 512:(tt % 2 + 1) * 512],
                            pg[:, col:col + 512], AF.Silu))

                    @blk.vector
                    def _(v, dg=dg, tt=tt):
                        wait(v, M[("silu", dg, tt)])
                        wait(v, M[("sxl", dg, tt)])
                        wait(v, M[("sxl2", dg, tt)])
                        wait(v, M.get(("ost", dg, tt - 2)))
                        o = otile[:, (tt % 2) * 512:(tt % 2 + 1) * 512]
                        fence(v, s_dv, v.tensor_tensor(
                            o, gt[:, (tt % 2) * 512:(tt % 2 + 1) * 512],
                            stile[:, tt * 512:(tt + 1) * 512], ALU.mult))
                        M[("comb", dg, tt)] = s_dv.mark(v.tensor_tensor(
                            o, o, xtile[:, tt * 512:(tt + 1) * 512], ALU.add))

                    @blk.sync
                    def _(sync, dg=dg, tt=tt):
                        wait(sync, M[("comb", dg, tt)])
                        M[("ost", dg, tt)] = s_st.mark(sync.dma_start(
                            out=out[tt * 128:(tt + 1) * 128,
                                    dg * 512:(dg + 1) * 512],
                            in_=otile[:, (tt % 2) * 512:(tt % 2 + 1) * 512]), 16)

        with nc.Block() as blk:
            @blk.sync
            def _(sync):
                sync.wait_ge(s_st.h, s_st.n)

            @blk.gpsimd
            def _(gq):
                gq.wait_ge(s_gs.h, s_gs.n)

        for cm, _h in reversed(f2_cms):
            cm.__exit__(None, None, None)
        hid_cm.__exit__(None, None, None)
        xnT_m_cm.__exit__(None, None, None)

    return nc



def _prep_weights(wp, wout, w1, w2, w3):
    wp = np.asarray(wp, np.float32)
    wq = wp[0:D].reshape(H, 128, NDC, 128)
    wk = wp[D:2 * D].reshape(H, 128, NDC, 128)
    wqk = np.stack([wq.transpose(0, 3, 2, 1), wk.transpose(0, 3, 2, 1)],
                   axis=3)
    wqk = np.ascontiguousarray(wqk.reshape(H, 128, NDC * 256))
    wvv = wp[2 * D:3 * D].reshape(4, 512, NDC, 128)
    wvv = np.ascontiguousarray(wvv.transpose(0, 3, 2, 1).reshape(4, 128, NDC * 512))
    wo_t = np.ascontiguousarray(
        np.asarray(wout, np.float32).T.reshape(NDC, 128, D))
    w1t = np.ascontiguousarray(
        np.asarray(w1, np.float32).reshape(NHC, 128, NDC, 128)
        .transpose(0, 3, 2, 1).reshape(NHC, 128, D))
    w2t = np.ascontiguousarray(
        np.asarray(w2, np.float32).reshape(NHC, 128, NDC, 128)
        .transpose(0, 3, 2, 1).reshape(NHC, 128, D))
    w3r = np.ascontiguousarray(np.asarray(w3, np.float32).T)
    return wqk, wvv, wo_t, w1t, w2t, w3r


def _masks(par):
    ii = np.arange(128)[:, None]
    jj = np.arange(128)[None, :]
    md = (ii <= jj).astype(ml_dtypes.bfloat16)
    ma = np.ones((128, 256), ml_dtypes.bfloat16)
    mb = np.zeros((128, 256), ml_dtypes.bfloat16)
    if par == 0:
        ma[:, 0:128] = 0
    else:
        mb[:, 128:256] = 1
    return md, ma, mb


NEG = np.float32(-1e9)


def _bigmask(par):
    dg = np.where(np.arange(128)[:, None] <= np.arange(128)[None, :],
                  np.float32(0), NEG)
    full = np.full((128, 128), NEG, np.float32)
    zero = np.zeros((128, 128), np.float32)
    eb = np.concatenate([full, dg], 1)
    if par == 0:
        ab = np.concatenate([full, zero], 1)
        bb = np.concatenate([full, full], 1)
    else:
        ab = np.concatenate([zero, zero], 1)
        bb = np.concatenate([full, zero], 1)
    bigm = np.concatenate([dg, eb, ab, bb], 1)
    return np.ascontiguousarray(bigm.astype(ml_dtypes.bfloat16))


def _perm_rows(par):
    mine = [par + 2 * j for j in range(8)]
    rest = [(1 - par) + 2 * j for j in range(8)]
    idx = []
    for gg in mine + rest:
        idx.extend(range(gg * 128, (gg + 1) * 128))
    return np.array(idx), mine


def kernel(L0_attn_x, L0_attn_w_project, L0_attn_w_out,
           L0_ffn_w1, L0_ffn_w2, L0_ffn_w3, _taps=(), _nc=None):
    x = np.asarray(L0_attn_x, np.float32)
    wqk, wvv, wo_t, w1t, w2t, w3r = _prep_weights(
        L0_attn_w_project, L0_attn_w_out, L0_ffn_w1, L0_ffn_w2, L0_ffn_w3)

    nc = _nc if _nc is not None else build_nc(_taps)
    in_maps = []
    metas = []
    for c in range(8):
        b, par = c // 2, c % 2
        idx, mine = _perm_rows(par)
        md, ma, mb = _masks(par)
        in_maps.append({
            "xp": np.ascontiguousarray(x[b][idx]),
            "wqk": wqk, "wv": wvv, "wo": wo_t,
            "w1t": w1t, "w2t": w2t, "w3r": w3r,
            "mask_d": md, "mask_a": ma, "mask_b": mb,
            "identd": np.ascontiguousarray(np.eye(128, dtype=ml_dtypes.bfloat16)),
            "bigmd": _bigmask(par),
        })
        metas.append((b, mine))

    res = run_bass_kernel_spmd(nc, in_maps, core_ids=list(range(8)))

    final = np.empty((4, T, D), np.float32)
    for c in range(8):
        b, mine = metas[c]
        o = res.results[c]["out"]
        for j, gg in enumerate(mine):
            final[b, gg * 128:(gg + 1) * 128, :] = o[j * 128:(j + 1) * 128, :]
    if _taps:
        return final, list(res.results)
    return final


# version 33
# speedup vs baseline: 1.7065x; 1.0089x over previous
import math
from contextlib import ExitStack

import numpy as np
import ml_dtypes
import sys

sys.path.insert(0, "/opt/trn_rl_repo")
import concourse.bass as bass
import concourse.mybir as mybir
from concourse.bass_utils import run_bass_kernel_spmd

F32 = mybir.dt.float32
BF16 = mybir.dt.bfloat16
AF = mybir.ActivationFunctionType
ALU = mybir.AluOpType
AX = mybir.AxisListType

D = 2048
T = 2048
TM = 1024
H = 16
HID = 5632
NHC = HID // 128
NDC = D // 128
EPS_LN = 1e-5
EPS_Q = 1e-8
SCL = 1.0 / math.sqrt(128.0)
QMAGIC = float(3 * 2 ** 22)


class Sem:
    def __init__(self, ctx, nc, name):
        self.h = ctx.enter_context(nc.semaphore(name))
        self.n = 0

    def mark(self, inst, k=1):
        inst.then_inc(self.h, k)
        self.n += k
        return (self, self.n)

    def mile(self):
        return (self, self.n)


def wait(eng, mile):
    if mile is not None:
        eng.wait_ge(mile[0].h, mile[1])


def fence(eng, sem, inst):
    mm = sem.mark(inst)
    wait(eng, mm)
    return mm


def build_nc(taps=()):
    taps = set(taps)
    nc = bass.Bass("TRN2", target_bir_lowering=False, debug=False)

    xp = nc.dram_tensor("xp", [T, D], F32, kind="ExternalInput").ap()
    wqk = nc.dram_tensor("wqk", [H, 128, NDC * 256], F32, kind="ExternalInput").ap()
    wv = nc.dram_tensor("wv", [4, 128, NDC * 512], F32, kind="ExternalInput").ap()
    wo = nc.dram_tensor("wo", [NDC, 128, D], F32, kind="ExternalInput").ap()
    w1t = nc.dram_tensor("w1t", [NHC, 128, D], F32, kind="ExternalInput").ap()
    w2t = nc.dram_tensor("w2t", [NHC, 128, D], F32, kind="ExternalInput").ap()
    w3r = nc.dram_tensor("w3r", [HID, D], F32, kind="ExternalInput").ap()
    mask_d = nc.dram_tensor("mask_d", [128, 128], BF16, kind="ExternalInput").ap()
    mask_a = nc.dram_tensor("mask_a", [128, 256], BF16, kind="ExternalInput").ap()
    mask_b = nc.dram_tensor("mask_b", [128, 256], BF16, kind="ExternalInput").ap()
    identd = nc.dram_tensor("identd", [128, 128], BF16, kind="ExternalInput").ap()
    bigmd = nc.dram_tensor("bigmd", [128, 896], BF16, kind="ExternalInput").ap()

    out = nc.dram_tensor("out", [TM, D], F32, kind="ExternalOutput").ap()

    xn_dram = nc.dram_tensor("xn_dram", [T, D], BF16).ap()
    s_dram = nc.dram_tensor("s_dram", [TM, D], F32).ap()
    w3q_h = nc.dram_tensor("w3q_dram", [HID, D], BF16)
    w3q_dram = w3q_h.ap()

    tapt = {}
    for name, shape, dt in [
        ("xn", [T, D], BF16), ("k0", [128, T], BF16), ("q0", [128, TM], BF16),
        ("v0", [128, 16 * 512], BF16), ("cat", [128, H * TM], BF16),
        ("sm", [TM, D], F32), ("gam", [128, 16], F32),
        ("hid", [128, NHC * TM], BF16), ("w3q", [HID, D], BF16),
        ("bias0", [128, 1], F32), ("pt0", [128, 2048], BF16),
    ]:
        if name in taps:
            tapt[name] = nc.dram_tensor("tap_" + name, shape, dt,
                                        kind="ExternalOutput").ap()

    M = {}

    with ExitStack() as ctx:
        def sb(name, shape, dt):
            return ctx.enter_context(nc.sbuf_tensor(name, shape, dt))

        def manual(name, shape, dt, side="right"):
            cm = nc.sbuf_tensor(name, shape, dt, side=side)
            return cm, cm.__enter__()

        ones_c = sb("ones_c", [128, 1], BF16)
        ones_r = sb("ones_r", [1, 128], BF16)
        ones_cf = sb("ones_cf", [128, 1], F32)
        ones_rf = sb("ones_rf", [1, 128], F32)
        identb = sb("identb", [128, 128], BF16)
        bigm = sb("bigm", [128, 896], BF16)
        gcols = sb("gcols", [128, 16], F32)
        scol = sb("scol", [128, 4], F32)
        bias4 = sb("bias4", [128, 4], F32)
        tiny = sb("tiny", [1, 16], F32)
        tinyb = sb("tinyb", [1, 4], BF16)
        epsc = sb("epsc", [128, 1], F32)

        pA = ctx.enter_context(nc.psum_tensor("pA", [128, 2048], F32))
        pB = ctx.enter_context(nc.psum_tensor("pB", [128, 2048], F32))

        s_ld = Sem(ctx, nc, "s_ld")
        s_gl = Sem(ctx, nc, "s_gl")
        s_st = Sem(ctx, nc, "s_st")
        s_gs = Sem(ctx, nc, "s_gs")
        s_pe = Sem(ctx, nc, "s_pe")
        s_dv = Sem(ctx, nc, "s_dv")
        s_ac = Sem(ctx, nc, "s_ac")
        s_pl = Sem(ctx, nc, "s_pl")
        s_gq = Sem(ctx, nc, "s_gq")
        s_cast = Sem(ctx, nc, "s_cast")

        xnT_m_cm, xnT_m = manual("xnT_m", [128, NDC * TM], BF16, side="left")
        grs_cm, grs = manual("grs", [128, 1024], F32, side="left")
        grs2_cm, grs2 = manual("grs2", [128, 1024], F32, side="left")
        catT_cm, catT = manual("catT", [128, H * TM], BF16, side="left")
        xnT_r_cm, xnT_r = manual("xnT_r", [128, NDC * TM], BF16, side="left")

        def xnT_chunk(c, tt):
            if tt < 8:
                return xnT_m[:, c * TM + tt * 128: c * TM + (tt + 1) * 128]
            return xnT_r[:, c * TM + (tt - 8) * 128: c * TM + (tt - 7) * 128]

        def xnT_half(c, tc):
            if tc < 2:
                return xnT_m[:, c * TM + tc * 512: c * TM + (tc + 1) * 512]
            return xnT_r[:, c * TM + (tc - 2) * 512: c * TM + (tc - 1) * 512]

        G_ORDER = []
        for j in range(NHC):
            for hh in range(2):
                G_ORDER.append(("w1", j, hh))
                G_ORDER.append(("w3", j, hh))
        for j in range(NHC):
            for hh in range(2):
                G_ORDER.append(("w2", j, hh))
        G_ACT = list(range(len(G_ORDER)))
        gst = {"dma": 0, "act": 0}

        def gamma_slab_in(i):
            m, j, hh = G_ORDER[i]
            if m == "w1":
                return w1t[j][:, hh * 1024:(hh + 1) * 1024]
            if m == "w2":
                return w2t[j][:, hh * 1024:(hh + 1) * 1024]
            return w3r[j * 128:(j + 1) * 128, hh * 1024:(hh + 1) * 1024]

        def gamma_dma(g, count):
            for _ in range(count):
                i = gst["dma"]
                if i >= len(G_ORDER):
                    return
                if i >= 2 and ("gfree", i - 2) not in M:
                    return
                buf = grs if i % 2 == 0 else grs2
                wait(g, M.get(("gfree", i - 2)))
                M[("gdma", i)] = s_gq.mark(
                    g.dma_start(out=buf[:, :], in_=gamma_slab_in(i)), 16)
                gst["dma"] += 1

        def gamma_reduce_dve(v, count):
            for _ in range(count):
                k = gst["act"]
                if k >= len(G_ACT):
                    return
                i = G_ACT[k]
                if ("gdma", i) not in M:
                    return
                buf = grs if i % 2 == 0 else grs2
                col = {"w1": 0, "w2": 1, "w3": 2}[G_ORDER[i][0]]
                wait(v, M[("gdma", i)])
                fence(v, s_dv, v.tensor_reduce(
                    gcols[:, 9:10], buf[:, :], AX.X, ALU.add,
                    apply_absolute_value=True))
                M[("gfree", i)] = fence(v, s_dv, v.tensor_tensor(
                    gcols[:, col:col + 1], gcols[:, col:col + 1],
                    gcols[:, 9:10], ALU.add))
                gst["act"] += 1

        with nc.Block() as blk:
            @blk.sync
            def _(sync):
                s_ld.mark(sync.dma_start(out=identb[:, :], in_=identd), 16)
                M["consts"] = s_ld.mark(sync.dma_start(out=bigm[:, :], in_=bigmd), 16)

            @blk.vector
            def _(v):
                v.memset(ones_c[:, :], 1.0)
                v.memset(ones_r[:, :], 1.0)
                v.memset(ones_cf[:, :], 1.0)
                v.memset(ones_rf[:, :], 1.0)
                v.memset(gcols[:, :], 0.0)
                v.memset(tiny[:, :], 0.0)
                v.memset(epsc[:, :], EPS_LN)

        ln_cms = []
        for nm, shp, dt in [("xbuf", [128, 2 * D], F32), ("xnb", [128, 2 * D], BF16),
                            ("lnsq", [128, D], F32), ("stat", [128, 8], F32)]:
            ln_cms.append(manual(nm, shp, dt))
        xbuf, xnb, lnsq, stat = [h for _, h in ln_cms]

        with nc.Block() as blk:
            @blk.sync
            def _(sync):
                for tt in range(2):
                    M[("lx", tt)] = s_ld.mark(sync.dma_start(
                        out=xbuf[:, tt * D:(tt + 1) * D],
                        in_=xp[tt * 128:(tt + 1) * 128, :]), 16)

            for tt in range(16):
                sl = (tt % 2) * D

                @blk.vector
                def _(v, tt=tt, sl=sl):
                    wait(v, M[("lx", tt)])
                    wait(v, M.get(("lnapp", tt - 1)))
                    fence(v, s_dv, v.tensor_reduce(
                        stat[:, 0:1], xbuf[:, sl:sl + D], AX.X, ALU.add))
                    fence(v, s_dv, v.tensor_scalar(
                        stat[:, 2:3], stat[:, 0:1], 1.0 / D, None, ALU.mult))
                    fence(v, s_dv, v.tensor_tensor(
                        lnsq[:, :], xbuf[:, sl:sl + D], xbuf[:, sl:sl + D],
                        ALU.mult))
                    fence(v, s_dv, v.tensor_reduce(
                        stat[:, 1:2], lnsq[:, :], AX.X, ALU.add))
                    fence(v, s_dv, v.tensor_tensor(
                        stat[:, 5:6], stat[:, 2:3], stat[:, 2:3], ALU.mult))
                    fence(v, s_dv, v.tensor_scalar(
                        stat[:, 3:4], stat[:, 1:2], 1.0 / D, None, ALU.mult))
                    M[("lnvar", tt)] = s_dv.mark(v.tensor_tensor(
                        stat[:, 3:4], stat[:, 3:4], stat[:, 5:6], ALU.subtract))

                @blk.scalar
                def _(a, tt=tt):
                    wait(a, M[("lnvar", tt)])
                    fence(a, s_ac, a.activation(stat[:, 6:7], stat[:, 3:4],
                                                AF.Ln, bias=epsc[:, 0:1]))
                    M[("lnrstd", tt)] = s_ac.mark(a.activation(
                        stat[:, 4:5], stat[:, 6:7], AF.Exp, scale=-0.5))

                @blk.vector
                def _(v, tt=tt, sl=sl):
                    wait(v, M[("lnrstd", tt)])
                    fence(v, s_dv, v.tensor_tensor(
                        stat[:, 5:6], stat[:, 2:3], stat[:, 4:5], ALU.mult))
                    fence(v, s_dv, v.tensor_scalar(
                        stat[:, 5:6], stat[:, 5:6], -1.0, None, ALU.mult))
                    wait(v, M.get(("xnst", tt - 2)))
                    M[("lnapp", tt)] = s_dv.mark(v.tensor_scalar(
                        xnb[:, sl:sl + D], xbuf[:, sl:sl + D],
                        stat[:, 4:5], stat[:, 5:6], ALU.mult, ALU.add))

                @blk.sync
                def _(sync, tt=tt, sl=sl):
                    wait(sync, M[("lnapp", tt)])
                    M[("xnst", tt)] = s_st.mark(sync.dma_start(
                        out=xn_dram[tt * 128:(tt + 1) * 128, :],
                        in_=xnb[:, sl:sl + D]), 16)
                    if tt + 2 < 16:
                        M[("lx", tt + 2)] = s_ld.mark(sync.dma_start(
                            out=xbuf[:, sl:sl + D],
                            in_=xp[(tt + 2) * 128:(tt + 3) * 128, :]), 16)

        with nc.Block() as blk:
            @blk.sync
            def _(sync):
                sync.wait_ge(s_st.h, s_st.n)
                for c in range(NDC):
                    s_ld.mark(sync.dma_start(
                        out=xnT_m[:, c * TM:(c + 1) * TM],
                        in_=xn_dram[0:TM, c * 128:(c + 1) * 128],
                        transpose=True), 16)
                    s_ld.mark(sync.dma_start(
                        out=xnT_r[:, c * TM:(c + 1) * TM],
                        in_=xn_dram[TM:T, c * 128:(c + 1) * 128],
                        transpose=True), 16)
                M["xt_all"] = s_ld.mile()

            for _ in range(16):
                @blk.sync
                def _(sync):
                    gamma_dma(sync, 2)

                @blk.vector
                def _(v):
                    gamma_reduce_dve(v, 2)

        if "xn" in tapt:
            with nc.Block() as blk:
                @blk.sync
                def _(sync):
                    mm = s_st.mark(sync.dma_start(out=tapt["xn"], in_=xn_dram), 16)
                    wait(sync, mm)

        for cm, _h in reversed(ln_cms):
            cm.__exit__(None, None, None)

        att_cms = []
        for nm, shp, dt in [("wqkb", [128, 3 * NDC * 256], BF16),
                            ("wvb", [128, NDC * 512], BF16),
                            ("Kh", [128, 4 * T], BF16),
                            ("Qh", [128, 4 * TM], BF16),
                            ("Vg", [128, 16 * 512], BF16),
                            ("PT", [128, 2 * 2048], BF16),
                            ("rrow", [1, 256], F32),
                            ("rrb", [1, 256], BF16),
                            ("rbc", [128, 256], BF16)]:
            att_cms.append(manual(nm, shp, dt))
        (wqkb, wvb, Kh, Qh, Vg, PT, rrow, rrb, rbc) = \
            [h for _, h in att_cms]
        sqb = PT[:, 0:2048]

        def pair_kbs(p):
            return list(range(0, 2 * p + 2)) + list(range(8, 8 + 2 * p + 2))

        def wqk_slot(h):
            return (h % 3) * 4096

        def emit_wqk_load(blk, h):
            @blk.gpsimd
            def _(gq, h=h):
                wait(gq, M.get(("qmm", h - 3)))
                M[("wqk", h)] = s_cast.mark(gq.dma_start(
                    out=wqkb[:, wqk_slot(h):wqk_slot(h) + 4096],
                    in_=wqk[h]), 16)

        def emit_wv_load(blk, g, guard):
            @blk.gpsimd
            def _(gq, g=g, guard=guard):
                wait(gq, guard)
                gq.dma_start(out=wvb[:, 0:4096], in_=wv[g][:, 0:4096])\
                    .then_inc(s_cast.h, 16)
                s_cast.n += 16
                M[("wvb", g)] = s_cast.mark(gq.dma_start(
                    out=wvb[:, 4096:8192], in_=wv[g][:, 4096:8192]), 16)

        for g in range(4):
            with nc.Block() as blk:
                if g == 0:
                    emit_wv_load(blk, 0, None)
                    emit_wqk_load(blk, 0)
                    emit_wqk_load(blk, 1)
                    emit_wqk_load(blk, 2)

                @blk.sync
                def _(sync, g=g):
                    gamma_dma(sync, 4)



                for tt in range(16):
                    bank = 1024 + (tt % 2) * 512

                    @blk.tensor
                    def _(t, g=g, tt=tt, bank=bank):
                        if tt == 0:
                            wait(t, M[("wvb", g)])
                            wait(t, M["xt_all"])
                        wait(t, M.get(("vev", g, tt - 2)))
                        last = None
                        for c in range(NDC):
                            last = t.matmul(pB[:, bank:bank + 512],
                                            xnT_chunk(c, tt),
                                            wvb[:, c * 512:(c + 1) * 512],
                                            start=(c == 0), stop=(c == NDC - 1))
                        M[("vmm", g, tt)] = s_pe.mark(last)

                    @blk.vector
                    def _(v, g=g, tt=tt, bank=bank):
                        wait(v, M[("vmm", g, tt)])
                        M[("vev", g, tt)] = s_dv.mark(v.tensor_copy(
                            Vg[:, tt * 512:(tt + 1) * 512], pB[:, bank:bank + 512]))

                for i in range(4):
                    h = g * 4 + i
                    wq = wqkb[:, wqk_slot(h):wqk_slot(h) + 4096]
                    khs = Kh[:, i * T:(i + 1) * T]
                    qhs = Qh[:, i * TM:(i + 1) * TM]

                    @blk.tensor
                    def _(t, h=h, i=i, g=g, wq=wq):
                        wait(t, M[("wqk", h)])
                        if i == 0:
                            wait(t, M.get(("exp_last", h - 1)))
                        else:
                            wait(t, M[("qnred", h - 1)])
                        last = None
                        for c in range(NDC):
                            lh = wq[:, c * 256 + 128: c * 256 + 256]
                            for tc in range(4):
                                last = t.matmul(pA[:, tc * 512:(tc + 1) * 512],
                                                lh, xnT_half(c, tc),
                                                start=(c == 0), stop=(c == NDC - 1))
                        M[("kmm", h)] = s_pe.mark(last)

                    @blk.vector
                    def _(v, h=h, khs=khs):
                        wait(v, M[("kmm", h)])
                        M[("kev", h)] = fence(v, s_dv, v.tensor_copy(khs, pA[:, :]))
                        M[("ksq", h)] = s_dv.mark(v.tensor_tensor(
                            sqb[:, :], khs, khs, ALU.mult))

                    @blk.tensor
                    def _(t, h=h, i=i, wq=wq):
                        if i == 0:
                            wait(t, M.get(("cat", h - 1, 3)))
                        else:
                            wait(t, M[("qev", h - 1)])
                        last = None
                        for c in range(NDC):
                            lh = wq[:, c * 256: c * 256 + 128]
                            for tc in range(2):
                                last = t.matmul(
                                    pB[:, tc * 512:(tc + 1) * 512], lh,
                                    xnT_m[:, c * TM + tc * 512: c * TM + (tc + 1) * 512],
                                    start=(c == 0), stop=(c == NDC - 1))
                        M[("qmm", h)] = s_pe.mark(last)

                    @blk.vector
                    def _(v, h=h, qhs=qhs):
                        wait(v, M[("qmm", h)])
                        M[("qev", h)] = fence(v, s_dv, v.tensor_copy(
                            qhs, pB[:, 0:1024]))
                        M[("qsq", h)] = s_dv.mark(v.tensor_tensor(
                            sqb[:, 0:1024], qhs, qhs, ALU.mult))

                    if g * 4 + i + 3 < H:
                        emit_wqk_load(blk, g * 4 + i + 3)

                    @blk.tensor
                    def _(t, h=h):
                        wait(t, M[("ksq", h)])
                        last = None
                        for j in range(4):
                            last = t.matmul(pA[0:1, j * 512:(j + 1) * 512],
                                            ones_c[:, :],
                                            sqb[:, j * 512:(j + 1) * 512],
                                            start=True, stop=True)
                        M[("nK", h)] = s_pe.mark(last)

                    @blk.vector
                    def _(v, h=h, i=i):
                        wait(v, M[("nK", h)])
                        M[("nKred", h)] = fence(v, s_dv, v.tensor_reduce(
                            tiny[0:1, 4 + i:5 + i], pA[0:1, 0:2048], AX.X,
                            ALU.max))

                    @blk.tensor
                    def _(t, h=h):
                        wait(t, M[("qsq", h)])
                        wait(t, M[("nKred", h)])
                        last = None
                        for j in range(2):
                            last = t.matmul(pA[0:1, j * 512:(j + 1) * 512],
                                            ones_c[:, :],
                                            sqb[:, j * 512:(j + 1) * 512],
                                            start=True, stop=True)
                        M[("nQ", h)] = s_pe.mark(last)

                    @blk.vector
                    def _(v, h=h, i=i):
                        wait(v, M[("nQ", h)])
                        M[("qnred", h)] = fence(v, s_dv, v.tensor_reduce(
                            tiny[0:1, 8 + i:9 + i], pA[0:1, 0:1024], AX.X,
                            ALU.max))

                @blk.vector
                def _(v, g=g):
                    wait(v, M[("qnred", g * 4 + 3)])
                    wait(v, M.get(("exp_last", g * 4 - 1)))
                    M[("m2g", g)] = fence(v, s_dv, v.tensor_tensor(
                        tiny[0:1, 4:8], tiny[0:1, 4:8], tiny[0:1, 8:12],
                        ALU.mult))

                @blk.scalar
                def _(a, g=g):
                    wait(a, M[("m2g", g)])
                    fence(a, s_ac, a.activation(tiny[0:1, 8:12], tiny[0:1, 4:8],
                                                AF.Ln))
                    M[("mhg", g)] = s_ac.mark(a.activation(
                        tiny[0:1, 4:8], tiny[0:1, 8:12], AF.Exp, scale=0.5))

                @blk.vector
                def _(v, g=g):
                    wait(v, M[("mhg", g)])
                    fence(v, s_dv, v.tensor_scalar(
                        tiny[0:1, 4:8], tiny[0:1, 4:8], -SCL, None, ALU.mult))
                    M[("negmg", g)] = s_dv.mark(v.tensor_copy(
                        tinyb[0:1, 0:4], tiny[0:1, 4:8]))

                @blk.tensor
                def _(t, g=g):
                    wait(t, M[("negmg", g)])
                    M[("biasmm", g)] = s_pe.mark(t.matmul(
                        pB[:, 1536:1540], ones_r[:, :], tinyb[0:1, 0:4],
                        start=True, stop=True))

                @blk.vector
                def _(v, g=g):
                    wait(v, M[("biasmm", g)])
                    M[("bias_g", g)] = s_dv.mark(v.tensor_copy(
                        bias4[:, 0:4], pB[:, 1536:1540]))

                pend = []
                rsav_pend = []

                def flush_pend():
                    while pend:
                        pend.pop(0)()

                def flush_rsav():
                    while rsav_pend:
                        rsav_pend.pop(0)()

                bg = 0
                for i in range(4):
                    h = g * 4 + i
                    khs = Kh[:, i * T:(i + 1) * T]
                    qhs = Qh[:, i * TM:(i + 1) * TM]
                    for p in range(4):
                        pg = i * 4 + p
                        q = pg % 2
                        av0 = q * 512
                        bc0 = q * 512 + 256
                        rs0 = 1024 + q * 512
                        kbs = pair_kbs(p)
                        nkb = len(kbs)
                        batches = [kbs[j:j + 8] for j in range(0, nkb, 8)]
                        kidx = 0
                        for bi, batch in enumerate(batches):
                            nb = len(batch)
                            half = (bg % 2) * 2048
                            mybg = bg

                            @blk.tensor
                            def _(t, h=h, g=g, p=p, bi=bi, batch=batch,
                                  khs=khs, qhs=qhs, half=half, mybg=mybg):
                                if mybg < 1:
                                    wait(t, M[("qnred", g * 4 + 3)])
                                else:
                                    wait(t, M[("expg", g, mybg - 1)])
                                last = None
                                for idx, kb in enumerate(batch):
                                    dst = pA[:, idx * 256:(idx + 1) * 256]
                                    if kb == 2 * p:
                                        t.matmul(dst, khs[:, kb * 128:(kb + 1) * 128],
                                                 qhs[:, p * 256:(p + 1) * 256],
                                                 start=True, stop=False)
                                        last = t.matmul(
                                            pA[:, idx * 256: idx * 256 + 128],
                                            identb[:, :], bigm[:, 0:128],
                                            start=False, stop=True)
                                    elif kb == 2 * p + 1:
                                        t.matmul(dst, khs[:, kb * 128:(kb + 1) * 128],
                                                 qhs[:, p * 256:(p + 1) * 256],
                                                 start=True, stop=False)
                                        last = t.matmul(
                                            dst, identb[:, :], bigm[:, 128:384],
                                            start=False, stop=True)
                                    elif kb == 8 + 2 * p:
                                        t.matmul(dst, khs[:, kb * 128:(kb + 1) * 128],
                                                 qhs[:, p * 256:(p + 1) * 256],
                                                 start=True, stop=False)
                                        last = t.matmul(
                                            dst, identb[:, :], bigm[:, 384:640],
                                            start=False, stop=True)
                                    elif kb == 9 + 2 * p:
                                        t.matmul(dst, khs[:, kb * 128:(kb + 1) * 128],
                                                 qhs[:, p * 256:(p + 1) * 256],
                                                 start=True, stop=False)
                                        last = t.matmul(
                                            dst, identb[:, :], bigm[:, 640:896],
                                            start=False, stop=True)
                                    else:
                                        last = t.matmul(
                                            dst, khs[:, kb * 128:(kb + 1) * 128],
                                            qhs[:, p * 256:(p + 1) * 256],
                                            start=True, stop=True)
                                M[("scg", g, mybg)] = s_pe.mark(last)

                            @blk.scalar
                            def _(a, g=g, i=i, nb=nb, half=half, mybg=mybg):
                                wait(a, M[("scg", g, mybg)])
                                if mybg == 0:
                                    wait(a, M[("bias_g", g)])
                                wait(a, M.get(("rsavg", g, mybg - 2)))
                                M[("expg", g, mybg)] = s_ac.mark(a.activation(
                                    PT[:, half:half + nb * 256],
                                    pA[:, 0:nb * 256],
                                    AF.Exp, scale=SCL, bias=bias4[:, i:i + 1]))

                            if "pt0" in tapt and h == 0 and p == 1 and bi == 0:
                                @blk.sync
                                def _(sync):
                                    wait(sync, M[("expg", 0, 2)])
                                    mm = s_st.mark(sync.dma_start(
                                        out=tapt["pt0"], in_=PT[:, 0:2048]), 16)
                                    wait(sync, mm)

                            def rsav_cl(h=h, g=g, p=p, bi=bi, batch=batch,
                                        kidx=kidx, nkb=nkb, half=half,
                                        mybg=mybg, pg=pg, av0=av0, rs0=rs0):
                                @blk.tensor
                                def _(t):
                                    wait(t, M[("expg", g, mybg)])
                                    if bi == 0:
                                        wait(t, M.get(("catg", g, pg - 2)))
                                    last = None
                                    for idx, kb in enumerate(batch):
                                        kg = kidx + idx
                                        t.matmul(pB[0:1, rs0:rs0 + 256],
                                                 ones_c[:, :],
                                                 PT[:, half + idx * 256:
                                                    half + (idx + 1) * 256],
                                                 start=(kg == 0),
                                                 stop=(kg == nkb - 1))
                                        u0 = kb * 512 + (h - 4 * g) * 128
                                        last = t.matmul(
                                            pB[:, av0:av0 + 256],
                                            Vg[:, u0:u0 + 128],
                                            PT[:, half + idx * 256:
                                               half + (idx + 1) * 256],
                                            start=(kg == 0), stop=(kg == nkb - 1))
                                    M[("rsavg", g, mybg)] = s_pe.mark(last)

                            rsav_pend.append(rsav_cl)
                            if len(rsav_pend) > 1:
                                rsav_pend.pop(0)()

                            kidx += len(batch)
                            bg += 1
                            if bi == 0:
                                flush_pend()

                        last_bg = bg - 1

                        def tail(h=h, g=g, p=p, pg=pg, av0=av0, bc0=bc0,
                                 rs0=rs0, last_bg=last_bg):
                            @blk.vector
                            def _(v):
                                wait(v, M[("rsavg", g, last_bg)])
                                fence(v, s_dv, v.tensor_copy(rrow[0:1, :],
                                                             pB[0:1, rs0:rs0 + 256]))
                                fence(v, s_dv, v.reciprocal(rrow[0:1, :],
                                                            rrow[0:1, :]))
                                M[("rrbg", g, pg)] = s_dv.mark(v.tensor_copy(
                                    rrb[0:1, :], rrow[0:1, :]))

                            @blk.tensor
                            def _(t):
                                wait(t, M[("rrbg", g, pg)])
                                M[("bcastg", g, pg)] = s_pe.mark(t.matmul(
                                    pB[:, bc0:bc0 + 256], ones_r[:, :],
                                    rrb[0:1, :], start=True, stop=True))

                            @blk.vector
                            def _(v):
                                wait(v, M[("bcastg", g, pg)])
                                fence(v, s_dv, v.tensor_copy(rbc[:, :],
                                                             pB[:, bc0:bc0 + 256]))
                                M[("catg", g, pg)] = s_dv.mark(v.tensor_tensor(
                                    catT[:, h * TM + p * 256:
                                         h * TM + (p + 1) * 256],
                                    pB[:, av0:av0 + 256], rbc[:, :], ALU.mult))
                                if p == 3:
                                    M[("cat", h, 3)] = M[("catg", g, pg)]

                        flush_rsav()
                        pend.append(tail)

                        for _ in range(2):
                            @blk.sync
                            def _(sync):
                                gamma_dma(sync, 2)

                            @blk.vector
                            def _(v):
                                gamma_reduce_dve(v, 2)

                flush_pend()

                if g < 3:
                    emit_wv_load(blk, g + 1, M[("vmm", g, 15)])

        for name, key in [("k0", ("kev", 0)), ("q0", ("qev", 0))]:
            if name in tapt:
                src = Kh[:, 0:T] if name == "k0" else Qh[:, 0:TM]
                with nc.Block() as blk:
                    @blk.sync
                    def _(sync, name=name, key=key, src=src):
                        wait(sync, M[key])
                        mm = s_st.mark(sync.dma_start(out=tapt[name], in_=src), 16)
                        wait(sync, mm)
        if "v0" in tapt:
            with nc.Block() as blk:
                @blk.sync
                def _(sync):
                    mm = s_st.mark(sync.dma_start(out=tapt["v0"], in_=Vg[:, :]), 16)
                    wait(sync, mm)
        if "cat" in tapt:
            with nc.Block() as blk:
                @blk.sync
                def _(sync):
                    wait(sync, M[("cat", 15, 3)])
                    mm = s_st.mark(sync.dma_start(out=tapt["cat"], in_=catT[:, :]),
                                   16)
                    wait(sync, mm)

        for cm, _h in reversed(att_cms):
            cm.__exit__(None, None, None)
        xnT_r_cm.__exit__(None, None, None)

        op_cms = [manual("woTb", [128, NDC * D], BF16),
                  manual("sstage", [128, 2 * D], F32)]
        woTb, sstage = [h for _, h in op_cms]

        with nc.Block() as blk:
            @blk.gpsimd
            def _(gq):
                for c in range(NDC):
                    M[("wo", c)] = s_cast.mark(gq.dma_start(
                        out=woTb[:, c * D:(c + 1) * D], in_=wo[c]), 16)
            for _ in range(12):
                @blk.sync
                def _(sync):
                    gamma_dma(sync, 2)

                @blk.vector
                def _(v):
                    gamma_reduce_dve(v, 2)

            for tt in range(8):
                pg = pA if tt % 2 == 0 else pB

                @blk.tensor
                def _(t, tt=tt, pg=pg):
                    if tt == 0:
                        wait(t, M[("wo", NDC - 1)])
                    wait(t, M.get(("sev", tt - 2)))
                    last = None
                    for c in range(NDC):
                        lh = catT[:, c * TM + tt * 128: c * TM + (tt + 1) * 128]
                        for oc in range(4):
                            last = t.matmul(
                                pg[:, oc * 512:(oc + 1) * 512], lh,
                                woTb[:, c * D + oc * 512: c * D + (oc + 1) * 512],
                                start=(c == 0), stop=(c == NDC - 1))
                    M[("smm", tt)] = s_pe.mark(last)

                @blk.vector
                def _(v, tt=tt, pg=pg):
                    wait(v, M[("smm", tt)])
                    wait(v, M.get(("sst", tt - 2)))
                    sl = (tt % 2) * D
                    M[("sev", tt)] = s_dv.mark(v.tensor_copy(
                        sstage[:, sl:sl + D], pg[:, :]))

                @blk.sync
                def _(sync, tt=tt):
                    wait(sync, M[("sev", tt)])
                    sl = (tt % 2) * D
                    M[("sst", tt)] = s_st.mark(sync.dma_start(
                        out=s_dram[tt * 128:(tt + 1) * 128, :],
                        in_=sstage[:, sl:sl + D]), 16)

            M["s_all"] = s_st.mile()

        if "sm" in tapt:
            with nc.Block() as blk:
                @blk.sync
                def _(sync):
                    sync.wait_ge(s_st.h, s_st.n)
                    mm = s_st.mark(sync.dma_start(out=tapt["sm"], in_=s_dram), 16)
                    wait(sync, mm)

        for cm, _h in reversed(op_cms):
            cm.__exit__(None, None, None)
        catT_cm.__exit__(None, None, None)

        with nc.Block() as blk:
            for _ in range(140):
                if gst["dma"] >= len(G_ORDER) and gst["act"] >= len(G_ACT):
                    break

                @blk.sync
                def _(sync):
                    gamma_dma(sync, 2)

                @blk.vector
                def _(v):
                    gamma_reduce_dve(v, 2)

            @blk.tensor
            def _(t):
                t.wait_ge(s_dv.h, s_dv.n)
                last = None
                for j in range(3):
                    last = t.matmul(pA[0:1, j * 512:j * 512 + 1], ones_cf[:, :],
                                    gcols[:, j:j + 1], start=True, stop=True)
                M[("gsum")] = s_pe.mark(last)

            @blk.vector
            def _(v):
                wait(v, M["gsum"])
                mm = None
                for j in range(3):
                    fence(v, s_dv, v.tensor_copy(tiny[0:1, j:j + 1],
                                                 pA[0:1, j * 512:j * 512 + 1]))
                    fence(v, s_dv, v.tensor_scalar(
                        tiny[0:1, j:j + 1], tiny[0:1, j:j + 1],
                        1.0 / (HID * D), EPS_Q, ALU.mult, ALU.add))
                    mm = fence(v, s_dv, v.reciprocal(tiny[0:1, j:j + 1],
                                                     tiny[0:1, j:j + 1]))
                M["ginv"] = mm

            @blk.tensor
            def _(t):
                wait(t, M["ginv"])
                last = None
                for j in range(3):
                    last = t.matmul(pB[:, j * 512:j * 512 + 1], ones_rf[:, :],
                                    tiny[0:1, j:j + 1], start=True, stop=True)
                M["gbc"] = s_pe.mark(last)

            @blk.vector
            def _(v):
                wait(v, M["gbc"])
                last = None
                for j in range(3):
                    last = v.tensor_copy(scol[:, j:j + 1], pB[:, j * 512:j * 512 + 1])
                M["scol"] = s_dv.mark(last)

        if "gam" in tapt:
            with nc.Block() as blk:
                @blk.sync
                def _(sync):
                    wait(sync, M["scol"])
                    mm = s_st.mark(sync.dma_start(out=tapt["gam"], in_=gcols[:, :]),
                                   16)
                    wait(sync, mm)

        grs2_cm.__exit__(None, None, None)
        grs_cm.__exit__(None, None, None)

        hid_cm, hidT = manual("hidT", [128, NHC * TM], BF16)
        ffn_cms = []
        for nm, shp, dt in [("w1f", [128, 2 * D], F32), ("w2f", [128, 2 * D], F32),
                            ("w1qb", [128, 2 * D], BF16), ("w2qb", [128, 2 * D], BF16),
                            ("w3f", [128, D], F32), ("w3qb", [128, D], BF16),
                            ("qi", [128, 2 * D], F32),
                            ("sigb", [128, TM], F32)]:
            ffn_cms.append(manual(nm, shp, dt))
        (w1f, w2f, w1qb, w2qb, w3f, w3qb, qi, sigb) = \
            [h for _, h in ffn_cms]

        def qi_slot(k):
            return qi[:, (k % 2) * D:(k % 2 + 1) * D]

        with nc.Block() as blk:
            ffn_pend = []

            def ffn_flush():
                while ffn_pend:
                    ffn_pend.pop(0)()

            for hc in range(NHC):
                sl = (hc % 2)

                @blk.sync
                def _(sync, hc=hc, sl=sl):
                    wait(sync, M.get(("q1a", hc - 1)))
                    M[("w1l", hc)] = s_ld.mark(sync.dma_start(
                        out=w1f[:, sl * D:(sl + 1) * D], in_=w1t[hc]), 16)
                    wait(sync, M.get(("q2a", hc - 1)))
                    M[("w2l", hc)] = s_ld.mark(sync.dma_start(
                        out=w2f[:, sl * D:(sl + 1) * D], in_=w2t[hc]), 16)

                @blk.gpsimd
                def _(gq, hc=hc):
                    wait(gq, M.get(("q3a", hc - 1)))
                    M[("w3l", hc)] = s_gl.mark(gq.dma_start(
                        out=w3f[:, :], in_=w3r[hc * 128:(hc + 1) * 128, :]), 16)

                @blk.scalar
                def _(a, hc=hc):
                    wait(a, M[("w3l", hc)])
                    if hc == 0:
                        wait(a, M["scol"])
                    wait(a, M.get(("dvq", 3 * hc + 1 - 2)))
                    M[("q3a", hc)] = s_ac.mark(a.mul(
                        qi_slot(3 * hc + 1), w3f[:, :], scol[:, 2:3]))

                @blk.vector
                def _(v, hc=hc):
                    wait(v, M[("q3a", hc)])
                    wait(v, M.get(("w3st", hc - 1)))
                    fence(v, s_dv, v.tensor_scalar(
                        qi_slot(3 * hc + 1), qi_slot(3 * hc + 1),
                        QMAGIC, -QMAGIC, ALU.add, ALU.add))
                    M[("w3q", hc)] = M[("dvq", 3 * hc + 1)] = fence(
                        v, s_dv, v.tensor_scalar(
                            w3qb[:, :], qi_slot(3 * hc + 1),
                            1.0, -1.0, ALU.min, ALU.max))

                @blk.gpsimd
                def _(gq, hc=hc):
                    wait(gq, M[("w3q", hc)])
                    M[("w3st", hc)] = s_gs.mark(gq.dma_start(
                        out=w3q_dram[hc * 128:(hc + 1) * 128, :],
                        in_=w3qb[:, :]), 16)

                @blk.scalar
                def _(a, hc=hc, sl=sl):
                    wait(a, M[("w1l", hc)])
                    if hc == 0:
                        wait(a, M["scol"])
                    wait(a, M.get(("dvq", 3 * hc + 0 - 2)))
                    M[("q1a", hc)] = s_ac.mark(a.mul(
                        qi_slot(3 * hc + 0),
                        w1f[:, sl * D:(sl + 1) * D], scol[:, 0:1]))

                @blk.vector
                def _(v, hc=hc, sl=sl):
                    wait(v, M[("q1a", hc)])
                    wait(v, M.get(("x12", hc - 2)))
                    fence(v, s_dv, v.tensor_scalar(
                        qi_slot(3 * hc + 0), qi_slot(3 * hc + 0),
                        QMAGIC, -QMAGIC, ALU.add, ALU.add))
                    M[("w1q", hc)] = M[("dvq", 3 * hc + 0)] = fence(
                        v, s_dv, v.tensor_scalar(
                            w1qb[:, sl * D:(sl + 1) * D], qi_slot(3 * hc + 0),
                            1.0, -1.0, ALU.min, ALU.max))

                @blk.scalar
                def _(a, hc=hc, sl=sl):
                    wait(a, M[("w2l", hc)])
                    if hc == 0:
                        wait(a, M["scol"])
                    wait(a, M.get(("dvq", 3 * hc + 2 - 2)))
                    M[("q2a", hc)] = s_ac.mark(a.mul(
                        qi_slot(3 * hc + 2),
                        w2f[:, sl * D:(sl + 1) * D], scol[:, 1:2]))

                @blk.vector
                def _(v, hc=hc, sl=sl):
                    wait(v, M[("q2a", hc)])
                    wait(v, M.get(("x12", hc - 2)))
                    fence(v, s_dv, v.tensor_scalar(
                        qi_slot(3 * hc + 2), qi_slot(3 * hc + 2),
                        QMAGIC, -QMAGIC, ALU.add, ALU.add))
                    M[("w12q", hc)] = M[("dvq", 3 * hc + 2)] = fence(
                        v, s_dv, v.tensor_scalar(
                            w2qb[:, sl * D:(sl + 1) * D], qi_slot(3 * hc + 2),
                            1.0, -1.0, ALU.min, ALU.max))

                pg = pA if hc % 2 == 0 else pB

                @blk.tensor
                def _(t, hc=hc, sl=sl, pg=pg):
                    wait(t, M[("w1q", hc)])
                    wait(t, M[("w12q", hc)])
                    wait(t, M.get(("hid", hc - 2)))
                    last = None
                    for th in range(2):
                        for c in range(NDC):
                            t.matmul(pg[:, th * 512:(th + 1) * 512],
                                     w1qb[:, sl * D + c * 128: sl * D + (c + 1) * 128],
                                     xnT_m[:, c * TM + th * 512: c * TM + (th + 1) * 512],
                                     start=(c == 0), stop=(c == NDC - 1))
                    for th in range(2):
                        for c in range(NDC):
                            last = t.matmul(
                                pg[:, 1024 + th * 512: 1024 + (th + 1) * 512],
                                w2qb[:, sl * D + c * 128: sl * D + (c + 1) * 128],
                                xnT_m[:, c * TM + th * 512: c * TM + (th + 1) * 512],
                                start=(c == 0), stop=(c == NDC - 1))
                    M[("x12", hc)] = s_pe.mark(last)

                def sighid(hc=hc, pg=pg):
                    @blk.scalar
                    def _(a, hc=hc, pg=pg):
                        wait(a, M[("x12", hc)])
                        wait(a, M.get(("hid", hc - 1)))
                        M[("sig", hc)] = s_ac.mark(a.activation(
                            sigb[:, :], pg[:, 0:1024], AF.Sigmoid))

                    @blk.vector
                    def _(v, hc=hc, pg=pg):
                        wait(v, M[("sig", hc)])
                        M[("hid", hc)] = s_dv.mark(v.tensor_tensor(
                            hidT[:, hc * TM:(hc + 1) * TM], pg[:, 1024:2048],
                            sigb[:, :], ALU.mult))

                ffn_pend.append(sighid)
                if hc >= 1:
                    ffn_flush() if False else None
                if len(ffn_pend) > 1:
                    ffn_pend.pop(0)()
            ffn_flush()

        if "hid" in tapt:
            with nc.Block() as blk:
                @blk.sync
                def _(sync):
                    wait(sync, M[("hid", NHC - 1)])
                    mm = s_st.mark(sync.dma_start(out=tapt["hid"], in_=hidT[:, :]),
                                   16)
                    wait(sync, mm)
        if "w3q" in tapt:
            with nc.Block() as blk:
                @blk.sync
                def _(sync):
                    sync.wait_ge(s_gs.h, s_gs.n)
                    mm = s_st.mark(sync.dma_start(out=tapt["w3q"], in_=w3q_dram),
                                   16)
                    wait(sync, mm)

        for cm, _h in reversed(ffn_cms):
            cm.__exit__(None, None, None)

        HHC = NHC // 2
        f2_cms = [manual("w3qd", [128, 3 * HHC * 512], BF16),
                  manual("gt", [128, 2 * 512], F32),
                  manual("stile", [128, 4 * 512], F32),
                  manual("xtile", [128, 3 * 512], F32)]
        w3qd, gt, stile, xtile = [h for _, h in f2_cms]

        w3l2_base = s_gl.n
        for j in range(8):
            M[("w3l2", j)] = (s_gl, w3l2_base + 16 * (j + 1))

        def emit_sxl(sync, g):
            dg, tt = g // 8, g % 8
            M[("sxl", dg, tt)] = s_ld.mark(sync.dma_start(
                out=stile[:, (g % 4) * 512:(g % 4 + 1) * 512],
                in_=s_dram[tt * 128:(tt + 1) * 128,
                           dg * 512:(dg + 1) * 512]), 16)

        def emit_sxl2(sync, g):
            dg, tt = g // 8, g % 8
            M[("sxl2", dg, tt)] = s_ld.mark(sync.dma_start(
                out=xtile[:, (g % 3) * 512:(g % 3 + 1) * 512],
                in_=xp[tt * 128:(tt + 1) * 128,
                       dg * 512:(dg + 1) * 512]), 16)

        def emit_half_load(gq, j):
            dg, hh = j // 2, j % 2
            if j == 0:
                wait(gq, M[("w3st", HHC - 1)])
            elif j == 1:
                wait(gq, M[("w3st", NHC - 1)])
            elif j >= 3:
                wait(gq, M[("ffh", j - 3)])
            mm = s_gl.mark(gq.dma_start(
                out=w3qd[:, (j % 3) * HHC * 512:(j % 3 + 1) * HHC * 512],
                in_=bass.AP(w3q_h, hh * HHC * 128 * D + dg * 512,
                            [[D, 128], [128 * D, HHC], [1, 512]])), 16)
            assert mm == M[("w3l2", j)]

        with nc.Block() as blk:
            @blk.gpsimd
            def _(gq):
                for j in range(3):
                    emit_half_load(gq, j)

            @blk.sync
            def _(sync):
                sync.wait_ge(s_st.h, M["s_all"][1])
                for g in range(4):
                    emit_sxl(sync, g)
                for g in range(3):
                    emit_sxl2(sync, g)

            for dg in range(4):
                for tt in range(8):
                    @blk.tensor
                    def _(t, dg=dg, tt=tt):
                        if tt == 0:
                            wait(t, M[("w3l2", 2 * dg)])
                        wait(t, M.get(("silu", dg - 1, tt)))
                        pg = pA if tt < 4 else pB
                        col = (tt % 4) * 512
                        last = None
                        for hc in range(NHC):
                            if tt == 0 and hc == HHC:
                                wait(t, M[("w3l2", 2 * dg + 1)])
                            slot = (2 * dg + hc // HHC) % 3
                            last = t.matmul(
                                pg[:, col:col + 512],
                                hidT[:, hc * TM + tt * 128: hc * TM + (tt + 1) * 128],
                                w3qd[:, slot * HHC * 512 + (hc % HHC) * 512:
                                     slot * HHC * 512 + (hc % HHC + 1) * 512],
                                start=(hc == 0), stop=(hc == NHC - 1))
                            if tt == 7 and hc == HHC - 1:
                                M[("ffh", 2 * dg)] = s_pe.mark(last)
                        M[("fft", dg, tt)] = s_pe.mark(last)
                        if tt == 7:
                            M[("ffh", 2 * dg + 1)] = M[("fft", dg, tt)]

                if dg < 3:
                    @blk.gpsimd
                    def _(gq, dg=dg):
                        for j in (2 * dg + 3, 2 * dg + 4):
                            if j < 8:
                                emit_half_load(gq, j)

                for tt in range(8):
                    pg = pA if tt < 4 else pB
                    col = (tt % 4) * 512

                    @blk.scalar
                    def _(a, dg=dg, tt=tt, pg=pg, col=col):
                        wait(a, M[("fft", dg, tt)])
                        wait(a, M.get(("comb", dg, tt - 2)))
                        M[("silu", dg, tt)] = s_ac.mark(a.activation(
                            gt[:, (tt % 2) * 512:(tt % 2 + 1) * 512],
                            pg[:, col:col + 512], AF.Silu))

                    @blk.vector
                    def _(v, dg=dg, tt=tt):
                        wait(v, M[("silu", dg, tt)])
                        wait(v, M[("sxl", dg, tt)])
                        wait(v, M[("sxl2", dg, tt)])
                        g = 8 * dg + tt
                        o = stile[:, (g % 4) * 512:(g % 4 + 1) * 512]
                        fence(v, s_dv, v.tensor_tensor(
                            o, gt[:, (tt % 2) * 512:(tt % 2 + 1) * 512],
                            o, ALU.mult))
                        M[("comb", dg, tt)] = s_dv.mark(v.tensor_tensor(
                            o, o, xtile[:, (g % 3) * 512:(g % 3 + 1) * 512],
                            ALU.add))

                    @blk.sync
                    def _(sync, dg=dg, tt=tt):
                        g = 8 * dg + tt
                        wait(sync, M[("comb", dg, tt)])
                        M[("ost", dg, tt)] = s_st.mark(sync.dma_start(
                            out=out[tt * 128:(tt + 1) * 128,
                                    dg * 512:(dg + 1) * 512],
                            in_=stile[:, (g % 4) * 512:(g % 4 + 1) * 512]), 16)
                        if g + 3 < 32:
                            emit_sxl2(sync, g + 3)
                        if g + 4 < 32:
                            wait(sync, M[("ost", dg, tt)])
                            emit_sxl(sync, g + 4)

        with nc.Block() as blk:
            @blk.sync
            def _(sync):
                sync.wait_ge(s_st.h, s_st.n)

            @blk.gpsimd
            def _(gq):
                gq.wait_ge(s_gs.h, s_gs.n)

        for cm, _h in reversed(f2_cms):
            cm.__exit__(None, None, None)
        hid_cm.__exit__(None, None, None)
        xnT_m_cm.__exit__(None, None, None)

    return nc



def _prep_weights(wp, wout, w1, w2, w3):
    wp = np.asarray(wp, np.float32)
    wq = wp[0:D].reshape(H, 128, NDC, 128)
    wk = wp[D:2 * D].reshape(H, 128, NDC, 128)
    wqk = np.stack([wq.transpose(0, 3, 2, 1), wk.transpose(0, 3, 2, 1)],
                   axis=3)
    wqk = np.ascontiguousarray(wqk.reshape(H, 128, NDC * 256))
    wvv = wp[2 * D:3 * D].reshape(4, 512, NDC, 128)
    wvv = np.ascontiguousarray(wvv.transpose(0, 3, 2, 1).reshape(4, 128, NDC * 512))
    wo_t = np.ascontiguousarray(
        np.asarray(wout, np.float32).T.reshape(NDC, 128, D))
    w1t = np.ascontiguousarray(
        np.asarray(w1, np.float32).reshape(NHC, 128, NDC, 128)
        .transpose(0, 3, 2, 1).reshape(NHC, 128, D))
    w2t = np.ascontiguousarray(
        np.asarray(w2, np.float32).reshape(NHC, 128, NDC, 128)
        .transpose(0, 3, 2, 1).reshape(NHC, 128, D))
    w3r = np.ascontiguousarray(np.asarray(w3, np.float32).T)
    return wqk, wvv, wo_t, w1t, w2t, w3r


def _masks(par):
    ii = np.arange(128)[:, None]
    jj = np.arange(128)[None, :]
    md = (ii <= jj).astype(ml_dtypes.bfloat16)
    ma = np.ones((128, 256), ml_dtypes.bfloat16)
    mb = np.zeros((128, 256), ml_dtypes.bfloat16)
    if par == 0:
        ma[:, 0:128] = 0
    else:
        mb[:, 128:256] = 1
    return md, ma, mb


NEG = np.float32(-1e9)


def _bigmask(par):
    dg = np.where(np.arange(128)[:, None] <= np.arange(128)[None, :],
                  np.float32(0), NEG)
    full = np.full((128, 128), NEG, np.float32)
    zero = np.zeros((128, 128), np.float32)
    eb = np.concatenate([full, dg], 1)
    if par == 0:
        ab = np.concatenate([full, zero], 1)
        bb = np.concatenate([full, full], 1)
    else:
        ab = np.concatenate([zero, zero], 1)
        bb = np.concatenate([full, zero], 1)
    bigm = np.concatenate([dg, eb, ab, bb], 1)
    return np.ascontiguousarray(bigm.astype(ml_dtypes.bfloat16))


def _perm_rows(par):
    mine = [par + 2 * j for j in range(8)]
    rest = [(1 - par) + 2 * j for j in range(8)]
    idx = []
    for gg in mine + rest:
        idx.extend(range(gg * 128, (gg + 1) * 128))
    return np.array(idx), mine


def kernel(L0_attn_x, L0_attn_w_project, L0_attn_w_out,
           L0_ffn_w1, L0_ffn_w2, L0_ffn_w3, _taps=(), _nc=None):
    x = np.asarray(L0_attn_x, np.float32)
    wqk, wvv, wo_t, w1t, w2t, w3r = _prep_weights(
        L0_attn_w_project, L0_attn_w_out, L0_ffn_w1, L0_ffn_w2, L0_ffn_w3)

    nc = _nc if _nc is not None else build_nc(_taps)
    in_maps = []
    metas = []
    for c in range(8):
        b, par = c // 2, c % 2
        idx, mine = _perm_rows(par)
        md, ma, mb = _masks(par)
        in_maps.append({
            "xp": np.ascontiguousarray(x[b][idx]),
            "wqk": wqk, "wv": wvv, "wo": wo_t,
            "w1t": w1t, "w2t": w2t, "w3r": w3r,
            "mask_d": md, "mask_a": ma, "mask_b": mb,
            "identd": np.ascontiguousarray(np.eye(128, dtype=ml_dtypes.bfloat16)),
            "bigmd": _bigmask(par),
        })
        metas.append((b, mine))

    res = run_bass_kernel_spmd(nc, in_maps, core_ids=list(range(8)))

    final = np.empty((4, T, D), np.float32)
    for c in range(8):
        b, mine = metas[c]
        o = res.results[c]["out"]
        for j, gg in enumerate(mine):
            final[b, gg * 128:(gg + 1) * 128, :] = o[j * 128:(j + 1) * 128, :]
    if _taps:
        return final, list(res.results)
    return final


# version 34
# speedup vs baseline: 1.7127x; 1.0036x over previous
import math
from contextlib import ExitStack

import numpy as np
import ml_dtypes
import sys

sys.path.insert(0, "/opt/trn_rl_repo")
import concourse.bass as bass
import concourse.mybir as mybir
from concourse.bass_utils import run_bass_kernel_spmd

F32 = mybir.dt.float32
BF16 = mybir.dt.bfloat16
AF = mybir.ActivationFunctionType
ALU = mybir.AluOpType
AX = mybir.AxisListType

D = 2048
T = 2048
TM = 1024
H = 16
HID = 5632
NHC = HID // 128
NDC = D // 128
EPS_LN = 1e-5
EPS_Q = 1e-8
SCL = 1.0 / math.sqrt(128.0)
QMAGIC = float(3 * 2 ** 22)


class Sem:
    def __init__(self, ctx, nc, name):
        self.h = ctx.enter_context(nc.semaphore(name))
        self.n = 0

    def mark(self, inst, k=1):
        inst.then_inc(self.h, k)
        self.n += k
        return (self, self.n)

    def mile(self):
        return (self, self.n)


def wait(eng, mile):
    if mile is not None:
        eng.wait_ge(mile[0].h, mile[1])


def fence(eng, sem, inst):
    mm = sem.mark(inst)
    wait(eng, mm)
    return mm


def build_nc(taps=()):
    taps = set(taps)
    nc = bass.Bass("TRN2", target_bir_lowering=False, debug=False)

    xp = nc.dram_tensor("xp", [T, D], F32, kind="ExternalInput").ap()
    wqk = nc.dram_tensor("wqk", [H, 128, NDC * 256], F32, kind="ExternalInput").ap()
    wv = nc.dram_tensor("wv", [4, 128, NDC * 512], F32, kind="ExternalInput").ap()
    wo = nc.dram_tensor("wo", [NDC, 128, D], F32, kind="ExternalInput").ap()
    w1t = nc.dram_tensor("w1t", [NHC, 128, D], F32, kind="ExternalInput").ap()
    w2t = nc.dram_tensor("w2t", [NHC, 128, D], F32, kind="ExternalInput").ap()
    w3r = nc.dram_tensor("w3r", [HID, D], F32, kind="ExternalInput").ap()
    mask_d = nc.dram_tensor("mask_d", [128, 128], BF16, kind="ExternalInput").ap()
    mask_a = nc.dram_tensor("mask_a", [128, 256], BF16, kind="ExternalInput").ap()
    mask_b = nc.dram_tensor("mask_b", [128, 256], BF16, kind="ExternalInput").ap()
    identd = nc.dram_tensor("identd", [128, 128], BF16, kind="ExternalInput").ap()
    bigmd = nc.dram_tensor("bigmd", [128, 896], BF16, kind="ExternalInput").ap()

    out = nc.dram_tensor("out", [TM, D], F32, kind="ExternalOutput").ap()

    xn_dram = nc.dram_tensor("xn_dram", [T, D], BF16).ap()
    s_dram = nc.dram_tensor("s_dram", [TM, D], F32).ap()
    w3q_h = nc.dram_tensor("w3q_dram", [HID, D], BF16)
    w3q_dram = w3q_h.ap()

    tapt = {}
    for name, shape, dt in [
        ("xn", [T, D], BF16), ("k0", [128, T], BF16), ("q0", [128, TM], BF16),
        ("v0", [128, 16 * 512], BF16), ("cat", [128, H * TM], BF16),
        ("sm", [TM, D], F32), ("gam", [128, 16], F32),
        ("hid", [128, NHC * TM], BF16), ("w3q", [HID, D], BF16),
        ("bias0", [128, 1], F32), ("pt0", [128, 2048], BF16),
    ]:
        if name in taps:
            tapt[name] = nc.dram_tensor("tap_" + name, shape, dt,
                                        kind="ExternalOutput").ap()

    M = {}

    with ExitStack() as ctx:
        def sb(name, shape, dt):
            return ctx.enter_context(nc.sbuf_tensor(name, shape, dt))

        def manual(name, shape, dt, side="right"):
            cm = nc.sbuf_tensor(name, shape, dt, side=side)
            return cm, cm.__enter__()

        ones_c = sb("ones_c", [128, 1], BF16)
        ones_r = sb("ones_r", [1, 128], BF16)
        ones_cf = sb("ones_cf", [128, 1], F32)
        ones_rf = sb("ones_rf", [1, 128], F32)
        identb = sb("identb", [128, 128], BF16)
        bigm = sb("bigm", [128, 896], BF16)
        gcols = sb("gcols", [128, 16], F32)
        scol = sb("scol", [128, 4], F32)
        bias4 = sb("bias4", [128, 4], F32)
        tiny = sb("tiny", [1, 16], F32)
        tinyb = sb("tinyb", [1, 4], BF16)
        epsc = sb("epsc", [128, 1], F32)

        pA = ctx.enter_context(nc.psum_tensor("pA", [128, 2048], F32))
        pB = ctx.enter_context(nc.psum_tensor("pB", [128, 2048], F32))

        s_ld = Sem(ctx, nc, "s_ld")
        s_gl = Sem(ctx, nc, "s_gl")
        s_st = Sem(ctx, nc, "s_st")
        s_gs = Sem(ctx, nc, "s_gs")
        s_pe = Sem(ctx, nc, "s_pe")
        s_dv = Sem(ctx, nc, "s_dv")
        s_ac = Sem(ctx, nc, "s_ac")
        s_pl = Sem(ctx, nc, "s_pl")
        s_gq = Sem(ctx, nc, "s_gq")
        s_cast = Sem(ctx, nc, "s_cast")

        xnT_m_cm, xnT_m = manual("xnT_m", [128, NDC * TM], BF16, side="left")
        grs_cm, grs = manual("grs", [128, 1024], F32, side="left")
        grs2_cm, grs2 = manual("grs2", [128, 1024], F32, side="left")
        catT_cm, catT = manual("catT", [128, H * TM], BF16, side="left")
        xnT_r_cm, xnT_r = manual("xnT_r", [128, NDC * TM], BF16, side="left")

        def xnT_chunk(c, tt):
            if tt < 8:
                return xnT_m[:, c * TM + tt * 128: c * TM + (tt + 1) * 128]
            return xnT_r[:, c * TM + (tt - 8) * 128: c * TM + (tt - 7) * 128]

        def xnT_half(c, tc):
            if tc < 2:
                return xnT_m[:, c * TM + tc * 512: c * TM + (tc + 1) * 512]
            return xnT_r[:, c * TM + (tc - 2) * 512: c * TM + (tc - 1) * 512]

        G_ORDER = []
        for j in range(NHC):
            for hh in range(2):
                G_ORDER.append(("w1", j, hh))
                G_ORDER.append(("w3", j, hh))
        for j in range(NHC):
            for hh in range(2):
                G_ORDER.append(("w2", j, hh))
        G_ACT = list(range(len(G_ORDER)))
        gst = {"dma": 0, "act": 0}

        def gamma_slab_in(i):
            m, j, hh = G_ORDER[i]
            if m == "w1":
                return w1t[j][:, hh * 1024:(hh + 1) * 1024]
            if m == "w2":
                return w2t[j][:, hh * 1024:(hh + 1) * 1024]
            return w3r[j * 128:(j + 1) * 128, hh * 1024:(hh + 1) * 1024]

        def gamma_dma(g, count):
            for _ in range(count):
                i = gst["dma"]
                if i >= len(G_ORDER):
                    return
                if i >= 2 and ("gfree", i - 2) not in M:
                    return
                buf = grs if i % 2 == 0 else grs2
                wait(g, M.get(("gfree", i - 2)))
                M[("gdma", i)] = s_gq.mark(
                    g.dma_start(out=buf[:, :], in_=gamma_slab_in(i)), 16)
                gst["dma"] += 1

        def gamma_reduce_dve(v, count):
            for _ in range(count):
                k = gst["act"]
                if k >= len(G_ACT):
                    return
                i = G_ACT[k]
                if ("gdma", i) not in M:
                    return
                buf = grs if i % 2 == 0 else grs2
                col = {"w1": 0, "w2": 1, "w3": 2}[G_ORDER[i][0]]
                wait(v, M[("gdma", i)])
                fence(v, s_dv, v.tensor_reduce(
                    gcols[:, 9:10], buf[:, :], AX.X, ALU.add,
                    apply_absolute_value=True))
                M[("gfree", i)] = fence(v, s_dv, v.tensor_tensor(
                    gcols[:, col:col + 1], gcols[:, col:col + 1],
                    gcols[:, 9:10], ALU.add))
                gst["act"] += 1

        with nc.Block() as blk:
            @blk.sync
            def _(sync):
                s_ld.mark(sync.dma_start(out=identb[:, :], in_=identd), 16)
                M["consts"] = s_ld.mark(sync.dma_start(out=bigm[:, :], in_=bigmd), 16)

            @blk.vector
            def _(v):
                v.memset(ones_c[:, :], 1.0)
                v.memset(ones_r[:, :], 1.0)
                v.memset(ones_cf[:, :], 1.0)
                v.memset(ones_rf[:, :], 1.0)
                v.memset(gcols[:, :], 0.0)
                v.memset(tiny[:, :], 0.0)
                v.memset(epsc[:, :], EPS_LN)

        ln_cms = []
        for nm, shp, dt in [("xbuf", [128, 6 * D], F32), ("xnb", [128, 6 * D], BF16),
                            ("lnsq", [128, D], F32), ("stat", [128, 8], F32)]:
            ln_cms.append(manual(nm, shp, dt))
        xbuf, xnb, lnsq, stat = [h for _, h in ln_cms]

        with nc.Block() as blk:
            @blk.sync
            def _(sync):
                for tt in range(6):
                    M[("lx", tt)] = s_ld.mark(sync.dma_start(
                        out=xbuf[:, tt * D:(tt + 1) * D],
                        in_=xp[tt * 128:(tt + 1) * 128, :]), 16)

            for tt in range(16):
                sl = (tt % 6) * D

                @blk.vector
                def _(v, tt=tt, sl=sl):
                    wait(v, M[("lx", tt)])
                    wait(v, M.get(("lnapp", tt - 1)))
                    fence(v, s_dv, v.tensor_reduce(
                        stat[:, 0:1], xbuf[:, sl:sl + D], AX.X, ALU.add))
                    fence(v, s_dv, v.tensor_scalar(
                        stat[:, 2:3], stat[:, 0:1], 1.0 / D, None, ALU.mult))
                    fence(v, s_dv, v.tensor_tensor(
                        lnsq[:, :], xbuf[:, sl:sl + D], xbuf[:, sl:sl + D],
                        ALU.mult))
                    fence(v, s_dv, v.tensor_reduce(
                        stat[:, 1:2], lnsq[:, :], AX.X, ALU.add))
                    fence(v, s_dv, v.tensor_tensor(
                        stat[:, 5:6], stat[:, 2:3], stat[:, 2:3], ALU.mult))
                    fence(v, s_dv, v.tensor_scalar(
                        stat[:, 3:4], stat[:, 1:2], 1.0 / D, None, ALU.mult))
                    M[("lnvar", tt)] = s_dv.mark(v.tensor_tensor(
                        stat[:, 3:4], stat[:, 3:4], stat[:, 5:6], ALU.subtract))

                @blk.scalar
                def _(a, tt=tt):
                    wait(a, M[("lnvar", tt)])
                    fence(a, s_ac, a.activation(stat[:, 6:7], stat[:, 3:4],
                                                AF.Ln, bias=epsc[:, 0:1]))
                    M[("lnrstd", tt)] = s_ac.mark(a.activation(
                        stat[:, 4:5], stat[:, 6:7], AF.Exp, scale=-0.5))

                @blk.vector
                def _(v, tt=tt, sl=sl):
                    wait(v, M[("lnrstd", tt)])
                    fence(v, s_dv, v.tensor_tensor(
                        stat[:, 5:6], stat[:, 2:3], stat[:, 4:5], ALU.mult))
                    fence(v, s_dv, v.tensor_scalar(
                        stat[:, 5:6], stat[:, 5:6], -1.0, None, ALU.mult))
                    wait(v, M.get(("xnst", tt - 6)))
                    M[("lnapp", tt)] = s_dv.mark(v.tensor_scalar(
                        xnb[:, sl:sl + D], xbuf[:, sl:sl + D],
                        stat[:, 4:5], stat[:, 5:6], ALU.mult, ALU.add))

                @blk.sync
                def _(sync, tt=tt, sl=sl):
                    wait(sync, M[("lnapp", tt)])
                    M[("xnst", tt)] = s_st.mark(sync.dma_start(
                        out=xn_dram[tt * 128:(tt + 1) * 128, :],
                        in_=xnb[:, sl:sl + D]), 16)
                    if tt + 6 < 16:
                        M[("lx", tt + 6)] = s_ld.mark(sync.dma_start(
                            out=xbuf[:, sl:sl + D],
                            in_=xp[(tt + 6) * 128:(tt + 7) * 128, :]), 16)

                if tt == 7:
                    @blk.sync
                    def _(sync):
                        wait(sync, M[("xnst", 7)])
                        for c in range(NDC):
                            s_ld.mark(sync.dma_start(
                                out=xnT_m[:, c * TM:(c + 1) * TM],
                                in_=xn_dram[0:TM, c * 128:(c + 1) * 128],
                                transpose=True), 16)

            @blk.sync
            def _(sync):
                wait(sync, M[("xnst", 15)])
                for c in range(NDC):
                    s_ld.mark(sync.dma_start(
                        out=xnT_r[:, c * TM:(c + 1) * TM],
                        in_=xn_dram[TM:T, c * 128:(c + 1) * 128],
                        transpose=True), 16)
                M["xt_all"] = s_ld.mile()

        with nc.Block() as blk:

            for _ in range(16):
                @blk.sync
                def _(sync):
                    gamma_dma(sync, 2)

                @blk.vector
                def _(v):
                    gamma_reduce_dve(v, 2)

        if "xn" in tapt:
            with nc.Block() as blk:
                @blk.sync
                def _(sync):
                    mm = s_st.mark(sync.dma_start(out=tapt["xn"], in_=xn_dram), 16)
                    wait(sync, mm)

        for cm, _h in reversed(ln_cms):
            cm.__exit__(None, None, None)

        att_cms = []
        for nm, shp, dt in [("wqkb", [128, 3 * NDC * 256], BF16),
                            ("wvb", [128, NDC * 512], BF16),
                            ("Kh", [128, 4 * T], BF16),
                            ("Qh", [128, 4 * TM], BF16),
                            ("Vg", [128, 16 * 512], BF16),
                            ("PT", [128, 2 * 2048], BF16),
                            ("rrow", [1, 256], F32),
                            ("rrb", [1, 256], BF16),
                            ("rbc", [128, 256], BF16)]:
            att_cms.append(manual(nm, shp, dt))
        (wqkb, wvb, Kh, Qh, Vg, PT, rrow, rrb, rbc) = \
            [h for _, h in att_cms]
        sqb = PT[:, 0:2048]

        def pair_kbs(p):
            return list(range(0, 2 * p + 2)) + list(range(8, 8 + 2 * p + 2))

        def wqk_slot(h):
            return (h % 3) * 4096

        def emit_wqk_load(blk, h):
            @blk.gpsimd
            def _(gq, h=h):
                wait(gq, M.get(("qmm", h - 3)))
                M[("wqk", h)] = s_cast.mark(gq.dma_start(
                    out=wqkb[:, wqk_slot(h):wqk_slot(h) + 4096],
                    in_=wqk[h]), 16)

        def emit_wv_load(blk, g, guard):
            @blk.gpsimd
            def _(gq, g=g, guard=guard):
                wait(gq, guard)
                gq.dma_start(out=wvb[:, 0:4096], in_=wv[g][:, 0:4096])\
                    .then_inc(s_cast.h, 16)
                s_cast.n += 16
                M[("wvb", g)] = s_cast.mark(gq.dma_start(
                    out=wvb[:, 4096:8192], in_=wv[g][:, 4096:8192]), 16)

        for g in range(4):
            with nc.Block() as blk:
                if g == 0:
                    emit_wv_load(blk, 0, None)
                    emit_wqk_load(blk, 0)
                    emit_wqk_load(blk, 1)
                    emit_wqk_load(blk, 2)

                @blk.sync
                def _(sync, g=g):
                    gamma_dma(sync, 4)



                for tt in range(16):
                    bank = 1024 + (tt % 2) * 512

                    @blk.tensor
                    def _(t, g=g, tt=tt, bank=bank):
                        if tt == 0:
                            wait(t, M[("wvb", g)])
                            wait(t, M["xt_all"])
                        wait(t, M.get(("vev", g, tt - 2)))
                        last = None
                        for c in range(NDC):
                            last = t.matmul(pB[:, bank:bank + 512],
                                            xnT_chunk(c, tt),
                                            wvb[:, c * 512:(c + 1) * 512],
                                            start=(c == 0), stop=(c == NDC - 1))
                        M[("vmm", g, tt)] = s_pe.mark(last)

                    @blk.vector
                    def _(v, g=g, tt=tt, bank=bank):
                        wait(v, M[("vmm", g, tt)])
                        M[("vev", g, tt)] = s_dv.mark(v.tensor_copy(
                            Vg[:, tt * 512:(tt + 1) * 512], pB[:, bank:bank + 512]))

                for i in range(4):
                    h = g * 4 + i
                    wq = wqkb[:, wqk_slot(h):wqk_slot(h) + 4096]
                    khs = Kh[:, i * T:(i + 1) * T]
                    qhs = Qh[:, i * TM:(i + 1) * TM]

                    @blk.tensor
                    def _(t, h=h, i=i, g=g, wq=wq):
                        wait(t, M[("wqk", h)])
                        if i == 0:
                            wait(t, M.get(("exp_last", h - 1)))
                        else:
                            wait(t, M[("qnred", h - 1)])
                        last = None
                        for c in range(NDC):
                            lh = wq[:, c * 256 + 128: c * 256 + 256]
                            for tc in range(4):
                                last = t.matmul(pA[:, tc * 512:(tc + 1) * 512],
                                                lh, xnT_half(c, tc),
                                                start=(c == 0), stop=(c == NDC - 1))
                        M[("kmm", h)] = s_pe.mark(last)

                    @blk.vector
                    def _(v, h=h, khs=khs):
                        wait(v, M[("kmm", h)])
                        M[("kev", h)] = fence(v, s_dv, v.tensor_copy(khs, pA[:, :]))
                        M[("ksq", h)] = s_dv.mark(v.tensor_tensor(
                            sqb[:, :], khs, khs, ALU.mult))

                    @blk.tensor
                    def _(t, h=h, i=i, wq=wq):
                        if i == 0:
                            wait(t, M.get(("cat", h - 1, 3)))
                        else:
                            wait(t, M[("qev", h - 1)])
                        last = None
                        for c in range(NDC):
                            lh = wq[:, c * 256: c * 256 + 128]
                            for tc in range(2):
                                last = t.matmul(
                                    pB[:, tc * 512:(tc + 1) * 512], lh,
                                    xnT_m[:, c * TM + tc * 512: c * TM + (tc + 1) * 512],
                                    start=(c == 0), stop=(c == NDC - 1))
                        M[("qmm", h)] = s_pe.mark(last)

                    @blk.vector
                    def _(v, h=h, qhs=qhs):
                        wait(v, M[("qmm", h)])
                        M[("qev", h)] = fence(v, s_dv, v.tensor_copy(
                            qhs, pB[:, 0:1024]))
                        M[("qsq", h)] = s_dv.mark(v.tensor_tensor(
                            sqb[:, 0:1024], qhs, qhs, ALU.mult))

                    if g * 4 + i + 3 < H:
                        emit_wqk_load(blk, g * 4 + i + 3)

                    @blk.tensor
                    def _(t, h=h):
                        wait(t, M[("ksq", h)])
                        last = None
                        for j in range(4):
                            last = t.matmul(pA[0:1, j * 512:(j + 1) * 512],
                                            ones_c[:, :],
                                            sqb[:, j * 512:(j + 1) * 512],
                                            start=True, stop=True)
                        M[("nK", h)] = s_pe.mark(last)

                    @blk.vector
                    def _(v, h=h, i=i):
                        wait(v, M[("nK", h)])
                        M[("nKred", h)] = fence(v, s_dv, v.tensor_reduce(
                            tiny[0:1, 4 + i:5 + i], pA[0:1, 0:2048], AX.X,
                            ALU.max))

                    @blk.tensor
                    def _(t, h=h):
                        wait(t, M[("qsq", h)])
                        wait(t, M[("nKred", h)])
                        last = None
                        for j in range(2):
                            last = t.matmul(pA[0:1, j * 512:(j + 1) * 512],
                                            ones_c[:, :],
                                            sqb[:, j * 512:(j + 1) * 512],
                                            start=True, stop=True)
                        M[("nQ", h)] = s_pe.mark(last)

                    @blk.vector
                    def _(v, h=h, i=i):
                        wait(v, M[("nQ", h)])
                        M[("qnred", h)] = fence(v, s_dv, v.tensor_reduce(
                            tiny[0:1, 8 + i:9 + i], pA[0:1, 0:1024], AX.X,
                            ALU.max))

                @blk.vector
                def _(v, g=g):
                    wait(v, M[("qnred", g * 4 + 3)])
                    wait(v, M.get(("exp_last", g * 4 - 1)))
                    M[("m2g", g)] = fence(v, s_dv, v.tensor_tensor(
                        tiny[0:1, 4:8], tiny[0:1, 4:8], tiny[0:1, 8:12],
                        ALU.mult))

                @blk.scalar
                def _(a, g=g):
                    wait(a, M[("m2g", g)])
                    fence(a, s_ac, a.activation(tiny[0:1, 8:12], tiny[0:1, 4:8],
                                                AF.Ln))
                    M[("mhg", g)] = s_ac.mark(a.activation(
                        tiny[0:1, 4:8], tiny[0:1, 8:12], AF.Exp, scale=0.5))

                @blk.vector
                def _(v, g=g):
                    wait(v, M[("mhg", g)])
                    fence(v, s_dv, v.tensor_scalar(
                        tiny[0:1, 4:8], tiny[0:1, 4:8], -SCL, None, ALU.mult))
                    M[("negmg", g)] = s_dv.mark(v.tensor_copy(
                        tinyb[0:1, 0:4], tiny[0:1, 4:8]))

                @blk.tensor
                def _(t, g=g):
                    wait(t, M[("negmg", g)])
                    M[("biasmm", g)] = s_pe.mark(t.matmul(
                        pB[:, 1536:1540], ones_r[:, :], tinyb[0:1, 0:4],
                        start=True, stop=True))

                @blk.vector
                def _(v, g=g):
                    wait(v, M[("biasmm", g)])
                    M[("bias_g", g)] = s_dv.mark(v.tensor_copy(
                        bias4[:, 0:4], pB[:, 1536:1540]))

                pend = []
                rsav_pend = []

                def flush_pend():
                    while pend:
                        pend.pop(0)()

                def flush_rsav():
                    while rsav_pend:
                        rsav_pend.pop(0)()

                bg = 0
                for i in range(4):
                    h = g * 4 + i
                    khs = Kh[:, i * T:(i + 1) * T]
                    qhs = Qh[:, i * TM:(i + 1) * TM]
                    for p in range(4):
                        pg = i * 4 + p
                        q = pg % 2
                        av0 = q * 512
                        bc0 = q * 512 + 256
                        rs0 = 1024 + q * 512
                        kbs = pair_kbs(p)
                        nkb = len(kbs)
                        batches = [kbs[j:j + 8] for j in range(0, nkb, 8)]
                        kidx = 0
                        for bi, batch in enumerate(batches):
                            nb = len(batch)
                            half = (bg % 2) * 2048
                            mybg = bg

                            @blk.tensor
                            def _(t, h=h, g=g, p=p, bi=bi, batch=batch,
                                  khs=khs, qhs=qhs, half=half, mybg=mybg):
                                if mybg < 1:
                                    wait(t, M[("qnred", g * 4 + 3)])
                                else:
                                    wait(t, M[("expg", g, mybg - 1)])
                                last = None
                                for idx, kb in enumerate(batch):
                                    dst = pA[:, idx * 256:(idx + 1) * 256]
                                    if kb == 2 * p:
                                        t.matmul(dst, khs[:, kb * 128:(kb + 1) * 128],
                                                 qhs[:, p * 256:(p + 1) * 256],
                                                 start=True, stop=False)
                                        last = t.matmul(
                                            pA[:, idx * 256: idx * 256 + 128],
                                            identb[:, :], bigm[:, 0:128],
                                            start=False, stop=True)
                                    elif kb == 2 * p + 1:
                                        t.matmul(dst, khs[:, kb * 128:(kb + 1) * 128],
                                                 qhs[:, p * 256:(p + 1) * 256],
                                                 start=True, stop=False)
                                        last = t.matmul(
                                            dst, identb[:, :], bigm[:, 128:384],
                                            start=False, stop=True)
                                    elif kb == 8 + 2 * p:
                                        t.matmul(dst, khs[:, kb * 128:(kb + 1) * 128],
                                                 qhs[:, p * 256:(p + 1) * 256],
                                                 start=True, stop=False)
                                        last = t.matmul(
                                            dst, identb[:, :], bigm[:, 384:640],
                                            start=False, stop=True)
                                    elif kb == 9 + 2 * p:
                                        t.matmul(dst, khs[:, kb * 128:(kb + 1) * 128],
                                                 qhs[:, p * 256:(p + 1) * 256],
                                                 start=True, stop=False)
                                        last = t.matmul(
                                            dst, identb[:, :], bigm[:, 640:896],
                                            start=False, stop=True)
                                    else:
                                        last = t.matmul(
                                            dst, khs[:, kb * 128:(kb + 1) * 128],
                                            qhs[:, p * 256:(p + 1) * 256],
                                            start=True, stop=True)
                                M[("scg", g, mybg)] = s_pe.mark(last)

                            @blk.scalar
                            def _(a, g=g, i=i, nb=nb, half=half, mybg=mybg):
                                wait(a, M[("scg", g, mybg)])
                                if mybg == 0:
                                    wait(a, M[("bias_g", g)])
                                wait(a, M.get(("rsavg", g, mybg - 2)))
                                M[("expg", g, mybg)] = s_ac.mark(a.activation(
                                    PT[:, half:half + nb * 256],
                                    pA[:, 0:nb * 256],
                                    AF.Exp, scale=SCL, bias=bias4[:, i:i + 1]))

                            if "pt0" in tapt and h == 0 and p == 1 and bi == 0:
                                @blk.sync
                                def _(sync):
                                    wait(sync, M[("expg", 0, 2)])
                                    mm = s_st.mark(sync.dma_start(
                                        out=tapt["pt0"], in_=PT[:, 0:2048]), 16)
                                    wait(sync, mm)

                            def rsav_cl(h=h, g=g, p=p, bi=bi, batch=batch,
                                        kidx=kidx, nkb=nkb, half=half,
                                        mybg=mybg, pg=pg, av0=av0, rs0=rs0):
                                @blk.tensor
                                def _(t):
                                    wait(t, M[("expg", g, mybg)])
                                    if bi == 0:
                                        wait(t, M.get(("catg", g, pg - 2)))
                                    last = None
                                    for idx, kb in enumerate(batch):
                                        kg = kidx + idx
                                        t.matmul(pB[0:1, rs0:rs0 + 256],
                                                 ones_c[:, :],
                                                 PT[:, half + idx * 256:
                                                    half + (idx + 1) * 256],
                                                 start=(kg == 0),
                                                 stop=(kg == nkb - 1))
                                        u0 = kb * 512 + (h - 4 * g) * 128
                                        last = t.matmul(
                                            pB[:, av0:av0 + 256],
                                            Vg[:, u0:u0 + 128],
                                            PT[:, half + idx * 256:
                                               half + (idx + 1) * 256],
                                            start=(kg == 0), stop=(kg == nkb - 1))
                                    M[("rsavg", g, mybg)] = s_pe.mark(last)

                            rsav_pend.append(rsav_cl)
                            if len(rsav_pend) > 1:
                                rsav_pend.pop(0)()

                            kidx += len(batch)
                            bg += 1
                            if bi == 0:
                                flush_pend()

                        last_bg = bg - 1

                        def tail(h=h, g=g, p=p, pg=pg, av0=av0, bc0=bc0,
                                 rs0=rs0, last_bg=last_bg):
                            @blk.vector
                            def _(v):
                                wait(v, M[("rsavg", g, last_bg)])
                                fence(v, s_dv, v.tensor_copy(rrow[0:1, :],
                                                             pB[0:1, rs0:rs0 + 256]))
                                fence(v, s_dv, v.reciprocal(rrow[0:1, :],
                                                            rrow[0:1, :]))
                                M[("rrbg", g, pg)] = s_dv.mark(v.tensor_copy(
                                    rrb[0:1, :], rrow[0:1, :]))

                            @blk.tensor
                            def _(t):
                                wait(t, M[("rrbg", g, pg)])
                                M[("bcastg", g, pg)] = s_pe.mark(t.matmul(
                                    pB[:, bc0:bc0 + 256], ones_r[:, :],
                                    rrb[0:1, :], start=True, stop=True))

                            @blk.vector
                            def _(v):
                                wait(v, M[("bcastg", g, pg)])
                                fence(v, s_dv, v.tensor_copy(rbc[:, :],
                                                             pB[:, bc0:bc0 + 256]))
                                M[("catg", g, pg)] = s_dv.mark(v.tensor_tensor(
                                    catT[:, h * TM + p * 256:
                                         h * TM + (p + 1) * 256],
                                    pB[:, av0:av0 + 256], rbc[:, :], ALU.mult))
                                if p == 3:
                                    M[("cat", h, 3)] = M[("catg", g, pg)]

                        flush_rsav()
                        pend.append(tail)

                        for _ in range(2):
                            @blk.sync
                            def _(sync):
                                gamma_dma(sync, 2)

                            @blk.vector
                            def _(v):
                                gamma_reduce_dve(v, 2)

                flush_pend()

                if g < 3:
                    emit_wv_load(blk, g + 1, M[("vmm", g, 15)])

        for name, key in [("k0", ("kev", 0)), ("q0", ("qev", 0))]:
            if name in tapt:
                src = Kh[:, 0:T] if name == "k0" else Qh[:, 0:TM]
                with nc.Block() as blk:
                    @blk.sync
                    def _(sync, name=name, key=key, src=src):
                        wait(sync, M[key])
                        mm = s_st.mark(sync.dma_start(out=tapt[name], in_=src), 16)
                        wait(sync, mm)
        if "v0" in tapt:
            with nc.Block() as blk:
                @blk.sync
                def _(sync):
                    mm = s_st.mark(sync.dma_start(out=tapt["v0"], in_=Vg[:, :]), 16)
                    wait(sync, mm)
        if "cat" in tapt:
            with nc.Block() as blk:
                @blk.sync
                def _(sync):
                    wait(sync, M[("cat", 15, 3)])
                    mm = s_st.mark(sync.dma_start(out=tapt["cat"], in_=catT[:, :]),
                                   16)
                    wait(sync, mm)

        for cm, _h in reversed(att_cms):
            cm.__exit__(None, None, None)
        xnT_r_cm.__exit__(None, None, None)

        op_cms = [manual("woTb", [128, NDC * D], BF16),
                  manual("sstage", [128, 2 * D], F32)]
        woTb, sstage = [h for _, h in op_cms]

        with nc.Block() as blk:
            @blk.gpsimd
            def _(gq):
                for c in range(NDC):
                    M[("wo", c)] = s_cast.mark(gq.dma_start(
                        out=woTb[:, c * D:(c + 1) * D], in_=wo[c]), 16)
            for _ in range(12):
                @blk.sync
                def _(sync):
                    gamma_dma(sync, 2)

                @blk.vector
                def _(v):
                    gamma_reduce_dve(v, 2)

            for tt in range(8):
                pg = pA if tt % 2 == 0 else pB

                @blk.tensor
                def _(t, tt=tt, pg=pg):
                    if tt == 0:
                        wait(t, M[("wo", NDC - 1)])
                    wait(t, M.get(("sev", tt - 2)))
                    last = None
                    for c in range(NDC):
                        lh = catT[:, c * TM + tt * 128: c * TM + (tt + 1) * 128]
                        for oc in range(4):
                            last = t.matmul(
                                pg[:, oc * 512:(oc + 1) * 512], lh,
                                woTb[:, c * D + oc * 512: c * D + (oc + 1) * 512],
                                start=(c == 0), stop=(c == NDC - 1))
                    M[("smm", tt)] = s_pe.mark(last)

                @blk.vector
                def _(v, tt=tt, pg=pg):
                    wait(v, M[("smm", tt)])
                    wait(v, M.get(("sst", tt - 2)))
                    sl = (tt % 2) * D
                    M[("sev", tt)] = s_dv.mark(v.tensor_copy(
                        sstage[:, sl:sl + D], pg[:, :]))

                @blk.sync
                def _(sync, tt=tt):
                    wait(sync, M[("sev", tt)])
                    sl = (tt % 2) * D
                    M[("sst", tt)] = s_st.mark(sync.dma_start(
                        out=s_dram[tt * 128:(tt + 1) * 128, :],
                        in_=sstage[:, sl:sl + D]), 16)

            M["s_all"] = s_st.mile()

        if "sm" in tapt:
            with nc.Block() as blk:
                @blk.sync
                def _(sync):
                    sync.wait_ge(s_st.h, s_st.n)
                    mm = s_st.mark(sync.dma_start(out=tapt["sm"], in_=s_dram), 16)
                    wait(sync, mm)

        for cm, _h in reversed(op_cms):
            cm.__exit__(None, None, None)
        catT_cm.__exit__(None, None, None)

        with nc.Block() as blk:
            for _ in range(140):
                if gst["dma"] >= len(G_ORDER) and gst["act"] >= len(G_ACT):
                    break

                @blk.sync
                def _(sync):
                    gamma_dma(sync, 2)

                @blk.vector
                def _(v):
                    gamma_reduce_dve(v, 2)

            @blk.tensor
            def _(t):
                t.wait_ge(s_dv.h, s_dv.n)
                last = None
                for j in range(3):
                    last = t.matmul(pA[0:1, j * 512:j * 512 + 1], ones_cf[:, :],
                                    gcols[:, j:j + 1], start=True, stop=True)
                M[("gsum")] = s_pe.mark(last)

            @blk.vector
            def _(v):
                wait(v, M["gsum"])
                mm = None
                for j in range(3):
                    fence(v, s_dv, v.tensor_copy(tiny[0:1, j:j + 1],
                                                 pA[0:1, j * 512:j * 512 + 1]))
                    fence(v, s_dv, v.tensor_scalar(
                        tiny[0:1, j:j + 1], tiny[0:1, j:j + 1],
                        1.0 / (HID * D), EPS_Q, ALU.mult, ALU.add))
                    mm = fence(v, s_dv, v.reciprocal(tiny[0:1, j:j + 1],
                                                     tiny[0:1, j:j + 1]))
                M["ginv"] = mm

            @blk.tensor
            def _(t):
                wait(t, M["ginv"])
                last = None
                for j in range(3):
                    last = t.matmul(pB[:, j * 512:j * 512 + 1], ones_rf[:, :],
                                    tiny[0:1, j:j + 1], start=True, stop=True)
                M["gbc"] = s_pe.mark(last)

            @blk.vector
            def _(v):
                wait(v, M["gbc"])
                last = None
                for j in range(3):
                    last = v.tensor_copy(scol[:, j:j + 1], pB[:, j * 512:j * 512 + 1])
                M["scol"] = s_dv.mark(last)

        if "gam" in tapt:
            with nc.Block() as blk:
                @blk.sync
                def _(sync):
                    wait(sync, M["scol"])
                    mm = s_st.mark(sync.dma_start(out=tapt["gam"], in_=gcols[:, :]),
                                   16)
                    wait(sync, mm)

        grs2_cm.__exit__(None, None, None)
        grs_cm.__exit__(None, None, None)

        hid_cm, hidT = manual("hidT", [128, NHC * TM], BF16)
        ffn_cms = []
        for nm, shp, dt in [("w1f", [128, 2 * D], F32), ("w2f", [128, 2 * D], F32),
                            ("w1qb", [128, 2 * D], BF16), ("w2qb", [128, 2 * D], BF16),
                            ("w3f", [128, D], F32), ("w3qb", [128, D], BF16),
                            ("qi", [128, 2 * D], F32),
                            ("sigb", [128, TM], F32)]:
            ffn_cms.append(manual(nm, shp, dt))
        (w1f, w2f, w1qb, w2qb, w3f, w3qb, qi, sigb) = \
            [h for _, h in ffn_cms]

        def qi_slot(k):
            return qi[:, (k % 2) * D:(k % 2 + 1) * D]

        with nc.Block() as blk:
            ffn_pend = []

            def ffn_flush():
                while ffn_pend:
                    ffn_pend.pop(0)()

            for hc in range(NHC):
                sl = (hc % 2)

                @blk.sync
                def _(sync, hc=hc, sl=sl):
                    wait(sync, M.get(("q1a", hc - 1)))
                    M[("w1l", hc)] = s_ld.mark(sync.dma_start(
                        out=w1f[:, sl * D:(sl + 1) * D], in_=w1t[hc]), 16)
                    wait(sync, M.get(("q2a", hc - 1)))
                    M[("w2l", hc)] = s_ld.mark(sync.dma_start(
                        out=w2f[:, sl * D:(sl + 1) * D], in_=w2t[hc]), 16)

                @blk.gpsimd
                def _(gq, hc=hc):
                    wait(gq, M.get(("q3a", hc - 1)))
                    M[("w3l", hc)] = s_gl.mark(gq.dma_start(
                        out=w3f[:, :], in_=w3r[hc * 128:(hc + 1) * 128, :]), 16)

                @blk.scalar
                def _(a, hc=hc):
                    wait(a, M[("w3l", hc)])
                    if hc == 0:
                        wait(a, M["scol"])
                    wait(a, M.get(("dvq", 3 * hc + 1 - 2)))
                    M[("q3a", hc)] = s_ac.mark(a.mul(
                        qi_slot(3 * hc + 1), w3f[:, :], scol[:, 2:3]))

                @blk.vector
                def _(v, hc=hc):
                    wait(v, M[("q3a", hc)])
                    wait(v, M.get(("w3st", hc - 1)))
                    fence(v, s_dv, v.tensor_scalar(
                        qi_slot(3 * hc + 1), qi_slot(3 * hc + 1),
                        QMAGIC, -QMAGIC, ALU.add, ALU.add))
                    M[("w3q", hc)] = M[("dvq", 3 * hc + 1)] = fence(
                        v, s_dv, v.tensor_scalar(
                            w3qb[:, :], qi_slot(3 * hc + 1),
                            1.0, -1.0, ALU.min, ALU.max))

                @blk.gpsimd
                def _(gq, hc=hc):
                    wait(gq, M[("w3q", hc)])
                    M[("w3st", hc)] = s_gs.mark(gq.dma_start(
                        out=w3q_dram[hc * 128:(hc + 1) * 128, :],
                        in_=w3qb[:, :]), 16)

                @blk.scalar
                def _(a, hc=hc, sl=sl):
                    wait(a, M[("w1l", hc)])
                    if hc == 0:
                        wait(a, M["scol"])
                    wait(a, M.get(("dvq", 3 * hc + 0 - 2)))
                    M[("q1a", hc)] = s_ac.mark(a.mul(
                        qi_slot(3 * hc + 0),
                        w1f[:, sl * D:(sl + 1) * D], scol[:, 0:1]))

                @blk.vector
                def _(v, hc=hc, sl=sl):
                    wait(v, M[("q1a", hc)])
                    wait(v, M.get(("x12", hc - 2)))
                    fence(v, s_dv, v.tensor_scalar(
                        qi_slot(3 * hc + 0), qi_slot(3 * hc + 0),
                        QMAGIC, -QMAGIC, ALU.add, ALU.add))
                    M[("w1q", hc)] = M[("dvq", 3 * hc + 0)] = fence(
                        v, s_dv, v.tensor_scalar(
                            w1qb[:, sl * D:(sl + 1) * D], qi_slot(3 * hc + 0),
                            1.0, -1.0, ALU.min, ALU.max))

                @blk.scalar
                def _(a, hc=hc, sl=sl):
                    wait(a, M[("w2l", hc)])
                    if hc == 0:
                        wait(a, M["scol"])
                    wait(a, M.get(("dvq", 3 * hc + 2 - 2)))
                    M[("q2a", hc)] = s_ac.mark(a.mul(
                        qi_slot(3 * hc + 2),
                        w2f[:, sl * D:(sl + 1) * D], scol[:, 1:2]))

                @blk.vector
                def _(v, hc=hc, sl=sl):
                    wait(v, M[("q2a", hc)])
                    wait(v, M.get(("x12", hc - 2)))
                    fence(v, s_dv, v.tensor_scalar(
                        qi_slot(3 * hc + 2), qi_slot(3 * hc + 2),
                        QMAGIC, -QMAGIC, ALU.add, ALU.add))
                    M[("w12q", hc)] = M[("dvq", 3 * hc + 2)] = fence(
                        v, s_dv, v.tensor_scalar(
                            w2qb[:, sl * D:(sl + 1) * D], qi_slot(3 * hc + 2),
                            1.0, -1.0, ALU.min, ALU.max))

                pg = pA if hc % 2 == 0 else pB

                @blk.tensor
                def _(t, hc=hc, sl=sl, pg=pg):
                    wait(t, M[("w1q", hc)])
                    wait(t, M[("w12q", hc)])
                    wait(t, M.get(("hid", hc - 2)))
                    last = None
                    for th in range(2):
                        for c in range(NDC):
                            t.matmul(pg[:, th * 512:(th + 1) * 512],
                                     w1qb[:, sl * D + c * 128: sl * D + (c + 1) * 128],
                                     xnT_m[:, c * TM + th * 512: c * TM + (th + 1) * 512],
                                     start=(c == 0), stop=(c == NDC - 1))
                    for th in range(2):
                        for c in range(NDC):
                            last = t.matmul(
                                pg[:, 1024 + th * 512: 1024 + (th + 1) * 512],
                                w2qb[:, sl * D + c * 128: sl * D + (c + 1) * 128],
                                xnT_m[:, c * TM + th * 512: c * TM + (th + 1) * 512],
                                start=(c == 0), stop=(c == NDC - 1))
                    M[("x12", hc)] = s_pe.mark(last)

                def sighid(hc=hc, pg=pg):
                    @blk.scalar
                    def _(a, hc=hc, pg=pg):
                        wait(a, M[("x12", hc)])
                        wait(a, M.get(("hid", hc - 1)))
                        M[("sig", hc)] = s_ac.mark(a.activation(
                            sigb[:, :], pg[:, 0:1024], AF.Sigmoid))

                    @blk.vector
                    def _(v, hc=hc, pg=pg):
                        wait(v, M[("sig", hc)])
                        M[("hid", hc)] = s_dv.mark(v.tensor_tensor(
                            hidT[:, hc * TM:(hc + 1) * TM], pg[:, 1024:2048],
                            sigb[:, :], ALU.mult))

                ffn_pend.append(sighid)
                if hc >= 1:
                    ffn_flush() if False else None
                if len(ffn_pend) > 1:
                    ffn_pend.pop(0)()
            ffn_flush()

        if "hid" in tapt:
            with nc.Block() as blk:
                @blk.sync
                def _(sync):
                    wait(sync, M[("hid", NHC - 1)])
                    mm = s_st.mark(sync.dma_start(out=tapt["hid"], in_=hidT[:, :]),
                                   16)
                    wait(sync, mm)
        if "w3q" in tapt:
            with nc.Block() as blk:
                @blk.sync
                def _(sync):
                    sync.wait_ge(s_gs.h, s_gs.n)
                    mm = s_st.mark(sync.dma_start(out=tapt["w3q"], in_=w3q_dram),
                                   16)
                    wait(sync, mm)

        for cm, _h in reversed(ffn_cms):
            cm.__exit__(None, None, None)

        HHC = NHC // 2
        f2_cms = [manual("w3qd", [128, 3 * HHC * 512], BF16),
                  manual("gt", [128, 2 * 512], F32),
                  manual("stile", [128, 4 * 512], F32),
                  manual("xtile", [128, 3 * 512], F32)]
        w3qd, gt, stile, xtile = [h for _, h in f2_cms]

        w3l2_base = s_gl.n
        for j in range(8):
            M[("w3l2", j)] = (s_gl, w3l2_base + 16 * (j + 1))

        def emit_sxl(sync, g):
            dg, tt = g // 8, g % 8
            M[("sxl", dg, tt)] = s_ld.mark(sync.dma_start(
                out=stile[:, (g % 4) * 512:(g % 4 + 1) * 512],
                in_=s_dram[tt * 128:(tt + 1) * 128,
                           dg * 512:(dg + 1) * 512]), 16)

        def emit_sxl2(sync, g):
            dg, tt = g // 8, g % 8
            M[("sxl2", dg, tt)] = s_ld.mark(sync.dma_start(
                out=xtile[:, (g % 3) * 512:(g % 3 + 1) * 512],
                in_=xp[tt * 128:(tt + 1) * 128,
                       dg * 512:(dg + 1) * 512]), 16)

        def emit_half_load(gq, j):
            dg, hh = j // 2, j % 2
            if j == 0:
                wait(gq, M[("w3st", HHC - 1)])
            elif j == 1:
                wait(gq, M[("w3st", NHC - 1)])
            elif j >= 3:
                wait(gq, M[("ffh", j - 3)])
            mm = s_gl.mark(gq.dma_start(
                out=w3qd[:, (j % 3) * HHC * 512:(j % 3 + 1) * HHC * 512],
                in_=bass.AP(w3q_h, hh * HHC * 128 * D + dg * 512,
                            [[D, 128], [128 * D, HHC], [1, 512]])), 16)
            assert mm == M[("w3l2", j)]

        with nc.Block() as blk:
            @blk.gpsimd
            def _(gq):
                for j in range(3):
                    emit_half_load(gq, j)

            @blk.sync
            def _(sync):
                sync.wait_ge(s_st.h, M["s_all"][1])
                for g in range(4):
                    emit_sxl(sync, g)
                for g in range(3):
                    emit_sxl2(sync, g)

            for dg in range(4):
                for tt in range(8):
                    @blk.tensor
                    def _(t, dg=dg, tt=tt):
                        if tt == 0:
                            wait(t, M[("w3l2", 2 * dg)])
                        wait(t, M.get(("silu", dg - 1, tt)))
                        pg = pA if tt < 4 else pB
                        col = (tt % 4) * 512
                        last = None
                        for hc in range(NHC):
                            if tt == 0 and hc == HHC:
                                wait(t, M[("w3l2", 2 * dg + 1)])
                            slot = (2 * dg + hc // HHC) % 3
                            last = t.matmul(
                                pg[:, col:col + 512],
                                hidT[:, hc * TM + tt * 128: hc * TM + (tt + 1) * 128],
                                w3qd[:, slot * HHC * 512 + (hc % HHC) * 512:
                                     slot * HHC * 512 + (hc % HHC + 1) * 512],
                                start=(hc == 0), stop=(hc == NHC - 1))
                            if tt == 7 and hc == HHC - 1:
                                M[("ffh", 2 * dg)] = s_pe.mark(last)
                        M[("fft", dg, tt)] = s_pe.mark(last)
                        if tt == 7:
                            M[("ffh", 2 * dg + 1)] = M[("fft", dg, tt)]

                if dg < 3:
                    @blk.gpsimd
                    def _(gq, dg=dg):
                        for j in (2 * dg + 3, 2 * dg + 4):
                            if j < 8:
                                emit_half_load(gq, j)

                for tt in range(8):
                    pg = pA if tt < 4 else pB
                    col = (tt % 4) * 512

                    @blk.scalar
                    def _(a, dg=dg, tt=tt, pg=pg, col=col):
                        wait(a, M[("fft", dg, tt)])
                        wait(a, M.get(("comb", dg, tt - 2)))
                        M[("silu", dg, tt)] = s_ac.mark(a.activation(
                            gt[:, (tt % 2) * 512:(tt % 2 + 1) * 512],
                            pg[:, col:col + 512], AF.Silu))

                    @blk.vector
                    def _(v, dg=dg, tt=tt):
                        wait(v, M[("silu", dg, tt)])
                        wait(v, M[("sxl", dg, tt)])
                        wait(v, M[("sxl2", dg, tt)])
                        g = 8 * dg + tt
                        o = stile[:, (g % 4) * 512:(g % 4 + 1) * 512]
                        fence(v, s_dv, v.tensor_tensor(
                            o, gt[:, (tt % 2) * 512:(tt % 2 + 1) * 512],
                            o, ALU.mult))
                        M[("comb", dg, tt)] = s_dv.mark(v.tensor_tensor(
                            o, o, xtile[:, (g % 3) * 512:(g % 3 + 1) * 512],
                            ALU.add))

                    @blk.sync
                    def _(sync, dg=dg, tt=tt):
                        g = 8 * dg + tt
                        wait(sync, M[("comb", dg, tt)])
                        M[("ost", dg, tt)] = s_st.mark(sync.dma_start(
                            out=out[tt * 128:(tt + 1) * 128,
                                    dg * 512:(dg + 1) * 512],
                            in_=stile[:, (g % 4) * 512:(g % 4 + 1) * 512]), 16)
                        if g + 3 < 32:
                            emit_sxl2(sync, g + 3)
                        if g + 4 < 32:
                            wait(sync, M[("ost", dg, tt)])
                            emit_sxl(sync, g + 4)

        with nc.Block() as blk:
            @blk.sync
            def _(sync):
                sync.wait_ge(s_st.h, s_st.n)

            @blk.gpsimd
            def _(gq):
                gq.wait_ge(s_gs.h, s_gs.n)

        for cm, _h in reversed(f2_cms):
            cm.__exit__(None, None, None)
        hid_cm.__exit__(None, None, None)
        xnT_m_cm.__exit__(None, None, None)

    return nc



def _prep_weights(wp, wout, w1, w2, w3):
    wp = np.asarray(wp, np.float32)
    wq = wp[0:D].reshape(H, 128, NDC, 128)
    wk = wp[D:2 * D].reshape(H, 128, NDC, 128)
    wqk = np.stack([wq.transpose(0, 3, 2, 1), wk.transpose(0, 3, 2, 1)],
                   axis=3)
    wqk = np.ascontiguousarray(wqk.reshape(H, 128, NDC * 256))
    wvv = wp[2 * D:3 * D].reshape(4, 512, NDC, 128)
    wvv = np.ascontiguousarray(wvv.transpose(0, 3, 2, 1).reshape(4, 128, NDC * 512))
    wo_t = np.ascontiguousarray(
        np.asarray(wout, np.float32).T.reshape(NDC, 128, D))
    w1t = np.ascontiguousarray(
        np.asarray(w1, np.float32).reshape(NHC, 128, NDC, 128)
        .transpose(0, 3, 2, 1).reshape(NHC, 128, D))
    w2t = np.ascontiguousarray(
        np.asarray(w2, np.float32).reshape(NHC, 128, NDC, 128)
        .transpose(0, 3, 2, 1).reshape(NHC, 128, D))
    w3r = np.ascontiguousarray(np.asarray(w3, np.float32).T)
    return wqk, wvv, wo_t, w1t, w2t, w3r


def _masks(par):
    ii = np.arange(128)[:, None]
    jj = np.arange(128)[None, :]
    md = (ii <= jj).astype(ml_dtypes.bfloat16)
    ma = np.ones((128, 256), ml_dtypes.bfloat16)
    mb = np.zeros((128, 256), ml_dtypes.bfloat16)
    if par == 0:
        ma[:, 0:128] = 0
    else:
        mb[:, 128:256] = 1
    return md, ma, mb


NEG = np.float32(-1e9)


def _bigmask(par):
    dg = np.where(np.arange(128)[:, None] <= np.arange(128)[None, :],
                  np.float32(0), NEG)
    full = np.full((128, 128), NEG, np.float32)
    zero = np.zeros((128, 128), np.float32)
    eb = np.concatenate([full, dg], 1)
    if par == 0:
        ab = np.concatenate([full, zero], 1)
        bb = np.concatenate([full, full], 1)
    else:
        ab = np.concatenate([zero, zero], 1)
        bb = np.concatenate([full, zero], 1)
    bigm = np.concatenate([dg, eb, ab, bb], 1)
    return np.ascontiguousarray(bigm.astype(ml_dtypes.bfloat16))


def _perm_rows(par):
    mine = [par + 2 * j for j in range(8)]
    rest = [(1 - par) + 2 * j for j in range(8)]
    idx = []
    for gg in mine + rest:
        idx.extend(range(gg * 128, (gg + 1) * 128))
    return np.array(idx), mine


def kernel(L0_attn_x, L0_attn_w_project, L0_attn_w_out,
           L0_ffn_w1, L0_ffn_w2, L0_ffn_w3, _taps=(), _nc=None):
    x = np.asarray(L0_attn_x, np.float32)
    wqk, wvv, wo_t, w1t, w2t, w3r = _prep_weights(
        L0_attn_w_project, L0_attn_w_out, L0_ffn_w1, L0_ffn_w2, L0_ffn_w3)

    nc = _nc if _nc is not None else build_nc(_taps)
    in_maps = []
    metas = []
    for c in range(8):
        b, par = c // 2, c % 2
        idx, mine = _perm_rows(par)
        md, ma, mb = _masks(par)
        in_maps.append({
            "xp": np.ascontiguousarray(x[b][idx]),
            "wqk": wqk, "wv": wvv, "wo": wo_t,
            "w1t": w1t, "w2t": w2t, "w3r": w3r,
            "mask_d": md, "mask_a": ma, "mask_b": mb,
            "identd": np.ascontiguousarray(np.eye(128, dtype=ml_dtypes.bfloat16)),
            "bigmd": _bigmask(par),
        })
        metas.append((b, mine))

    res = run_bass_kernel_spmd(nc, in_maps, core_ids=list(range(8)))

    final = np.empty((4, T, D), np.float32)
    for c in range(8):
        b, mine = metas[c]
        o = res.results[c]["out"]
        for j, gg in enumerate(mine):
            final[b, gg * 128:(gg + 1) * 128, :] = o[j * 128:(j + 1) * 128, :]
    if _taps:
        return final, list(res.results)
    return final


# version 35 (final)
# speedup vs baseline: 1.7365x; 1.0139x over previous
import math
from contextlib import ExitStack

import numpy as np
import ml_dtypes
import sys

sys.path.insert(0, "/opt/trn_rl_repo")
import concourse.bass as bass
import concourse.mybir as mybir
from concourse.bass_utils import run_bass_kernel_spmd

F32 = mybir.dt.float32
BF16 = mybir.dt.bfloat16
AF = mybir.ActivationFunctionType
ALU = mybir.AluOpType
AX = mybir.AxisListType

D = 2048
T = 2048
TM = 1024
H = 16
HID = 5632
NHC = HID // 128
NDC = D // 128
EPS_LN = 1e-5
EPS_Q = 1e-8
SCL = 1.0 / math.sqrt(128.0)
QMAGIC = float(3 * 2 ** 22)


class Sem:
    def __init__(self, ctx, nc, name):
        self.h = ctx.enter_context(nc.semaphore(name))
        self.n = 0

    def mark(self, inst, k=1):
        inst.then_inc(self.h, k)
        self.n += k
        return (self, self.n)

    def mile(self):
        return (self, self.n)


def wait(eng, mile):
    if mile is not None:
        eng.wait_ge(mile[0].h, mile[1])


def fence(eng, sem, inst):
    mm = sem.mark(inst)
    wait(eng, mm)
    return mm


def build_nc(taps=()):
    taps = set(taps)
    nc = bass.Bass("TRN2", target_bir_lowering=False, debug=False)

    xp = nc.dram_tensor("xp", [T, D], F32, kind="ExternalInput").ap()
    wqk = nc.dram_tensor("wqk", [H, 128, NDC * 256], F32, kind="ExternalInput").ap()
    wv = nc.dram_tensor("wv", [4, 128, NDC * 512], F32, kind="ExternalInput").ap()
    wo = nc.dram_tensor("wo", [NDC, 128, D], F32, kind="ExternalInput").ap()
    w1t = nc.dram_tensor("w1t", [NHC, 128, D], F32, kind="ExternalInput").ap()
    w2t = nc.dram_tensor("w2t", [NHC, 128, D], F32, kind="ExternalInput").ap()
    w3r = nc.dram_tensor("w3r", [HID, D], F32, kind="ExternalInput").ap()
    mask_d = nc.dram_tensor("mask_d", [128, 128], BF16, kind="ExternalInput").ap()
    mask_a = nc.dram_tensor("mask_a", [128, 256], BF16, kind="ExternalInput").ap()
    mask_b = nc.dram_tensor("mask_b", [128, 256], BF16, kind="ExternalInput").ap()
    identd = nc.dram_tensor("identd", [128, 128], BF16, kind="ExternalInput").ap()
    bigmd = nc.dram_tensor("bigmd", [128, 896], BF16, kind="ExternalInput").ap()

    out = nc.dram_tensor("out", [TM, D], F32, kind="ExternalOutput").ap()

    xn_dram = nc.dram_tensor("xn_dram", [T, D], BF16).ap()
    s_dram = nc.dram_tensor("s_dram", [TM, D], F32).ap()
    w3q_h = nc.dram_tensor("w3q_dram", [HID, D], BF16)
    w3q_dram = w3q_h.ap()

    tapt = {}
    for name, shape, dt in [
        ("xn", [T, D], BF16), ("k0", [128, T], BF16), ("q0", [128, TM], BF16),
        ("v0", [128, 16 * 512], BF16), ("cat", [128, H * TM], BF16),
        ("sm", [TM, D], F32), ("gam", [128, 16], F32),
        ("hid", [128, NHC * TM], BF16), ("w3q", [HID, D], BF16),
        ("bias0", [128, 1], F32), ("pt0", [128, 2048], BF16),
    ]:
        if name in taps:
            tapt[name] = nc.dram_tensor("tap_" + name, shape, dt,
                                        kind="ExternalOutput").ap()

    M = {}

    with ExitStack() as ctx:
        def sb(name, shape, dt):
            return ctx.enter_context(nc.sbuf_tensor(name, shape, dt))

        def manual(name, shape, dt, side="right"):
            cm = nc.sbuf_tensor(name, shape, dt, side=side)
            return cm, cm.__enter__()

        ones_c = sb("ones_c", [128, 1], BF16)
        ones_r = sb("ones_r", [1, 128], BF16)
        ones_cf = sb("ones_cf", [128, 1], F32)
        ones_rf = sb("ones_rf", [1, 128], F32)
        identb = sb("identb", [128, 128], BF16)
        bigm = sb("bigm", [128, 896], BF16)
        gcols = sb("gcols", [128, 16], F32)
        scol = sb("scol", [128, 4], F32)
        bias4 = sb("bias4", [128, 4], F32)
        tiny = sb("tiny", [1, 16], F32)
        tinyb = sb("tinyb", [1, 4], BF16)
        epsc = sb("epsc", [128, 1], F32)

        pA = ctx.enter_context(nc.psum_tensor("pA", [128, 2048], F32))
        pB = ctx.enter_context(nc.psum_tensor("pB", [128, 2048], F32))

        s_ld = Sem(ctx, nc, "s_ld")
        s_gl = Sem(ctx, nc, "s_gl")
        s_st = Sem(ctx, nc, "s_st")
        s_gs = Sem(ctx, nc, "s_gs")
        s_pe = Sem(ctx, nc, "s_pe")
        s_dv = Sem(ctx, nc, "s_dv")
        s_ac = Sem(ctx, nc, "s_ac")
        s_pl = Sem(ctx, nc, "s_pl")
        s_gq = Sem(ctx, nc, "s_gq")
        s_cast = Sem(ctx, nc, "s_cast")

        xnT_m_cm, xnT_m = manual("xnT_m", [128, NDC * TM], BF16, side="left")
        grs_cm, grs = manual("grs", [128, 1024], F32, side="left")
        grs2_cm, grs2 = manual("grs2", [128, 1024], F32, side="left")
        catT_cm, catT = manual("catT", [128, H * TM], BF16, side="left")
        xnT_r_cm, xnT_r = manual("xnT_r", [128, NDC * TM], BF16, side="left")

        def xnT_chunk(c, tt):
            if tt < 8:
                return xnT_m[:, c * TM + tt * 128: c * TM + (tt + 1) * 128]
            return xnT_r[:, c * TM + (tt - 8) * 128: c * TM + (tt - 7) * 128]

        def xnT_half(c, tc):
            if tc < 2:
                return xnT_m[:, c * TM + tc * 512: c * TM + (tc + 1) * 512]
            return xnT_r[:, c * TM + (tc - 2) * 512: c * TM + (tc - 1) * 512]

        G_ORDER = []
        for j in range(NHC):
            for hh in range(2):
                G_ORDER.append(("w1", j, hh))
                G_ORDER.append(("w3", j, hh))
        for j in range(NHC):
            for hh in range(2):
                G_ORDER.append(("w2", j, hh))
        G_ACT = list(range(len(G_ORDER)))
        gst = {"dma": 0, "act": 0}

        def gamma_slab_in(i):
            m, j, hh = G_ORDER[i]
            if m == "w1":
                return w1t[j][:, hh * 1024:(hh + 1) * 1024]
            if m == "w2":
                return w2t[j][:, hh * 1024:(hh + 1) * 1024]
            return w3r[j * 128:(j + 1) * 128, hh * 1024:(hh + 1) * 1024]

        def gamma_dma(g, count):
            for _ in range(count):
                i = gst["dma"]
                if i >= len(G_ORDER):
                    return
                if i >= 2 and ("gfree", i - 2) not in M:
                    return
                buf = grs if i % 2 == 0 else grs2
                wait(g, M.get(("gfree", i - 2)))
                M[("gdma", i)] = s_gq.mark(
                    g.dma_start(out=buf[:, :], in_=gamma_slab_in(i)), 16)
                gst["dma"] += 1

        def gamma_reduce_dve(v, count):
            for _ in range(count):
                k = gst["act"]
                if k >= len(G_ACT):
                    return
                i = G_ACT[k]
                if ("gdma", i) not in M:
                    return
                buf = grs if i % 2 == 0 else grs2
                col = {"w1": 0, "w2": 1, "w3": 2}[G_ORDER[i][0]]
                wait(v, M[("gdma", i)])
                fence(v, s_dv, v.tensor_reduce(
                    gcols[:, 9:10], buf[:, :], AX.X, ALU.add,
                    apply_absolute_value=True))
                M[("gfree", i)] = fence(v, s_dv, v.tensor_tensor(
                    gcols[:, col:col + 1], gcols[:, col:col + 1],
                    gcols[:, 9:10], ALU.add))
                gst["act"] += 1

        with nc.Block() as blk:
            @blk.sync
            def _(sync):
                s_ld.mark(sync.dma_start(out=identb[:, :], in_=identd), 16)
                M["consts"] = s_ld.mark(sync.dma_start(out=bigm[:, :], in_=bigmd), 16)

            @blk.vector
            def _(v):
                v.memset(ones_c[:, :], 1.0)
                v.memset(ones_r[:, :], 1.0)
                v.memset(ones_cf[:, :], 1.0)
                v.memset(ones_rf[:, :], 1.0)
                v.memset(gcols[:, :], 0.0)
                v.memset(tiny[:, :], 0.0)
                v.memset(epsc[:, :], EPS_LN)

        ln_cms = []
        for nm, shp, dt in [("xbuf", [128, 6 * D], F32), ("xnb", [128, 6 * D], BF16),
                            ("lnsq", [128, D], F32), ("stat", [128, 8], F32)]:
            ln_cms.append(manual(nm, shp, dt))
        xbuf, xnb, lnsq, stat = [h for _, h in ln_cms]

        with nc.Block() as blk:
            @blk.sync
            def _(sync):
                for tt in range(6):
                    M[("lx", tt)] = s_ld.mark(sync.dma_start(
                        out=xbuf[:, tt * D:(tt + 1) * D],
                        in_=xp[tt * 128:(tt + 1) * 128, :]), 16)

            for tt in range(16):
                sl = (tt % 6) * D

                @blk.vector
                def _(v, tt=tt, sl=sl):
                    wait(v, M[("lx", tt)])
                    wait(v, M.get(("lnapp", tt - 1)))
                    fence(v, s_dv, v.tensor_reduce(
                        stat[:, 0:1], xbuf[:, sl:sl + D], AX.X, ALU.add))
                    fence(v, s_dv, v.tensor_scalar(
                        stat[:, 2:3], stat[:, 0:1], 1.0 / D, None, ALU.mult))
                    fence(v, s_dv, v.tensor_tensor(
                        lnsq[:, :], xbuf[:, sl:sl + D], xbuf[:, sl:sl + D],
                        ALU.mult))
                    fence(v, s_dv, v.tensor_reduce(
                        stat[:, 1:2], lnsq[:, :], AX.X, ALU.add))
                    fence(v, s_dv, v.tensor_tensor(
                        stat[:, 5:6], stat[:, 2:3], stat[:, 2:3], ALU.mult))
                    fence(v, s_dv, v.tensor_scalar(
                        stat[:, 3:4], stat[:, 1:2], 1.0 / D, None, ALU.mult))
                    M[("lnvar", tt)] = s_dv.mark(v.tensor_tensor(
                        stat[:, 3:4], stat[:, 3:4], stat[:, 5:6], ALU.subtract))

                @blk.scalar
                def _(a, tt=tt):
                    wait(a, M[("lnvar", tt)])
                    fence(a, s_ac, a.activation(stat[:, 6:7], stat[:, 3:4],
                                                AF.Ln, bias=epsc[:, 0:1]))
                    M[("lnrstd", tt)] = s_ac.mark(a.activation(
                        stat[:, 4:5], stat[:, 6:7], AF.Exp, scale=-0.5))

                @blk.vector
                def _(v, tt=tt, sl=sl):
                    wait(v, M[("lnrstd", tt)])
                    fence(v, s_dv, v.tensor_tensor(
                        stat[:, 5:6], stat[:, 2:3], stat[:, 4:5], ALU.mult))
                    fence(v, s_dv, v.tensor_scalar(
                        stat[:, 5:6], stat[:, 5:6], -1.0, None, ALU.mult))
                    wait(v, M.get(("xnst", tt - 6)))
                    M[("lnapp", tt)] = s_dv.mark(v.tensor_scalar(
                        xnb[:, sl:sl + D], xbuf[:, sl:sl + D],
                        stat[:, 4:5], stat[:, 5:6], ALU.mult, ALU.add))

                @blk.sync
                def _(sync, tt=tt, sl=sl):
                    wait(sync, M[("lnapp", tt)])
                    M[("xnst", tt)] = s_st.mark(sync.dma_start(
                        out=xn_dram[tt * 128:(tt + 1) * 128, :],
                        in_=xnb[:, sl:sl + D]), 16)
                    if tt + 6 < 16:
                        M[("lx", tt + 6)] = s_ld.mark(sync.dma_start(
                            out=xbuf[:, sl:sl + D],
                            in_=xp[(tt + 6) * 128:(tt + 7) * 128, :]), 16)

                if tt == 7:
                    @blk.sync
                    def _(sync):
                        wait(sync, M[("xnst", 7)])
                        for c in range(NDC):
                            s_ld.mark(sync.dma_start(
                                out=xnT_m[:, c * TM:(c + 1) * TM],
                                in_=xn_dram[0:TM, c * 128:(c + 1) * 128],
                                transpose=True), 16)

            @blk.sync
            def _(sync):
                wait(sync, M[("xnst", 15)])
                for c in range(NDC):
                    s_ld.mark(sync.dma_start(
                        out=xnT_r[:, c * TM:(c + 1) * TM],
                        in_=xn_dram[TM:T, c * 128:(c + 1) * 128],
                        transpose=True), 16)
                M["xt_all"] = s_ld.mile()

        with nc.Block() as blk:

            for _ in range(16):
                @blk.sync
                def _(sync):
                    gamma_dma(sync, 1)

                @blk.gpsimd
                def _(gq):
                    gamma_dma(gq, 1)

                @blk.vector
                def _(v):
                    gamma_reduce_dve(v, 2)

        if "xn" in tapt:
            with nc.Block() as blk:
                @blk.sync
                def _(sync):
                    mm = s_st.mark(sync.dma_start(out=tapt["xn"], in_=xn_dram), 16)
                    wait(sync, mm)

        for cm, _h in reversed(ln_cms):
            cm.__exit__(None, None, None)

        att_cms = []
        for nm, shp, dt in [("wqkb", [128, 3 * NDC * 256], BF16),
                            ("wvb", [128, NDC * 512], BF16),
                            ("Kh", [128, 4 * T], BF16),
                            ("Qh", [128, 4 * TM], BF16),
                            ("Vg", [128, 16 * 512], BF16),
                            ("PT", [128, 2 * 2048], BF16),
                            ("rrow", [1, 256], F32),
                            ("rrb", [1, 256], BF16),
                            ("rbc", [128, 256], BF16)]:
            att_cms.append(manual(nm, shp, dt))
        (wqkb, wvb, Kh, Qh, Vg, PT, rrow, rrb, rbc) = \
            [h for _, h in att_cms]
        sqb = PT[:, 0:2048]

        def pair_kbs(p):
            return list(range(0, 2 * p + 2)) + list(range(8, 8 + 2 * p + 2))

        def wqk_slot(h):
            return (h % 3) * 4096

        def emit_wqk_load(blk, h):
            @blk.gpsimd
            def _(gq, h=h):
                wait(gq, M.get(("qmm", h - 3)))
                M[("wqk", h)] = s_cast.mark(gq.dma_start(
                    out=wqkb[:, wqk_slot(h):wqk_slot(h) + 4096],
                    in_=wqk[h]), 16)

        def emit_wv_load(blk, g, guard):
            @blk.gpsimd
            def _(gq, g=g, guard=guard):
                wait(gq, guard)
                gq.dma_start(out=wvb[:, 0:4096], in_=wv[g][:, 0:4096])\
                    .then_inc(s_cast.h, 16)
                s_cast.n += 16
                M[("wvb", g)] = s_cast.mark(gq.dma_start(
                    out=wvb[:, 4096:8192], in_=wv[g][:, 4096:8192]), 16)

        for g in range(4):
            with nc.Block() as blk:
                if g == 0:
                    emit_wv_load(blk, 0, None)
                    emit_wqk_load(blk, 0)
                    emit_wqk_load(blk, 1)
                    emit_wqk_load(blk, 2)

                @blk.sync
                def _(sync, g=g):
                    gamma_dma(sync, 4)



                for tt in range(16):
                    bank = 1024 + (tt % 2) * 512

                    @blk.tensor
                    def _(t, g=g, tt=tt, bank=bank):
                        if tt == 0:
                            wait(t, M[("wvb", g)])
                            wait(t, M["xt_all"])
                        wait(t, M.get(("vev", g, tt - 2)))
                        last = None
                        for c in range(NDC):
                            last = t.matmul(pB[:, bank:bank + 512],
                                            xnT_chunk(c, tt),
                                            wvb[:, c * 512:(c + 1) * 512],
                                            start=(c == 0), stop=(c == NDC - 1))
                        M[("vmm", g, tt)] = s_pe.mark(last)

                    @blk.vector
                    def _(v, g=g, tt=tt, bank=bank):
                        wait(v, M[("vmm", g, tt)])
                        M[("vev", g, tt)] = s_dv.mark(v.tensor_copy(
                            Vg[:, tt * 512:(tt + 1) * 512], pB[:, bank:bank + 512]))

                for i in range(4):
                    h = g * 4 + i
                    wq = wqkb[:, wqk_slot(h):wqk_slot(h) + 4096]
                    khs = Kh[:, i * T:(i + 1) * T]
                    qhs = Qh[:, i * TM:(i + 1) * TM]

                    @blk.tensor
                    def _(t, h=h, i=i, g=g, wq=wq):
                        wait(t, M[("wqk", h)])
                        if i == 0:
                            wait(t, M.get(("exp_last", h - 1)))
                        else:
                            wait(t, M[("qnred", h - 1)])
                        last = None
                        for c in range(NDC):
                            lh = wq[:, c * 256 + 128: c * 256 + 256]
                            for tc in range(4):
                                last = t.matmul(pA[:, tc * 512:(tc + 1) * 512],
                                                lh, xnT_half(c, tc),
                                                start=(c == 0), stop=(c == NDC - 1))
                        M[("kmm", h)] = s_pe.mark(last)

                    @blk.vector
                    def _(v, h=h, khs=khs):
                        wait(v, M[("kmm", h)])
                        M[("kev", h)] = fence(v, s_dv, v.tensor_copy(khs, pA[:, :]))
                        M[("ksq", h)] = s_dv.mark(v.tensor_tensor(
                            sqb[:, :], khs, khs, ALU.mult))

                    @blk.tensor
                    def _(t, h=h, i=i, wq=wq):
                        if i == 0:
                            wait(t, M.get(("cat", h - 1, 3)))
                        else:
                            wait(t, M[("qev", h - 1)])
                        last = None
                        for c in range(NDC):
                            lh = wq[:, c * 256: c * 256 + 128]
                            for tc in range(2):
                                last = t.matmul(
                                    pB[:, tc * 512:(tc + 1) * 512], lh,
                                    xnT_m[:, c * TM + tc * 512: c * TM + (tc + 1) * 512],
                                    start=(c == 0), stop=(c == NDC - 1))
                        M[("qmm", h)] = s_pe.mark(last)

                    @blk.vector
                    def _(v, h=h, qhs=qhs):
                        wait(v, M[("qmm", h)])
                        M[("qev", h)] = fence(v, s_dv, v.tensor_copy(
                            qhs, pB[:, 0:1024]))
                        M[("qsq", h)] = s_dv.mark(v.tensor_tensor(
                            sqb[:, 0:1024], qhs, qhs, ALU.mult))

                    if g * 4 + i + 3 < H:
                        emit_wqk_load(blk, g * 4 + i + 3)

                    @blk.tensor
                    def _(t, h=h):
                        wait(t, M[("ksq", h)])
                        last = None
                        for j in range(4):
                            last = t.matmul(pA[0:1, j * 512:(j + 1) * 512],
                                            ones_c[:, :],
                                            sqb[:, j * 512:(j + 1) * 512],
                                            start=True, stop=True)
                        M[("nK", h)] = s_pe.mark(last)

                    @blk.vector
                    def _(v, h=h, i=i):
                        wait(v, M[("nK", h)])
                        M[("nKred", h)] = fence(v, s_dv, v.tensor_reduce(
                            tiny[0:1, 4 + i:5 + i], pA[0:1, 0:2048], AX.X,
                            ALU.max))

                    @blk.tensor
                    def _(t, h=h):
                        wait(t, M[("qsq", h)])
                        wait(t, M[("nKred", h)])
                        last = None
                        for j in range(2):
                            last = t.matmul(pA[0:1, j * 512:(j + 1) * 512],
                                            ones_c[:, :],
                                            sqb[:, j * 512:(j + 1) * 512],
                                            start=True, stop=True)
                        M[("nQ", h)] = s_pe.mark(last)

                    @blk.vector
                    def _(v, h=h, i=i):
                        wait(v, M[("nQ", h)])
                        M[("qnred", h)] = fence(v, s_dv, v.tensor_reduce(
                            tiny[0:1, 8 + i:9 + i], pA[0:1, 0:1024], AX.X,
                            ALU.max))

                @blk.vector
                def _(v, g=g):
                    wait(v, M[("qnred", g * 4 + 3)])
                    wait(v, M.get(("exp_last", g * 4 - 1)))
                    M[("m2g", g)] = fence(v, s_dv, v.tensor_tensor(
                        tiny[0:1, 4:8], tiny[0:1, 4:8], tiny[0:1, 8:12],
                        ALU.mult))

                @blk.scalar
                def _(a, g=g):
                    wait(a, M[("m2g", g)])
                    fence(a, s_ac, a.activation(tiny[0:1, 8:12], tiny[0:1, 4:8],
                                                AF.Ln))
                    M[("mhg", g)] = s_ac.mark(a.activation(
                        tiny[0:1, 4:8], tiny[0:1, 8:12], AF.Exp, scale=0.5))

                @blk.vector
                def _(v, g=g):
                    wait(v, M[("mhg", g)])
                    fence(v, s_dv, v.tensor_scalar(
                        tiny[0:1, 4:8], tiny[0:1, 4:8], -SCL, None, ALU.mult))
                    M[("negmg", g)] = s_dv.mark(v.tensor_copy(
                        tinyb[0:1, 0:4], tiny[0:1, 4:8]))

                @blk.tensor
                def _(t, g=g):
                    wait(t, M[("negmg", g)])
                    M[("biasmm", g)] = s_pe.mark(t.matmul(
                        pB[:, 1536:1540], ones_r[:, :], tinyb[0:1, 0:4],
                        start=True, stop=True))

                @blk.vector
                def _(v, g=g):
                    wait(v, M[("biasmm", g)])
                    M[("bias_g", g)] = s_dv.mark(v.tensor_copy(
                        bias4[:, 0:4], pB[:, 1536:1540]))

                pend = []
                rsav_pend = []

                def flush_pend():
                    while pend:
                        pend.pop(0)()

                def flush_rsav():
                    while rsav_pend:
                        rsav_pend.pop(0)()

                bg = 0
                for i in range(4):
                    h = g * 4 + i
                    khs = Kh[:, i * T:(i + 1) * T]
                    qhs = Qh[:, i * TM:(i + 1) * TM]
                    for p in range(4):
                        pg = i * 4 + p
                        q = pg % 2
                        av0 = q * 512
                        bc0 = q * 512 + 256
                        rs0 = 1024 + q * 512
                        kbs = pair_kbs(p)
                        nkb = len(kbs)
                        batches = [kbs[j:j + 8] for j in range(0, nkb, 8)]
                        kidx = 0
                        for bi, batch in enumerate(batches):
                            nb = len(batch)
                            half = (bg % 2) * 2048
                            mybg = bg

                            @blk.tensor
                            def _(t, h=h, g=g, p=p, bi=bi, batch=batch,
                                  khs=khs, qhs=qhs, half=half, mybg=mybg):
                                if mybg < 1:
                                    wait(t, M[("qnred", g * 4 + 3)])
                                else:
                                    wait(t, M[("expg", g, mybg - 1)])
                                last = None
                                for idx, kb in enumerate(batch):
                                    dst = pA[:, idx * 256:(idx + 1) * 256]
                                    if kb == 2 * p:
                                        t.matmul(dst, khs[:, kb * 128:(kb + 1) * 128],
                                                 qhs[:, p * 256:(p + 1) * 256],
                                                 start=True, stop=False)
                                        last = t.matmul(
                                            pA[:, idx * 256: idx * 256 + 128],
                                            identb[:, :], bigm[:, 0:128],
                                            start=False, stop=True)
                                    elif kb == 2 * p + 1:
                                        t.matmul(dst, khs[:, kb * 128:(kb + 1) * 128],
                                                 qhs[:, p * 256:(p + 1) * 256],
                                                 start=True, stop=False)
                                        last = t.matmul(
                                            dst, identb[:, :], bigm[:, 128:384],
                                            start=False, stop=True)
                                    elif kb == 8 + 2 * p:
                                        t.matmul(dst, khs[:, kb * 128:(kb + 1) * 128],
                                                 qhs[:, p * 256:(p + 1) * 256],
                                                 start=True, stop=False)
                                        last = t.matmul(
                                            dst, identb[:, :], bigm[:, 384:640],
                                            start=False, stop=True)
                                    elif kb == 9 + 2 * p:
                                        t.matmul(dst, khs[:, kb * 128:(kb + 1) * 128],
                                                 qhs[:, p * 256:(p + 1) * 256],
                                                 start=True, stop=False)
                                        last = t.matmul(
                                            dst, identb[:, :], bigm[:, 640:896],
                                            start=False, stop=True)
                                    else:
                                        last = t.matmul(
                                            dst, khs[:, kb * 128:(kb + 1) * 128],
                                            qhs[:, p * 256:(p + 1) * 256],
                                            start=True, stop=True)
                                M[("scg", g, mybg)] = s_pe.mark(last)

                            @blk.scalar
                            def _(a, g=g, i=i, nb=nb, half=half, mybg=mybg):
                                wait(a, M[("scg", g, mybg)])
                                if mybg == 0:
                                    wait(a, M[("bias_g", g)])
                                wait(a, M.get(("rsavg", g, mybg - 2)))
                                M[("expg", g, mybg)] = s_ac.mark(a.activation(
                                    PT[:, half:half + nb * 256],
                                    pA[:, 0:nb * 256],
                                    AF.Exp, scale=SCL, bias=bias4[:, i:i + 1]))

                            if "pt0" in tapt and h == 0 and p == 1 and bi == 0:
                                @blk.sync
                                def _(sync):
                                    wait(sync, M[("expg", 0, 2)])
                                    mm = s_st.mark(sync.dma_start(
                                        out=tapt["pt0"], in_=PT[:, 0:2048]), 16)
                                    wait(sync, mm)

                            def rsav_cl(h=h, g=g, p=p, bi=bi, batch=batch,
                                        kidx=kidx, nkb=nkb, half=half,
                                        mybg=mybg, pg=pg, av0=av0, rs0=rs0):
                                @blk.tensor
                                def _(t):
                                    wait(t, M[("expg", g, mybg)])
                                    if bi == 0:
                                        wait(t, M.get(("catg", g, pg - 2)))
                                    last = None
                                    for idx, kb in enumerate(batch):
                                        kg = kidx + idx
                                        t.matmul(pB[0:1, rs0:rs0 + 256],
                                                 ones_c[:, :],
                                                 PT[:, half + idx * 256:
                                                    half + (idx + 1) * 256],
                                                 start=(kg == 0),
                                                 stop=(kg == nkb - 1))
                                        u0 = kb * 512 + (h - 4 * g) * 128
                                        last = t.matmul(
                                            pB[:, av0:av0 + 256],
                                            Vg[:, u0:u0 + 128],
                                            PT[:, half + idx * 256:
                                               half + (idx + 1) * 256],
                                            start=(kg == 0), stop=(kg == nkb - 1))
                                    M[("rsavg", g, mybg)] = s_pe.mark(last)

                            rsav_pend.append(rsav_cl)
                            if len(rsav_pend) > 1:
                                rsav_pend.pop(0)()

                            kidx += len(batch)
                            bg += 1
                            if bi == 0:
                                flush_pend()

                        last_bg = bg - 1

                        def tail(h=h, g=g, p=p, pg=pg, av0=av0, bc0=bc0,
                                 rs0=rs0, last_bg=last_bg):
                            @blk.vector
                            def _(v):
                                wait(v, M[("rsavg", g, last_bg)])
                                fence(v, s_dv, v.tensor_copy(rrow[0:1, :],
                                                             pB[0:1, rs0:rs0 + 256]))
                                fence(v, s_dv, v.reciprocal(rrow[0:1, :],
                                                            rrow[0:1, :]))
                                M[("rrbg", g, pg)] = s_dv.mark(v.tensor_copy(
                                    rrb[0:1, :], rrow[0:1, :]))

                            @blk.tensor
                            def _(t):
                                wait(t, M[("rrbg", g, pg)])
                                M[("bcastg", g, pg)] = s_pe.mark(t.matmul(
                                    pB[:, bc0:bc0 + 256], ones_r[:, :],
                                    rrb[0:1, :], start=True, stop=True))

                            @blk.vector
                            def _(v):
                                wait(v, M[("bcastg", g, pg)])
                                fence(v, s_dv, v.tensor_copy(rbc[:, :],
                                                             pB[:, bc0:bc0 + 256]))
                                M[("catg", g, pg)] = s_dv.mark(v.tensor_tensor(
                                    catT[:, h * TM + p * 256:
                                         h * TM + (p + 1) * 256],
                                    pB[:, av0:av0 + 256], rbc[:, :], ALU.mult))
                                if p == 3:
                                    M[("cat", h, 3)] = M[("catg", g, pg)]

                        flush_rsav()
                        pend.append(tail)

                        for _ in range(2):
                            @blk.sync
                            def _(sync):
                                gamma_dma(sync, 2)

                            @blk.vector
                            def _(v):
                                gamma_reduce_dve(v, 2)

                flush_pend()

                if g < 3:
                    emit_wv_load(blk, g + 1, M[("vmm", g, 15)])

        for name, key in [("k0", ("kev", 0)), ("q0", ("qev", 0))]:
            if name in tapt:
                src = Kh[:, 0:T] if name == "k0" else Qh[:, 0:TM]
                with nc.Block() as blk:
                    @blk.sync
                    def _(sync, name=name, key=key, src=src):
                        wait(sync, M[key])
                        mm = s_st.mark(sync.dma_start(out=tapt[name], in_=src), 16)
                        wait(sync, mm)
        if "v0" in tapt:
            with nc.Block() as blk:
                @blk.sync
                def _(sync):
                    mm = s_st.mark(sync.dma_start(out=tapt["v0"], in_=Vg[:, :]), 16)
                    wait(sync, mm)
        if "cat" in tapt:
            with nc.Block() as blk:
                @blk.sync
                def _(sync):
                    wait(sync, M[("cat", 15, 3)])
                    mm = s_st.mark(sync.dma_start(out=tapt["cat"], in_=catT[:, :]),
                                   16)
                    wait(sync, mm)

        for cm, _h in reversed(att_cms):
            cm.__exit__(None, None, None)
        xnT_r_cm.__exit__(None, None, None)

        op_cms = [manual("woTb", [128, NDC * D], BF16),
                  manual("sstage", [128, 2 * D], F32)]
        woTb, sstage = [h for _, h in op_cms]

        with nc.Block() as blk:
            @blk.gpsimd
            def _(gq):
                for c in range(NDC):
                    M[("wo", c)] = s_cast.mark(gq.dma_start(
                        out=woTb[:, c * D:(c + 1) * D], in_=wo[c]), 16)
            for _ in range(12):
                @blk.sync
                def _(sync):
                    gamma_dma(sync, 2)

                @blk.vector
                def _(v):
                    gamma_reduce_dve(v, 2)

            for tt in range(8):
                pg = pA if tt % 2 == 0 else pB

                @blk.tensor
                def _(t, tt=tt, pg=pg):
                    if tt == 0:
                        wait(t, M[("wo", NDC - 1)])
                    wait(t, M.get(("sev", tt - 2)))
                    last = None
                    for c in range(NDC):
                        lh = catT[:, c * TM + tt * 128: c * TM + (tt + 1) * 128]
                        for oc in range(4):
                            last = t.matmul(
                                pg[:, oc * 512:(oc + 1) * 512], lh,
                                woTb[:, c * D + oc * 512: c * D + (oc + 1) * 512],
                                start=(c == 0), stop=(c == NDC - 1))
                    M[("smm", tt)] = s_pe.mark(last)

                @blk.vector
                def _(v, tt=tt, pg=pg):
                    wait(v, M[("smm", tt)])
                    wait(v, M.get(("sst", tt - 2)))
                    sl = (tt % 2) * D
                    M[("sev", tt)] = s_dv.mark(v.tensor_copy(
                        sstage[:, sl:sl + D], pg[:, :]))

                @blk.sync
                def _(sync, tt=tt):
                    wait(sync, M[("sev", tt)])
                    sl = (tt % 2) * D
                    M[("sst", tt)] = s_st.mark(sync.dma_start(
                        out=s_dram[tt * 128:(tt + 1) * 128, :],
                        in_=sstage[:, sl:sl + D]), 16)

            M["s_all"] = s_st.mile()

        if "sm" in tapt:
            with nc.Block() as blk:
                @blk.sync
                def _(sync):
                    sync.wait_ge(s_st.h, s_st.n)
                    mm = s_st.mark(sync.dma_start(out=tapt["sm"], in_=s_dram), 16)
                    wait(sync, mm)

        for cm, _h in reversed(op_cms):
            cm.__exit__(None, None, None)
        catT_cm.__exit__(None, None, None)

        with nc.Block() as blk:
            for _ in range(140):
                if gst["dma"] >= len(G_ORDER) and gst["act"] >= len(G_ACT):
                    break

                @blk.sync
                def _(sync):
                    gamma_dma(sync, 2)

                @blk.vector
                def _(v):
                    gamma_reduce_dve(v, 2)

            @blk.tensor
            def _(t):
                t.wait_ge(s_dv.h, s_dv.n)
                last = None
                for j in range(3):
                    last = t.matmul(pA[0:1, j * 512:j * 512 + 1], ones_cf[:, :],
                                    gcols[:, j:j + 1], start=True, stop=True)
                M[("gsum")] = s_pe.mark(last)

            @blk.vector
            def _(v):
                wait(v, M["gsum"])
                mm = None
                for j in range(3):
                    fence(v, s_dv, v.tensor_copy(tiny[0:1, j:j + 1],
                                                 pA[0:1, j * 512:j * 512 + 1]))
                    fence(v, s_dv, v.tensor_scalar(
                        tiny[0:1, j:j + 1], tiny[0:1, j:j + 1],
                        1.0 / (HID * D), EPS_Q, ALU.mult, ALU.add))
                    mm = fence(v, s_dv, v.reciprocal(tiny[0:1, j:j + 1],
                                                     tiny[0:1, j:j + 1]))
                M["ginv"] = mm

            @blk.tensor
            def _(t):
                wait(t, M["ginv"])
                last = None
                for j in range(3):
                    last = t.matmul(pB[:, j * 512:j * 512 + 1], ones_rf[:, :],
                                    tiny[0:1, j:j + 1], start=True, stop=True)
                M["gbc"] = s_pe.mark(last)

            @blk.vector
            def _(v):
                wait(v, M["gbc"])
                last = None
                for j in range(3):
                    last = v.tensor_copy(scol[:, j:j + 1], pB[:, j * 512:j * 512 + 1])
                M["scol"] = s_dv.mark(last)

        if "gam" in tapt:
            with nc.Block() as blk:
                @blk.sync
                def _(sync):
                    wait(sync, M["scol"])
                    mm = s_st.mark(sync.dma_start(out=tapt["gam"], in_=gcols[:, :]),
                                   16)
                    wait(sync, mm)

        grs2_cm.__exit__(None, None, None)
        grs_cm.__exit__(None, None, None)

        hid_cm, hidT = manual("hidT", [128, NHC * TM], BF16)
        ffn_cms = []
        for nm, shp, dt in [("w1f", [128, 2 * D], F32), ("w2f", [128, 2 * D], F32),
                            ("w1qb", [128, 2 * D], BF16), ("w2qb", [128, 2 * D], BF16),
                            ("w3f", [128, D], F32), ("w3qb", [128, D], BF16),
                            ("qi", [128, 2 * D], F32),
                            ("sigb", [128, TM], F32)]:
            ffn_cms.append(manual(nm, shp, dt))
        (w1f, w2f, w1qb, w2qb, w3f, w3qb, qi, sigb) = \
            [h for _, h in ffn_cms]

        def qi_slot(k):
            return qi[:, (k % 2) * D:(k % 2 + 1) * D]

        with nc.Block() as blk:
            ffn_pend = []

            def ffn_flush():
                while ffn_pend:
                    ffn_pend.pop(0)()

            for hc in range(NHC):
                sl = (hc % 2)

                @blk.sync
                def _(sync, hc=hc, sl=sl):
                    wait(sync, M.get(("q1a", hc - 1)))
                    M[("w1l", hc)] = s_ld.mark(sync.dma_start(
                        out=w1f[:, sl * D:(sl + 1) * D], in_=w1t[hc]), 16)
                    wait(sync, M.get(("q2a", hc - 1)))
                    M[("w2l", hc)] = s_ld.mark(sync.dma_start(
                        out=w2f[:, sl * D:(sl + 1) * D], in_=w2t[hc]), 16)

                @blk.gpsimd
                def _(gq, hc=hc):
                    wait(gq, M.get(("q3a", hc - 1)))
                    M[("w3l", hc)] = s_gl.mark(gq.dma_start(
                        out=w3f[:, :], in_=w3r[hc * 128:(hc + 1) * 128, :]), 16)

                @blk.scalar
                def _(a, hc=hc):
                    wait(a, M[("w3l", hc)])
                    if hc == 0:
                        wait(a, M["scol"])
                    wait(a, M.get(("dvq", 3 * hc + 1 - 2)))
                    M[("q3a", hc)] = s_ac.mark(a.mul(
                        qi_slot(3 * hc + 1), w3f[:, :], scol[:, 2:3]))

                @blk.vector
                def _(v, hc=hc):
                    wait(v, M[("q3a", hc)])
                    wait(v, M.get(("w3st", hc - 1)))
                    fence(v, s_dv, v.tensor_scalar(
                        qi_slot(3 * hc + 1), qi_slot(3 * hc + 1),
                        QMAGIC, -QMAGIC, ALU.add, ALU.add))
                    M[("w3q", hc)] = M[("dvq", 3 * hc + 1)] = fence(
                        v, s_dv, v.tensor_scalar(
                            w3qb[:, :], qi_slot(3 * hc + 1),
                            1.0, -1.0, ALU.min, ALU.max))

                @blk.gpsimd
                def _(gq, hc=hc):
                    wait(gq, M[("w3q", hc)])
                    M[("w3st", hc)] = s_gs.mark(gq.dma_start(
                        out=w3q_dram[hc * 128:(hc + 1) * 128, :],
                        in_=w3qb[:, :]), 16)

                @blk.scalar
                def _(a, hc=hc, sl=sl):
                    wait(a, M[("w1l", hc)])
                    if hc == 0:
                        wait(a, M["scol"])
                    wait(a, M.get(("dvq", 3 * hc + 0 - 2)))
                    M[("q1a", hc)] = s_ac.mark(a.mul(
                        qi_slot(3 * hc + 0),
                        w1f[:, sl * D:(sl + 1) * D], scol[:, 0:1]))

                @blk.vector
                def _(v, hc=hc, sl=sl):
                    wait(v, M[("q1a", hc)])
                    wait(v, M.get(("x12", hc - 2)))
                    fence(v, s_dv, v.tensor_scalar(
                        qi_slot(3 * hc + 0), qi_slot(3 * hc + 0),
                        QMAGIC, -QMAGIC, ALU.add, ALU.add))
                    M[("w1q", hc)] = M[("dvq", 3 * hc + 0)] = fence(
                        v, s_dv, v.tensor_scalar(
                            w1qb[:, sl * D:(sl + 1) * D], qi_slot(3 * hc + 0),
                            1.0, -1.0, ALU.min, ALU.max))

                @blk.scalar
                def _(a, hc=hc, sl=sl):
                    wait(a, M[("w2l", hc)])
                    if hc == 0:
                        wait(a, M["scol"])
                    wait(a, M.get(("dvq", 3 * hc + 2 - 2)))
                    M[("q2a", hc)] = s_ac.mark(a.mul(
                        qi_slot(3 * hc + 2),
                        w2f[:, sl * D:(sl + 1) * D], scol[:, 1:2]))

                @blk.vector
                def _(v, hc=hc, sl=sl):
                    wait(v, M[("q2a", hc)])
                    wait(v, M.get(("x12", hc - 2)))
                    fence(v, s_dv, v.tensor_scalar(
                        qi_slot(3 * hc + 2), qi_slot(3 * hc + 2),
                        QMAGIC, -QMAGIC, ALU.add, ALU.add))
                    M[("w12q", hc)] = M[("dvq", 3 * hc + 2)] = fence(
                        v, s_dv, v.tensor_scalar(
                            w2qb[:, sl * D:(sl + 1) * D], qi_slot(3 * hc + 2),
                            1.0, -1.0, ALU.min, ALU.max))

                pg = pA if hc % 2 == 0 else pB

                @blk.tensor
                def _(t, hc=hc, sl=sl, pg=pg):
                    wait(t, M[("w1q", hc)])
                    wait(t, M[("w12q", hc)])
                    wait(t, M.get(("hid", hc - 2)))
                    last = None
                    for th in range(2):
                        for c in range(NDC):
                            t.matmul(pg[:, th * 512:(th + 1) * 512],
                                     w1qb[:, sl * D + c * 128: sl * D + (c + 1) * 128],
                                     xnT_m[:, c * TM + th * 512: c * TM + (th + 1) * 512],
                                     start=(c == 0), stop=(c == NDC - 1))
                    for th in range(2):
                        for c in range(NDC):
                            last = t.matmul(
                                pg[:, 1024 + th * 512: 1024 + (th + 1) * 512],
                                w2qb[:, sl * D + c * 128: sl * D + (c + 1) * 128],
                                xnT_m[:, c * TM + th * 512: c * TM + (th + 1) * 512],
                                start=(c == 0), stop=(c == NDC - 1))
                    M[("x12", hc)] = s_pe.mark(last)

                def sighid(hc=hc, pg=pg):
                    @blk.scalar
                    def _(a, hc=hc, pg=pg):
                        wait(a, M[("x12", hc)])
                        wait(a, M.get(("hid", hc - 1)))
                        M[("sig", hc)] = s_ac.mark(a.activation(
                            sigb[:, :], pg[:, 0:1024], AF.Sigmoid))

                    @blk.vector
                    def _(v, hc=hc, pg=pg):
                        wait(v, M[("sig", hc)])
                        M[("hid", hc)] = s_dv.mark(v.tensor_tensor(
                            hidT[:, hc * TM:(hc + 1) * TM], pg[:, 1024:2048],
                            sigb[:, :], ALU.mult))

                ffn_pend.append(sighid)
                if hc >= 1:
                    ffn_flush() if False else None
                if len(ffn_pend) > 1:
                    ffn_pend.pop(0)()
            ffn_flush()

        if "hid" in tapt:
            with nc.Block() as blk:
                @blk.sync
                def _(sync):
                    wait(sync, M[("hid", NHC - 1)])
                    mm = s_st.mark(sync.dma_start(out=tapt["hid"], in_=hidT[:, :]),
                                   16)
                    wait(sync, mm)
        if "w3q" in tapt:
            with nc.Block() as blk:
                @blk.sync
                def _(sync):
                    sync.wait_ge(s_gs.h, s_gs.n)
                    mm = s_st.mark(sync.dma_start(out=tapt["w3q"], in_=w3q_dram),
                                   16)
                    wait(sync, mm)

        for cm, _h in reversed(ffn_cms):
            cm.__exit__(None, None, None)

        HHC = NHC // 2
        f2_cms = [manual("w3qd", [128, 3 * HHC * 512], BF16),
                  manual("gt", [128, 2 * 512], F32),
                  manual("stile", [128, 4 * 512], F32),
                  manual("xtile", [128, 3 * 512], F32)]
        w3qd, gt, stile, xtile = [h for _, h in f2_cms]

        w3l2_base = s_gl.n
        for j in range(8):
            M[("w3l2", j)] = (s_gl, w3l2_base + 16 * (j + 1))

        def emit_sxl(sync, g):
            dg, tt = g // 8, g % 8
            M[("sxl", dg, tt)] = s_ld.mark(sync.dma_start(
                out=stile[:, (g % 4) * 512:(g % 4 + 1) * 512],
                in_=s_dram[tt * 128:(tt + 1) * 128,
                           dg * 512:(dg + 1) * 512]), 16)

        def emit_sxl2(sync, g):
            dg, tt = g // 8, g % 8
            M[("sxl2", dg, tt)] = s_ld.mark(sync.dma_start(
                out=xtile[:, (g % 3) * 512:(g % 3 + 1) * 512],
                in_=xp[tt * 128:(tt + 1) * 128,
                       dg * 512:(dg + 1) * 512]), 16)

        def emit_half_load(gq, j):
            dg, hh = j // 2, j % 2
            if j == 0:
                wait(gq, M[("w3st", HHC - 1)])
            elif j == 1:
                wait(gq, M[("w3st", NHC - 1)])
            elif j >= 3:
                wait(gq, M[("ffh", j - 3)])
            mm = s_gl.mark(gq.dma_start(
                out=w3qd[:, (j % 3) * HHC * 512:(j % 3 + 1) * HHC * 512],
                in_=bass.AP(w3q_h, hh * HHC * 128 * D + dg * 512,
                            [[D, 128], [128 * D, HHC], [1, 512]])), 16)
            assert mm == M[("w3l2", j)]

        with nc.Block() as blk:
            @blk.gpsimd
            def _(gq):
                for j in range(3):
                    emit_half_load(gq, j)

            @blk.sync
            def _(sync):
                sync.wait_ge(s_st.h, M["s_all"][1])
                for g in range(4):
                    emit_sxl(sync, g)
                for g in range(3):
                    emit_sxl2(sync, g)

            for dg in range(4):
                for tt in range(8):
                    @blk.tensor
                    def _(t, dg=dg, tt=tt):
                        if tt == 0:
                            wait(t, M[("w3l2", 2 * dg)])
                        wait(t, M.get(("silu", dg - 1, tt)))
                        pg = pA if tt < 4 else pB
                        col = (tt % 4) * 512
                        last = None
                        for hc in range(NHC):
                            if tt == 0 and hc == HHC:
                                wait(t, M[("w3l2", 2 * dg + 1)])
                            slot = (2 * dg + hc // HHC) % 3
                            last = t.matmul(
                                pg[:, col:col + 512],
                                hidT[:, hc * TM + tt * 128: hc * TM + (tt + 1) * 128],
                                w3qd[:, slot * HHC * 512 + (hc % HHC) * 512:
                                     slot * HHC * 512 + (hc % HHC + 1) * 512],
                                start=(hc == 0), stop=(hc == NHC - 1))
                            if tt == 7 and hc == HHC - 1:
                                M[("ffh", 2 * dg)] = s_pe.mark(last)
                        M[("fft", dg, tt)] = s_pe.mark(last)
                        if tt == 7:
                            M[("ffh", 2 * dg + 1)] = M[("fft", dg, tt)]

                if dg < 3:
                    @blk.gpsimd
                    def _(gq, dg=dg):
                        for j in (2 * dg + 3, 2 * dg + 4):
                            if j < 8:
                                emit_half_load(gq, j)

                for tt in range(8):
                    pg = pA if tt < 4 else pB
                    col = (tt % 4) * 512

                    @blk.scalar
                    def _(a, dg=dg, tt=tt, pg=pg, col=col):
                        wait(a, M[("fft", dg, tt)])
                        wait(a, M.get(("comb", dg, tt - 2)))
                        M[("silu", dg, tt)] = s_ac.mark(a.activation(
                            gt[:, (tt % 2) * 512:(tt % 2 + 1) * 512],
                            pg[:, col:col + 512], AF.Silu))

                    @blk.vector
                    def _(v, dg=dg, tt=tt):
                        wait(v, M[("silu", dg, tt)])
                        wait(v, M[("sxl", dg, tt)])
                        wait(v, M[("sxl2", dg, tt)])
                        g = 8 * dg + tt
                        o = stile[:, (g % 4) * 512:(g % 4 + 1) * 512]
                        fence(v, s_dv, v.tensor_tensor(
                            o, gt[:, (tt % 2) * 512:(tt % 2 + 1) * 512],
                            o, ALU.mult))
                        M[("comb", dg, tt)] = s_dv.mark(v.tensor_tensor(
                            o, o, xtile[:, (g % 3) * 512:(g % 3 + 1) * 512],
                            ALU.add))

                    @blk.sync
                    def _(sync, dg=dg, tt=tt):
                        g = 8 * dg + tt
                        wait(sync, M[("comb", dg, tt)])
                        M[("ost", dg, tt)] = s_st.mark(sync.dma_start(
                            out=out[tt * 128:(tt + 1) * 128,
                                    dg * 512:(dg + 1) * 512],
                            in_=stile[:, (g % 4) * 512:(g % 4 + 1) * 512]), 16)
                        if g + 3 < 32:
                            emit_sxl2(sync, g + 3)
                        if g + 4 < 32:
                            wait(sync, M[("ost", dg, tt)])
                            emit_sxl(sync, g + 4)

        with nc.Block() as blk:
            @blk.sync
            def _(sync):
                sync.wait_ge(s_st.h, s_st.n)

            @blk.gpsimd
            def _(gq):
                gq.wait_ge(s_gs.h, s_gs.n)

        for cm, _h in reversed(f2_cms):
            cm.__exit__(None, None, None)
        hid_cm.__exit__(None, None, None)
        xnT_m_cm.__exit__(None, None, None)

    return nc



def _prep_weights(wp, wout, w1, w2, w3):
    wp = np.asarray(wp, np.float32)
    wq = wp[0:D].reshape(H, 128, NDC, 128)
    wk = wp[D:2 * D].reshape(H, 128, NDC, 128)
    wqk = np.stack([wq.transpose(0, 3, 2, 1), wk.transpose(0, 3, 2, 1)],
                   axis=3)
    wqk = np.ascontiguousarray(wqk.reshape(H, 128, NDC * 256))
    wvv = wp[2 * D:3 * D].reshape(4, 512, NDC, 128)
    wvv = np.ascontiguousarray(wvv.transpose(0, 3, 2, 1).reshape(4, 128, NDC * 512))
    wo_t = np.ascontiguousarray(
        np.asarray(wout, np.float32).T.reshape(NDC, 128, D))
    w1t = np.ascontiguousarray(
        np.asarray(w1, np.float32).reshape(NHC, 128, NDC, 128)
        .transpose(0, 3, 2, 1).reshape(NHC, 128, D))
    w2t = np.ascontiguousarray(
        np.asarray(w2, np.float32).reshape(NHC, 128, NDC, 128)
        .transpose(0, 3, 2, 1).reshape(NHC, 128, D))
    w3r = np.ascontiguousarray(np.asarray(w3, np.float32).T)
    return wqk, wvv, wo_t, w1t, w2t, w3r


def _masks(par):
    ii = np.arange(128)[:, None]
    jj = np.arange(128)[None, :]
    md = (ii <= jj).astype(ml_dtypes.bfloat16)
    ma = np.ones((128, 256), ml_dtypes.bfloat16)
    mb = np.zeros((128, 256), ml_dtypes.bfloat16)
    if par == 0:
        ma[:, 0:128] = 0
    else:
        mb[:, 128:256] = 1
    return md, ma, mb


NEG = np.float32(-1e9)


def _bigmask(par):
    dg = np.where(np.arange(128)[:, None] <= np.arange(128)[None, :],
                  np.float32(0), NEG)
    full = np.full((128, 128), NEG, np.float32)
    zero = np.zeros((128, 128), np.float32)
    eb = np.concatenate([full, dg], 1)
    if par == 0:
        ab = np.concatenate([full, zero], 1)
        bb = np.concatenate([full, full], 1)
    else:
        ab = np.concatenate([zero, zero], 1)
        bb = np.concatenate([full, zero], 1)
    bigm = np.concatenate([dg, eb, ab, bb], 1)
    return np.ascontiguousarray(bigm.astype(ml_dtypes.bfloat16))


def _perm_rows(par):
    mine = [par + 2 * j for j in range(8)]
    rest = [(1 - par) + 2 * j for j in range(8)]
    idx = []
    for gg in mine + rest:
        idx.extend(range(gg * 128, (gg + 1) * 128))
    return np.array(idx), mine


def kernel(L0_attn_x, L0_attn_w_project, L0_attn_w_out,
           L0_ffn_w1, L0_ffn_w2, L0_ffn_w3, _taps=(), _nc=None):
    x = np.asarray(L0_attn_x, np.float32)
    wqk, wvv, wo_t, w1t, w2t, w3r = _prep_weights(
        L0_attn_w_project, L0_attn_w_out, L0_ffn_w1, L0_ffn_w2, L0_ffn_w3)

    nc = _nc if _nc is not None else build_nc(_taps)
    in_maps = []
    metas = []
    for c in range(8):
        b, par = c // 2, c % 2
        idx, mine = _perm_rows(par)
        md, ma, mb = _masks(par)
        in_maps.append({
            "xp": np.ascontiguousarray(x[b][idx]),
            "wqk": wqk, "wv": wvv, "wo": wo_t,
            "w1t": w1t, "w2t": w2t, "w3r": w3r,
            "mask_d": md, "mask_a": ma, "mask_b": mb,
            "identd": np.ascontiguousarray(np.eye(128, dtype=ml_dtypes.bfloat16)),
            "bigmd": _bigmask(par),
        })
        metas.append((b, mine))

    res = run_bass_kernel_spmd(nc, in_maps, core_ids=list(range(8)))

    final = np.empty((4, T, D), np.float32)
    for c in range(8):
        b, mine = metas[c]
        o = res.results[c]["out"]
        for j, gg in enumerate(mine):
            final[b, gg * 128:(gg + 1) * 128, :] = o[j * 128:(j + 1) * 128, :]
    if _taps:
        return final, list(res.results)
    return final
